# version 06
# baseline (speedup 1.0000x reference)
import math
import re
from contextlib import ExitStack

import numpy as np
import ml_dtypes

import concourse.bass as bass
import concourse.mybir as mybir
from concourse.bass_utils import run_bass_kernel_spmd
from concourse.expressions import RuntimeValue

F32 = mybir.dt.float32
BF16 = mybir.dt.bfloat16
FP8 = mybir.dt.float8e4
I32 = mybir.dt.int32
AF = mybir.ActivationFunctionType
ALU = mybir.AluOpType
AX = mybir.AxisListType

NBIG = -1.0e38
CSHIFT = 14.0
MAGIC_SENT = 0x0BAD0001
MAGIC_ATTN = 0x51CA0001


class Cfg:
    def __init__(self, B=2, S=4096, D=2048, H=16, FF=8192, n_cores=8,
                 bisect_iters=42, theta=10000.0, stop_after=99):
        self.B, self.S, self.D, self.H, self.FF = B, S, D, H, FF
        self.n_cores = n_cores
        self.dh = 128
        self.theta = theta
        self.bisect_iters = bisect_iters
        self.stop_after = stop_after
        assert D == H * self.dh
        self.cpb = n_cores // B
        assert self.cpb % 2 == 0
        self.hpc = H // 2
        self.ownT = B * S // n_cores
        self.heavy = int(0.1 * S)
        self.recent = int(0.1 * S)
        self.FIN = D // 128
        self.KC = S // 128
        self.CH = S // 512
        self.QT = S // 128
        self.QG = S // 512
        self.HC = FF // 128
        self.TT = S // 128
        self.TCOL = S // 512
        self.OTT = self.ownT // 128
        self.TGF = min(512, self.ownT)
        self.NTG = self.ownT // self.TGF
        self.HD2 = self.hpc * self.dh
        assert self.HD2 % 512 == 0
        self.HG = self.HD2 // 512
        self.UPG = min(self.hpc, 128 // self.KC)
        self.NBG = (self.hpc + self.UPG - 1) // self.UPG
        assert self.hpc % self.UPG == 0
        assert self.CH >= 1 and S % 512 == 0
        assert self.ownT % self.TGF == 0 and D % 512 == 0


def _rope_tables(cfg):
    dh = cfg.dh
    ex = np.arange(0, dh, 2, dtype=np.float64) / dh
    inv = 1.0 / (cfg.theta ** ex)
    pos = np.arange(cfg.S, dtype=np.float64)
    fr = inv[:, None] * pos[None, :]
    cos = np.cos(fr)
    sin = np.sin(fr)
    cos_full = np.concatenate([cos, cos], 0)
    sinpm = np.concatenate([-sin, sin], 0)
    qs = 1.0 / math.sqrt(dh)
    out = {}
    for name, tab in (("cq", cos_full * qs), ("sq", sinpm * qs),
                      ("ck", cos_full), ("sk", sinpm)):
        hi = tab.astype(np.float32)
        lo = (tab - hi.astype(np.float64)).astype(np.float32)
        out[name + "_hi"] = np.ascontiguousarray(hi)
        out[name + "_lo"] = np.ascontiguousarray(lo)
    return out


def _const_tables(cfg):
    c = {}
    c["ident_bf"] = np.eye(128, dtype=ml_dtypes.bfloat16)
    c["ident_f32"] = np.eye(128, dtype=np.float32)
    c["ones_col_bf"] = np.ones((128, 1), dtype=ml_dtypes.bfloat16)
    c["ones_row_f32"] = np.ones((1, 128), dtype=np.float32)
    p = np.arange(128, dtype=np.float32).reshape(128, 1)
    j = np.arange(512, dtype=np.float32)[None, :]
    big = np.float32(NBIG)
    for r in range(4):
        c["p1tri_%d" % r] = np.ascontiguousarray(np.where(
            j < 128 * r + p + 1, np.float32(0), big).astype(ml_dtypes.bfloat16))
    c["eps_col"] = np.full((128, 1), 1e-5, dtype=np.float32)
    c["negC_col"] = np.full((128, 1), -CSHIFT, dtype=np.float32)
    for rel in range(-512, 385, 128):
        cs_ = np.clip(rel + p, 0, 512)
        be_ = np.clip(rel + p + cfg.recent + 1, 0, 512)
        c["p3ca_%d" % rel] = np.ascontiguousarray(np.where(
            j >= cs_, np.float32(0), big).astype(ml_dtypes.bfloat16))
        c["p3ba_%d" % rel] = np.ascontiguousarray(np.where(
            (j >= cs_) & (j < be_), np.float32(0), big).astype(ml_dtypes.bfloat16))
    for ci in range(cfg.CH):
        m = np.zeros((128, cfg.CH), dtype=np.float32)
        m[:, ci] = 1.0
        c["oh_%d" % ci] = m
    upg = cfg.UPG
    pbg = upg * cfg.KC
    sel = np.zeros((pbg, upg), dtype=np.float32)
    for u in range(upg):
        sel[u * cfg.KC:(u + 1) * cfg.KC, u] = 1
    c["bis_sel"] = sel
    bc = np.zeros((upg, pbg), dtype=np.float32)
    for u in range(upg):
        bc[u, u * cfg.KC:(u + 1) * cfg.KC] = 1
    c["bis_bc"] = bc
    return c


class P:
    def __init__(self, nc, ctx):
        self.nc = nc
        self.ctx = ctx
        self.sems = {}
        self.cnt = {}

    def s(self, name):
        if name not in self.sems:
            self.sems[name] = self.ctx.enter_context(self.nc.semaphore(name))
            self.cnt[name] = 0
        return self.sems[name]

    def inc(self, instr, name, k=1):
        instr.then_inc(self.s(name), k)
        self.cnt[name] += k
        return self.cnt[name]

    def wait(self, eng, name, value=None):
        v = self.cnt.get(name, 0) if value is None else value
        self.s(name)
        if v > 0:
            eng.wait_ge(self.sems[name], v)

    def n(self, name):
        self.s(name)
        return self.cnt[name]


def build_program(cfg, debug_taps=(), hbm_taps=()):
    nc = bass.Bass("TRN2", target_bir_lowering=False, debug=False,
                   enable_partition_id=False, detect_race_conditions=False)

    def internal(name):
        return "ExternalOutput" if name in hbm_taps else "Internal"
    B, S, D, FF, dh = cfg.B, cfg.S, cfg.D, cfg.FF, cfg.dh
    hpc, FIN, KC, CH, QT, QG, HC = (cfg.hpc, cfg.FIN, cfg.KC, cfg.CH,
                                    cfg.QT, cfg.QG, cfg.HC)
    ownT, TT, TCOL, OTT, TGF, NTG = (cfg.ownT, cfg.TT, cfg.TCOL, cfg.OTT,
                                     cfg.TGF, cfg.NTG)
    HD2, HG = cfg.HD2, cfg.HG
    NFG = D // 512

    xb = nc.dram_tensor("xb", [S, D], F32, kind="ExternalInput").ap()
    xown = nc.dram_tensor("xown", [ownT, D], F32, kind="ExternalInput").ap()
    wq_t = nc.dram_tensor("wq_t", [D, HD2], F32, kind="ExternalInput").ap()
    wk_t = nc.dram_tensor("wk_t", [D, HD2], F32, kind="ExternalInput").ap()
    wv_t = nc.dram_tensor("wv_t", [D, HD2], F32, kind="ExternalInput").ap()
    wo_t = nc.dram_tensor("wo_t", [D, D], F32, kind="ExternalInput").ap()
    wg = nc.dram_tensor("wg", [2, D, FF], F32, kind="ExternalInput").ap()
    wl = nc.dram_tensor("wl", [FF, D], F32, kind="ExternalInput").ap()
    tabs = {}
    for nme in ("cq_hi", "cq_lo", "sq_hi", "sq_lo",
                "ck_hi", "ck_lo", "sk_hi", "sk_lo"):
        tabs[nme] = nc.dram_tensor(nme, [128, S], F32, kind="ExternalInput").ap()
    out = nc.dram_tensor("out", [ownT, D], F32, kind="ExternalOutput").ap()

    wg8 = nc.dram_tensor("wg8", [2 * D * FF // 4096, 4096], FP8,
                         kind="Internal").ap()
    wl8 = nc.dram_tensor("wl8", [FF * D // 4096, 4096], FP8,
                         kind="Internal").ap()
    wqkv16 = nc.dram_tensor("wqkv16", [3 * D * HD2 // 4096, 4096], BF16,
                            kind="Internal").ap()
    xn8 = nc.dram_tensor("xn8", [FIN * 128, ownT], FP8,
                         kind=internal("xn8")).ap()
    q_hbm = nc.dram_tensor("q_hbm", [hpc * 128, S], BF16,
                           kind=internal("q_hbm")).ap()
    k_hbm = nc.dram_tensor("k_hbm", [hpc * 128, S], BF16,
                           kind=internal("k_hbm")).ap()
    v_hbm = nc.dram_tensor("v_hbm", [128, hpc * TT * 128], BF16,
                           kind=internal("v_hbm")).ap()
    at_hbm = nc.dram_tensor("at_hbm", [hpc * 128, S], BF16,
                            kind=internal("at_hbm")).ap()
    sh_attn = nc.dram_tensor("sh_attn", [2, hpc * 128, ownT], BF16,
                             kind="Internal", addr_space="Shared").ap()
    sh_flag = nc.dram_tensor("sh_flag", [2, 1, 32], I32,
                             kind="Internal", addr_space="Shared").ap()

    dbg = {}
    consts = _const_tables(cfg)
    cth = {k: nc.inline_tensor(np.ascontiguousarray(v), "c_" + k)
           for k, v in consts.items()}

    outer = ExitStack()
    p = P(nc, outer)
    sy, te, ve, sc, gp = nc.sync, nc.tensor, nc.vector, nc.scalar, nc.gpsimd

    def barrier():
        nc.all_engine_barrier()

    def arm_parity(reg, mod):
        for arm in gp.switch_core_id(n=128):
            try:
                lg = arm.logical % 8
            except Exception:
                lg = 0
            gp.reg_mov(reg, lg % mod)
        gp.snap(reg, min_val=0, max_val=mod - 1)

    with outer:
        cs = {}
        for k, v in consts.items():
            t = outer.enter_context(
                nc.sbuf_tensor("cs_" + k, list(v.shape),
                               mybir.dt.from_np(v.dtype)))
            p.inc(sy.dma_start(out=t[:], in_=cth[k].ap()), "d_c", 16)
            cs[k] = t

        tsum = [outer.enter_context(
            nc.sbuf_tensor("tsum%d" % g, [128, 128], F32))
            for g in range(cfg.NBG)]
        Aadd = outer.enter_context(
            nc.sbuf_tensor("Aadd", [128, hpc * KC], F32))
        Amin = outer.enter_context(
            nc.sbuf_tensor("Amin", [128, hpc * KC], F32))
        Abias = outer.enter_context(
            nc.sbuf_tensor("Abias", [128, hpc * KC], F32))
        lntap = outer.enter_context(nc.sbuf_tensor("lntap", [128, 8], F32))
        for g in range(cfg.NBG):
            ve.memset(tsum[g][:], 0.0)

        if not getattr(cfg, "dbg_no_sentinel", False):
            with gp.register("oc0") as oc0:
                arm_parity(oc0, 2)
                with gp.register("sv0") as sv0:
                    gp.reg_mov(sv0, MAGIC_SENT)
                    for s in range(2):
                        with gp.If_eq(oc0, s):
                            gp.store(sh_flag[s, 0:1, 1:2], sv0)

        with ExitStack() as ph:
            qb = [ph.enter_context(nc.sbuf_tensor("qb%d" % i, [128, 4096], F32))
                  for i in range(2)]
            qb8 = [ph.enter_context(nc.sbuf_tensor("qb8%d" % i, [128, 4096],
                                                   FP8))
                   for i in range(2)]
            qb16 = [ph.enter_context(nc.sbuf_tensor("qb16%d" % i, [128, 4096],
                                                    BF16))
                    for i in range(2)]
            wgf = wg.rearrange("a b c -> (a b c)").rearrange(
                "(n p m) -> n p m", p=128, m=4096)
            wlf = wl.rearrange("a b -> (a b)").rearrange(
                "(n p m) -> n p m", p=128, m=4096)
            wg8d = wg8.rearrange("(n p) m -> n p m", p=128)
            wl8d = wl8.rearrange("(n p) m -> n p m", p=128)
            wqkvd = wqkv16.rearrange("(n p) m -> n p m", p=128)
            nqkv = D * HD2 // (128 * 4096)
            srcs16 = []
            for wi, w_src in enumerate((wq_t, wk_t, wv_t)):
                wsf = w_src.rearrange("a b -> (a b)").rearrange(
                    "(n p m) -> n p m", p=128, m=4096)
                for i in range(nqkv):
                    srcs16.append((wsf[i], wqkvd[wi * nqkv + i]))
            ntile_g = 2 * D * FF // (128 * 4096)
            ntile_l = FF * D // (128 * 4096)
            n16 = len(srcs16)
            nW = 0 if getattr(cfg, "dbg_no_phasew", False) else \
                n16 + ntile_g + ntile_l
            if getattr(cfg, "dbg_phasew_nops", False):
                nW = 0
                for e_ in (ve, sc, te, gp, sy):
                    for _ in range(1500):
                        e_.nop()
            wmode = getattr(cfg, "dbg_phasew_mode", "full")
            for i in range(nW):
                bi = i % 2
                if i < n16:
                    srcA, dstA = srcs16[i]
                    stg = qb16[bi]
                elif i < n16 + ntile_g:
                    srcA, dstA = wgf[i - n16], wg8d[i - n16]
                    stg = qb8[bi]
                else:
                    srcA = wlf[i - n16 - ntile_g]
                    dstA = wl8d[i - n16 - ntile_g]
                    stg = qb8[bi]
                if wmode in ("full", "load"):
                    if wmode == "load":
                        p.wait(sy, "d_q", max(0, (i - 1) * 16))
                    else:
                        p.wait(sy, "g_q8", max(0, (i - 1) * 16))
                    p.inc(sy.dma_start(out=qb[bi][:], in_=srcA), "d_q", 16)
                if wmode in ("full", "cast"):
                    eng = ve if i % 2 == 0 else sc
                    p.wait(eng, "d_q")
                    if eng is ve:
                        ve.tensor_copy(out=stg[:], in_=qb[bi][:])
                    else:
                        sc.copy(out=stg[:], in_=qb[bi][:])
                    p.inc(eng.drain(), "c_q8")
                if wmode in ("full", "store"):
                    sub = getattr(cfg, "dbg_store_subset", "")
                    skip = (sub == "16" and i >= n16) or \
                           (sub == "g" and not (n16 <= i < n16 + ntile_g)) or \
                           (sub == "l" and i < n16 + ntile_g)
                    if sub == "16x":
                        dstA = wqkvd[i % nqkv]
                        stg = qb16[bi]
                        skip = False
                    if not skip:
                        p.wait(gp, "c_q8")
                        p.inc(gp.dma_start(out=dstA, in_=stg[:]), "g_q8", 16)
            p.wait(gp, "g_q8")
            if getattr(cfg, "dbg_zero_qb", False):
                p.wait(ve, "g_q8")
                for t_ in qb + qb8 + qb16:
                    ve.memset(t_[:], 0.0)
                ve.drain()
            barrier()

        if "wdump" in debug_taps:
            dwd = nc.dram_tensor("dbg_wdump", [128, 1664], F32,
                                 kind="ExternalOutput").ap()
            dbg["wdump"] = dwd
            p.inc(gp.dma_start(out=dwd[:, 0:128],
                               in_=cth["ident_f32"].ap()), "g_wd", 16)
            p.inc(gp.dma_start(out=dwd[:, 128:640],
                               in_=tabs["cq_hi"][:, 0:512]), "g_wd", 16)
            p.inc(gp.dma_start(out=dwd[:, 640:1152],
                               in_=xb[0:128, 0:512]), "g_wd", 16)
            p.inc(gp.dma_start(out=dwd[:, 1152:1664],
                               in_=wo_t[0:128, 0:512]), "g_wd", 16)
            p.wait(gp, "g_wd")
            barrier()

        with ExitStack() as ph:
            sb = lambda n, sh, dt=F32: ph.enter_context(
                nc.sbuf_tensor(n, list(sh), dt))
            ps = lambda n, sh, dt=F32: ph.enter_context(
                nc.psum_tensor(n, list(sh), dt))
            wqkb = [sb("wqkb%d" % i, [128, FIN * HD2], BF16) for i in range(2)]
            x8s = sb("x8s", [128, 128], FP8)
            xf = [sb("xf%d" % i, [128, D]) for i in range(2)]
            xnf = sb("xnf", [128, D])
            xnb = sb("xnb", [128, D], BF16)
            xsq = sb("xsq", [128, D])
            smn = sb("smn", [128, 1])
            ssq = sb("ssq", [128, 1])
            mean = sb("mean", [128, 1])
            ex2 = sb("ex2", [128, 1])
            t1v = sb("t1v", [128, 1])
            var1 = sb("var1", [128, 1])
            sd1 = sb("sd1", [128, 1])
            rstd1 = sb("rstd1", [128, 1])
            nmr1 = sb("nmr1", [128, 1])
            xnTc = sb("xnTc", [128, FIN * 512], BF16)
            thi = sb("thi", [128, 512])
            tb4 = [sb("tb4_%d" % i, [128, 512]) for i in range(4)]
            qraw = sb("qraw", [128, 512], BF16)
            kraw = sb("kraw", [128, 512], BF16)
            qrot = sb("qrot", [128, 512], BF16)
            krot = sb("krot", [128, 512], BF16)
            rt1 = sb("rt1", [128, 512], BF16)
            qcol = sb("qcol", [128, 512], BF16)
            kcol = sb("kcol", [128, 512], BF16)
            vcol = sb("vcol", [128, HD2], BF16)
            ps_tb = [ps("ps_tb%d" % i, [128, 128], BF16) for i in range(2)]
            ps_tf = [ps("ps_tf%d" % i, [128, 128]) for i in range(2)]
            ps_q = ps("ps_q", [128, 512])
            ps_k = ps("ps_k", [128, 512])
            ps_v = [ps("ps_v%d" % i, [128, 512]) for i in range(HG)]

            wqkvv = wqkv16.rearrange("a b -> (a b)").rearrange(
                "(w d h) -> w d h", w=3, d=D)

            def load_w(widx, tcol):
                gi = 3 * tcol + widx
                bi = gi % 2
                p.wait(sy, "t_wq", max(0, gi - 1))
                for f in range(FIN):
                    p.inc(sy.dma_start(
                        out=wqkb[bi][:, f * HD2:(f + 1) * HD2],
                        in_=wqkvv[widx, f * 128:(f + 1) * 128, :]), "d_wq", 16)
                return bi

            def ln_tile(src_ap, dst_f32, dst_bf=None):
                bi = p.n("d_x") // 16 % 2
                p.wait(sy, "s_ln", max(0, p.n("s_ln") - 1))
                p.inc(sy.dma_start(out=xf[bi][:], in_=src_ap), "d_x", 16)
                if getattr(cfg, "dbg_no_ln", False):
                    p.wait(ve, "d_x")
                    p.wait(ve, "t_tp")
                    ve.tensor_copy(out=dst_f32[:], in_=xf[bi][:])
                    p.inc(ve.drain(), "s_ln")
                    if dst_bf is not None:
                        ve.tensor_copy(out=dst_bf[:], in_=dst_f32[:])
                        p.inc(ve.drain(), "v_xnb")
                    return
                p.wait(sc, "d_x")
                p.wait(sc, "v_sq", p.n("v_sq"))
                sc.activation(out=xsq[:], in_=xf[bi][:], func=AF.Square,
                              accum_out=ssq[:])
                p.inc(sc.drain(), "s_sq")
                p.wait(ve, "d_x")
                ve.reduce_sum(smn[:], xf[bi][:], axis=AX.X)
                ve.drain()
                p.wait(ve, "s_sq")
                p.inc(ve.tensor_scalar(out=ex2[:], in0=ssq[:],
                                       scalar1=1.0 / D, scalar2=None,
                                       op0=ALU.mult), "v_sq")
                ve.tensor_scalar(out=mean[:], in0=smn[:], scalar1=1.0 / D,
                                 scalar2=None, op0=ALU.mult)
                ve.drain()
                ve.tensor_mul(t1v[:], mean[:], mean[:])
                ve.drain()
                ve.tensor_sub(var1[:], ex2[:], t1v[:])
                p.inc(ve.drain(), "v_ln")
                p.wait(sc, "v_ln")
                sc.activation(out=sd1[:], in_=var1[:], func=AF.Sqrt,
                              bias=cs["eps_col"][:, 0:1])
                p.inc(sc.drain(), "s_sd")
                p.wait(ve, "s_sd")
                ve.reciprocal(out=rstd1[:], in_=sd1[:])
                ve.drain()
                ve.tensor_mul(nmr1[:], mean[:], rstd1[:])
                ve.drain()
                ve.tensor_scalar(out=nmr1[:], in0=nmr1[:], scalar1=-1.0,
                                 scalar2=None, op0=ALU.mult)
                p.inc(ve.drain(), "v_rs")
                p.wait(sc, "v_rs")
                p.wait(sc, "t_tp")
                sc.activation(out=dst_f32[:], in_=xf[bi][:],
                              func=AF.Identity, bias=nmr1[:],
                              scale=rstd1[:])
                p.inc(sc.drain(), "s_ln")
                if dst_bf is not None:
                    p.wait(ve, "s_ln")
                    ve.tensor_copy(out=dst_bf[:], in_=dst_f32[:])
                    p.inc(ve.drain(), "v_xnb")
                if "lntap" in debug_taps and p.n("s_ln") == 1:
                    p.wait(gp, "v_rs")
                    for j, tt_ in enumerate((smn, ssq, mean, ex2, var1, sd1,
                                             rstd1, nmr1)):
                        p.inc(gp.dma_start(out=lntap[:, j:j + 1],
                                           in_=tt_[:]), "g_lt", 16)

            def transpose128(src, dst, f32, ready):
                n = p.n("t_tp")
                pb = n % 2
                pt = ps_tf[pb] if f32 else ps_tb[pb]
                p.wait(te, ready)
                p.wait(te, "v_tp", max(0, n - 1))
                te.transpose(pt[:, 0:128], src,
                             cs["ident_f32" if f32 else "ident_bf"][:])
                p.inc(te.drain(), "t_tp")
                p.wait(ve, "t_tp")
                ve.tensor_copy(out=dst, in_=pt[:, 0:128])
                p.inc(ve.drain(), "v_tp")

            for tcol in range(TCOL):
                tsl = slice(tcol * 512, (tcol + 1) * 512)
                no_qkv = getattr(cfg, "dbg_no_qkv", False)
                no_rope = getattr(cfg, "dbg_no_rope", False)
                for i, nme in enumerate(("cq", "sq", "ck", "sk")):
                    if no_qkv or no_rope:
                        break
                    p.wait(sy, "v_rope", max(0, 2 * hpc * tcol))
                    p.wait(sy, "v_tab", max(0, 4 * tcol + i))
                    p.inc(sy.dma_start(out=thi[:],
                                       in_=tabs[nme + "_hi"][:, tsl]),
                          "d_tab", 16)
                    p.inc(sy.dma_start(out=tb4[i][:],
                                       in_=tabs[nme + "_lo"][:, tsl]),
                          "d_tab", 16)
                    p.wait(ve, "d_tab")
                    p.inc(ve.tensor_add(tb4[i][:], thi[:], tb4[i][:]), "v_tab")

                if not no_qkv:
                    p.wait(ve, "t_qk", max(0, 2 * hpc * tcol))
                    p.wait(ve, "t_v", max(0, 4 * HG * tcol))
                for t4 in range(4):
                    ti = tcol * 4 + t4
                    ln_tile(xb[ti * 128:(ti + 1) * 128, :], xnf, xnb)
                    for f in range(FIN):
                        transpose128(
                            xnb[:, f * 128:(f + 1) * 128],
                            xnTc[:, f * 512 + t4 * 128:f * 512 + (t4 + 1) * 128],
                            f32=False, ready="v_xnb")

                if tcol == 0 and "xnTc" in debug_taps:
                    dxnTc = nc.dram_tensor("dbg_xnTc", [128, FIN * 512], BF16,
                                           kind="ExternalOutput").ap()
                    dxnb = nc.dram_tensor("dbg_xnb", [128, D], BF16,
                                          kind="ExternalOutput").ap()
                    dbg["xnTc"] = dxnTc
                    dbg["xnb"] = dxnb
                    tapeng = gp if getattr(cfg, "dbg_tap_gp", True) else sy
                    p.wait(tapeng, "v_tp")
                    p.wait(tapeng, "v_xnb")
                    p.inc(tapeng.dma_start(out=dxnTc, in_=xnTc[:]),
                          "g_dbg", 16)
                    p.inc(tapeng.dma_start(out=dxnb, in_=xnb[:]),
                          "g_dbg", 16)
                    p.wait(te, "g_dbg")

                if no_qkv:
                    continue
                bq = load_w(0, tcol)
                bk = load_w(1, tcol)
                bv = load_w(2, tcol)
                dwq = p.n("d_wq")
                for widx, (bw, hbmt) in enumerate(((bq, q_hbm), (bk, k_hbm))):
                    p.wait(te, "d_wq", dwq - (2 - widx) * 16 * FIN)
                    p.wait(te, "v_tp")
                    raw = qraw if widx == 0 else kraw
                    rot = qrot if widx == 0 else krot
                    colt = qcol if widx == 0 else kcol
                    ti0, ti1 = (0, 1) if widx == 0 else (2, 3)
                    for h in range(hpc):
                        pdst = ps_q if h % 2 == 0 else ps_k
                        p.wait(te, "v_qk", max(0, p.n("t_qk") - 1))
                        lastm = None
                        for f in range(FIN):
                            lastm = te.matmul(
                                pdst[:],
                                wqkb[bw][:, f * HD2 + h * dh:
                                         f * HD2 + (h + 1) * dh],
                                xnTc[:, f * 512:(f + 1) * 512],
                                start=(f == 0), stop=(f == FIN - 1))
                        p.inc(te.drain(), "t_qk")
                        p.wait(ve, "t_qk")
                        p.wait(ve, "g_rot", p.n("g_rot"))
                        ve.tensor_copy(out=raw[:], in_=pdst[:])
                        p.inc(ve.drain(), "v_qk")
                        if no_rope:
                            p.wait(ve, "g_qk", p.n("g_qk"))
                            ve.tensor_copy(out=colt[:], in_=raw[:])
                            p.inc(ve.drain(), "v_rope")
                            p.wait(gp, "v_rope")
                            p.inc(gp.dma_start(
                                out=hbmt[h * 128:(h + 1) * 128, tsl],
                                in_=colt[:]), "g_qk", 16)
                            continue
                        p.wait(gp, "v_qk")
                        p.inc(gp.dma_start(out=rot[0:64, :],
                                           in_=raw[64:128, :]), "g_rot", 16)
                        p.inc(gp.dma_start(out=rot[64:128, :],
                                           in_=raw[0:64, :]), "g_rot", 16)
                        p.wait(ve, "v_tab")
                        p.wait(ve, "g_rot")
                        p.wait(ve, "g_qk", p.n("g_qk"))
                        ve.tensor_mul(rt1[:], raw[:], tb4[ti0][:])
                        ve.tensor_mul(colt[:], rot[:], tb4[ti1][:])
                        ve.tensor_add(colt[:], rt1[:], colt[:])
                        p.inc(ve.drain(), "v_rope")
                        p.wait(gp, "v_rope")
                        p.inc(gp.dma_start(
                            out=hbmt[h * 128:(h + 1) * 128, tsl],
                            in_=colt[:]), "g_qk", 16)
                    p.inc(te.nop(), "t_wq")

                p.wait(te, "d_wq", dwq)
                for hg in range(HG):
                    for t4 in range(4):
                        ti = tcol * 4 + t4
                        p.wait(te, "v_v", p.n("t_v"))
                        lastv = None
                        for f in range(FIN):
                            lastv = te.matmul(
                                ps_v[hg][:],
                                xnTc[:, f * 512 + t4 * 128:
                                     f * 512 + (t4 + 1) * 128],
                                wqkb[bv][:, f * HD2 + hg * 512:
                                         f * HD2 + (hg + 1) * 512],
                                start=(f == 0), stop=(f == FIN - 1))
                        p.inc(te.drain(), "t_v")
                        p.wait(ve, "t_v")
                        p.wait(ve, "g_v", p.n("g_v"))
                        ve.tensor_copy(
                            out=vcol[:, hg * 512:(hg + 1) * 512],
                            in_=ps_v[hg][:])
                        p.inc(ve.drain(), "v_v")
                        p.wait(gp, "v_v")
                        for hh in range(4):
                            u = hg * 4 + hh
                            p.inc(gp.dma_start(
                                out=v_hbm[:, (u * TT + ti) * 128:
                                          (u * TT + ti + 1) * 128],
                                in_=vcol[:, hg * 512 + hh * 128:
                                         hg * 512 + (hh + 1) * 128]),
                                "g_v", 16)
                p.inc(te.nop(), "t_wq")

            for tt in range(0 if getattr(cfg, "dbg_no_xn8", False) else OTT):
                ln_tile(xown[tt * 128:(tt + 1) * 128, :], xnf, None)
                for f in range(FIN):
                    p.wait(ve, "g_x8", p.n("g_x8"))
                    transpose128(xnf[:, f * 128:(f + 1) * 128],
                                 x8s[:, 0:128], f32=True, ready="s_ln")
                    p.wait(gp, "v_tp")
                    p.inc(gp.dma_start(
                        out=xn8[f * 128:(f + 1) * 128,
                                tt * 128:(tt + 1) * 128],
                        in_=x8s[:, 0:128]), "g_x8", 16)
            p.wait(gp, "g_qk")
            p.wait(gp, "g_v")
            p.wait(gp, "g_x8")
            barrier()

        if cfg.stop_after < 2:
            _finish(p, gp, out, cs, dbg, debug_taps, locals())
            return nc, dbg

        with ExitStack() as ph:
            sb = lambda n, sh, dt=F32: ph.enter_context(
                nc.sbuf_tensor(n, list(sh), dt))
            ps = lambda n, sh, dt=F32: ph.enter_context(
                nc.psum_tensor(n, list(sh), dt))
            qTh = [sb("qTh%d" % i, [128, S], BF16) for i in range(2)]
            kTh = [sb("kTh%d" % i, [128, S], BF16) for i in range(2)]
            Ebuf = sb("Ebuf", [128, S], BF16)
            lacc = sb("lacc", [128, max(CH, 1)])
            ltot = sb("ltot", [128, 1])
            invl = sb("invl", [128, 1])
            ohw = [sb("ohw%d" % i, [128, CH], BF16) for i in range(2)]
            smask = sb("smask", [128, 512])
            tmp_sb = sb("tmp_sb", [max(CH, 1), 512])
            ps_s = [ps("ps_s%d" % i, [128, 512]) for i in range(2)]
            ps_tmp = ps("ps_tmp", [max(CH, 1), 512])

            nEx = 0
            nCS = 0
            for u in range(hpc):
                ub = u % 2
                p.wait(sy, "t_p12", max(0, u - 1))
                p.inc(sy.dma_start(out=qTh[ub][:],
                                   in_=q_hbm[u * 128:(u + 1) * 128, :]),
                      "d_qk1", 16)
                p.inc(sy.dma_start(out=kTh[ub][:],
                                   in_=k_hbm[u * 128:(u + 1) * 128, :]),
                      "d_qk1", 16)
                p.wait(te, "d_qk1")
                for qt in range(QT):
                    nch = qt // 4 + 1
                    qsl = slice(qt * 128, (qt + 1) * 128)
                    for c in range(nch):
                        bi = nEx % 2
                        p.wait(te, "p1_free", max(0, nEx - 1))
                        te.matmul(ps_s[bi][:], qTh[ub][:, qsl],
                                  kTh[ub][:, c * 512:(c + 1) * 512],
                                  start=True, stop=True)
                        p.inc(te.drain(), "t_s1")
                        diag = (c == qt // 4)
                        if diag:
                            p.wait(ve, "t_s1")
                            ve.tensor_add(
                                smask[:], ps_s[bi][:],
                                cs["p1tri_%d" % (qt % 4)][:])
                            p.inc(ve.drain(), "v_m1")
                            p.wait(sc, "v_m1")
                            src = smask[:]
                        else:
                            p.wait(sc, "t_s1")
                            src = ps_s[bi][:]
                        sc.activation(
                            out=Ebuf[:, c * 512:(c + 1) * 512], in_=src,
                            func=AF.Exp, bias=cs["negC_col"][:, 0:1],
                            accum_out=lacc[:, c:c + 1])
                        p.inc(sc.drain(), "p1_free")
                        nEx += 1
                    p.wait(ve, "p1_free")
                    ve.reduce_sum(ltot[:], lacc[:, 0:nch], axis=AX.X)
                    ve.drain()
                    p.inc(ve.reciprocal(out=invl[:], in_=ltot[:]), "v_il")
                    ve.drain()
                    for c in range(nch):
                        ob = nCS % 2
                        p.wait(ve, "t_cs", max(0, nCS - 1))
                        ve.tensor_scalar(out=ohw[ob][:],
                                         in0=cs["oh_%d" % c][:],
                                         scalar1=invl[:], scalar2=None,
                                         op0=ALU.mult)
                        p.inc(ve.drain(), "v_oh")
                        p.wait(te, "v_oh")
                        mcs = te.matmul(ps_tmp[:], ohw[ob][:],
                                        Ebuf[:, c * 512:(c + 1) * 512],
                                        start=(qt == 0 and c == 0),
                                        stop=(qt == QT - 1 and c == nch - 1))
                        if qt == QT - 1 and c == nch - 1:
                            p.inc(te.drain(), "t_cs")
                        else:
                            p.inc(mcs, "t_cs")
                        nCS += 1
                p.wait(ve, "t_cs")
                ve.tensor_copy(out=tmp_sb[:], in_=ps_tmp[:])
                p.inc(ve.drain(), "v_ts")
                p.wait(gp, "v_ts")
                g = u // cfg.UPG
                ul = u % cfg.UPG
                for cch in range(max(CH, 1)):
                    p.inc(gp.dma_start(
                        out=tsum[g][ul * KC + cch * 4:
                                    ul * KC + (cch + 1) * 4, :],
                        in_=tmp_sb[cch:cch + 1, :]), "g_ts", 16)
                p.wait(te, "g_ts")
                p.inc(te.nop(), "t_p12")
            barrier()

        if cfg.stop_after < 3:
            _finish(p, gp, out, cs, dbg, debug_taps, locals())
            return nc, dbg

        with ExitStack() as ph:
            sb = lambda n, sh, dt=F32: ph.enter_context(
                nc.sbuf_tensor(n, list(sh), dt))
            upg = cfg.UPG
            PBG = upg * KC
            b_ge = sb("b_ge", [128, 128])
            b_cnt = sb("b_cnt", [128, 1])
            b_c4 = sb("b_c4", [upg, 1])
            b_L = sb("b_L", [upg, 1])
            b_H = sb("b_H", [upg, 1])
            b_m = sb("b_m", [upg, 1])
            b_sel = sb("b_sel", [upg, 1])
            b_nsel = sb("b_nsel", [upg, 1])
            b_t1 = sb("b_t1", [upg, 1])
            b_T = sb("b_T", [128, 1])
            h01 = sb("h01", [128, 128])
            ps_b = ph.enter_context(nc.psum_tensor("ps_b", [128, 512], F32))

            for g in range(cfg.NBG):
                p.wait(te, "v_A", g)
                ve.memset(b_L[:], 0.0)
                p.inc(ve.memset(b_H[:], float(S + 1)), "v_bi")

                def thresh_bcast(src_col):
                    p.wait(te, "v_bi")
                    te.matmul(ps_b[0:PBG, 0:1], cs["bis_bc"][:],
                              src_col, start=True, stop=True)
                    p.inc(te.drain(), "t_bT")
                    p.wait(ve, "t_bT")
                    p.inc(ve.tensor_copy(out=b_T[0:PBG, :],
                                         in_=ps_b[0:PBG, 0:1]), "v_bT")
                    ve.drain()

                for it in range(cfg.bisect_iters):
                    ve.tensor_add(b_m[:], b_L[:], b_H[:])
                    ve.drain()
                    ve.tensor_scalar(out=b_m[:], in0=b_m[:],
                                     scalar1=0.5, scalar2=None,
                                     op0=ALU.mult)
                    p.inc(ve.drain(), "v_bi")
                    thresh_bcast(b_m[:])
                    ve.tensor_scalar(out=b_ge[0:PBG, :],
                                     in0=tsum[g][0:PBG, :],
                                     scalar1=b_T[0:PBG, :], scalar2=None,
                                     op0=ALU.is_ge)
                    ve.drain()
                    ve.reduce_sum(b_cnt[0:PBG, :], b_ge[0:PBG, :],
                                  axis=AX.X)
                    p.inc(ve.drain(), "v_bc")
                    p.wait(te, "v_bc")
                    te.matmul(ps_b[0:upg, 128:129],
                              cs["bis_sel"][0:PBG, :],
                              b_cnt[0:PBG, :], start=True, stop=True)
                    p.inc(te.drain(), "t_bc")
                    p.wait(ve, "t_bc")
                    ve.tensor_copy(out=b_c4[:], in_=ps_b[0:upg, 128:129])
                    ve.drain()
                    ve.tensor_scalar(out=b_sel[:], in0=b_c4[:],
                                     scalar1=float(cfg.heavy), scalar2=None,
                                     op0=ALU.is_ge)
                    ve.drain()
                    ve.tensor_scalar(out=b_nsel[:], in0=b_sel[:],
                                     scalar1=-1.0, scalar2=1.0,
                                     op0=ALU.mult, op1=ALU.add)
                    ve.tensor_sub(b_t1[:], b_m[:], b_L[:])
                    ve.drain()
                    ve.tensor_mul(b_t1[:], b_t1[:], b_sel[:])
                    ve.drain()
                    ve.tensor_add(b_L[:], b_L[:], b_t1[:])
                    ve.tensor_sub(b_t1[:], b_m[:], b_H[:])
                    ve.drain()
                    ve.tensor_mul(b_t1[:], b_t1[:], b_nsel[:])
                    ve.drain()
                    ve.tensor_add(b_H[:], b_H[:], b_t1[:])
                    p.inc(ve.drain(), "v_bi")
                thresh_bcast(b_L[:])
                ve.tensor_scalar(out=h01[0:PBG, :],
                                 in0=tsum[g][0:PBG, :],
                                 scalar1=b_T[0:PBG, :], scalar2=None,
                                 op0=ALU.is_ge)
                p.inc(ve.drain(), "v_h01")
                p.wait(te, "v_h01")
                te.transpose(ps_b[:, 256:256 + PBG], h01[0:PBG, :],
                             cs["ident_f32"][0:PBG, 0:PBG])
                p.inc(te.drain(), "t_h01")
                p.wait(ve, "t_h01")
                asl = slice(g * PBG, (g + 1) * PBG)
                ve.tensor_scalar(out=Aadd[:, asl],
                                 in0=ps_b[:, 256:256 + PBG],
                                 scalar1=1.0e38, scalar2=NBIG,
                                 op0=ALU.mult, op1=ALU.add)
                ve.tensor_scalar(out=Amin[:, asl],
                                 in0=ps_b[:, 256:256 + PBG],
                                 scalar1=2.0e38, scalar2=NBIG,
                                 op0=ALU.mult, op1=ALU.add)
                ve.drain()
                ve.tensor_scalar(out=Abias[:, asl], in0=Aadd[:, asl],
                                 scalar1=-CSHIFT, scalar2=None,
                                 op0=ALU.add)
                p.inc(ve.drain(), "v_A")
            barrier()

        if cfg.stop_after < 4:
            _finish(p, gp, out, cs, dbg, debug_taps, locals())
            return nc, dbg

        with ExitStack() as ph:
            sb = lambda n, sh, dt=F32: ph.enter_context(
                nc.sbuf_tensor(n, list(sh), dt))
            ps = lambda n, sh, dt=F32: ph.enter_context(
                nc.psum_tensor(n, list(sh), dt))
            qTh = [sb("qTh3%d" % i, [128, S], BF16) for i in range(2)]
            kTh = [sb("kTh3%d" % i, [128, S], BF16) for i in range(2)]
            vh = [sb("vh%d" % i, [128, TT * 128], BF16) for i in range(2)]
            E2 = [sb("E2_%d" % i, [128, 512], BF16) for i in range(2)]
            tcaus = sb("tcaus", [128, 512])
            tband = sb("tband", [128, 512])
            smask3 = sb("smask3", [128, 512])
            l2sb = sb("l2sb", [1, 512])
            invl2 = sb("invl2", [1, 512])
            il2bc = sb("il2bc", [128, 512])
            atq = sb("atq", [128, 512], BF16)
            ps_s3 = [ps("ps_s3%d" % i, [128, 512]) for i in range(2)]
            ps_pv = ps("ps_pv", [128, 512])
            ps_l2 = ps("ps_l2", [1, 512])
            ps_bc = ps("ps_bc", [128, 512])

            nK = 0
            nQG = 0
            for u in range(hpc):
                ub = u % 2
                p.wait(sy, "t_p3", max(0, (u - 1) * QG))
                p.inc(sy.dma_start(out=qTh[ub][:],
                                   in_=q_hbm[u * 128:(u + 1) * 128, :]),
                      "d_qk3", 16)
                p.inc(sy.dma_start(out=kTh[ub][:],
                                   in_=k_hbm[u * 128:(u + 1) * 128, :]),
                      "d_qk3", 16)
                p.wait(gp, "t_p3", max(0, (u - 1) * QG))
                p.inc(gp.dma_start(
                    out=vh[ub][:],
                    in_=v_hbm[:, u * TT * 128:(u + 1) * TT * 128]),
                    "g_v3", 16)
                p.wait(te, "d_qk3")
                p.wait(te, "g_v3")
                for qg in range(QG):
                    qsl = slice(qg * 512, (qg + 1) * 512)
                    kmax = (qg * 512 + 511) // 128
                    for kc in range(kmax + 1):
                        bi = nK % 2
                        p.wait(te, "p3_free", max(0, nK - 1))
                        te.matmul(ps_s3[bi][:],
                                  kTh[ub][:, kc * 128:(kc + 1) * 128],
                                  qTh[ub][:, qsl],
                                  start=True, stop=True)
                        p.inc(te.drain(), "t_s3")
                        rel = 128 * kc - 512 * qg
                        acol = Amin[:, u * KC + kc:u * KC + kc + 1]
                        abias = Abias[:, u * KC + kc:u * KC + kc + 1]
                        p.wait(sc, "t_pv", max(0, nK - 1))
                        if rel < -(cfg.recent + 127):
                            p.wait(sc, "t_s3")
                            sc.activation(out=E2[bi][:],
                                          in_=ps_s3[bi][:],
                                          func=AF.Exp, bias=abias)
                            p.inc(sc.drain(), "s_e2")
                            sc.sem_inc(p.s("p3_free"), 1)
                            p.cnt["p3_free"] += 1
                        else:
                            p.wait(ve, "t_s3")
                            p.wait(ve, "s_e2", max(0, nK))
                            ve.tensor_add(tcaus[:], ps_s3[bi][:],
                                          cs["p3ca_%d" % rel][:])
                            p.inc(ve.tensor_add(tband[:], ps_s3[bi][:],
                                                cs["p3ba_%d" % rel][:]),
                                  "p3_free")
                            ve.scalar_tensor_tensor(
                                out=smask3[:], in0=tcaus[:], scalar=acol,
                                in1=tband[:], op0=ALU.min, op1=ALU.max)
                            p.inc(ve.drain(), "v_m3")
                            p.wait(sc, "v_m3")
                            sc.activation(out=E2[bi][:], in_=smask3[:],
                                          func=AF.Exp,
                                          bias=cs["negC_col"][:, 0:1])
                            p.inc(sc.drain(), "s_e2")
                        p.wait(te, "s_e2")
                        te.matmul(ps_pv[:],
                                  vh[ub][:, kc * 128:(kc + 1) * 128],
                                  E2[bi][:], start=(kc == 0), stop=(kc == kmax))
                        ml2 = te.matmul(ps_l2[:], cs["ones_col_bf"][:],
                                        E2[bi][:], start=(kc == 0),
                                        stop=(kc == kmax))
                        if kc == kmax:
                            p.inc(te.drain(), "t_pv")
                        else:
                            p.inc(ml2, "t_pv")
                        nK += 1
                    p.wait(ve, "t_pv")
                    ve.tensor_copy(out=l2sb[:], in_=ps_l2[:])
                    ve.drain()
                    ve.reciprocal(out=invl2[:], in_=l2sb[:])
                    p.inc(ve.drain(), "v_il2")
                    p.wait(te, "v_il2")
                    te.matmul(ps_bc[:], cs["ones_row_f32"][:], invl2[:],
                              start=True, stop=True)
                    p.inc(te.drain(), "t_bc3")
                    p.wait(ve, "t_bc3")
                    ve.tensor_copy(out=il2bc[:], in_=ps_bc[:])
                    ve.drain()
                    p.wait(ve, "g_at", max(0, p.n("g_at") - 16))
                    ve.tensor_mul(atq[:], ps_pv[:], il2bc[:])
                    p.inc(ve.drain(), "v_at")
                    p.wait(gp, "v_at")
                    p.inc(gp.dma_start(
                        out=at_hbm[u * 128:(u + 1) * 128, qsl],
                        in_=atq[:]), "g_at", 16)
                    nQG += 1
                    p.wait(te, "v_at", nQG)
                    p.inc(te.nop(), "t_p3")
            p.wait(gp, "g_at")
            barrier()

        if cfg.stop_after < 5:
            _finish(p, gp, out, cs, dbg, debug_taps, locals())
            return nc, dbg

        ffnO = outer.enter_context(
            nc.sbuf_tensor("ffnO", [128, OTT * D], BF16, side="right"))
        attnF = outer.enter_context(
            nc.sbuf_tensor("attnF", [128, FIN * ownT], BF16, side="right"))
        with ExitStack() as ph:
            exb = ph.enter_context(
                nc.sbuf_tensor("exb", [128, 2 * ownT], BF16))
            s_ld = p.s("x_ld")
            s_st = p.s("x_st")
            s_af = p.s("x_af")
            with gp.register("oc") as oc, gp.register("sv") as sv, \
                    gp.register("pf") as pf:
                arm_parity(oc, cfg.cpb)
                for jj in range(cfg.cpb):
                    with gp.If_eq(oc, jj):
                        w = jj % 2
                        pbc = (jj ^ 1) * ownT
                        obc = jj * ownT
                        for u in range(hpc):
                            sl = (u % 2) * ownT
                            if u >= 2:
                                gp.wait_ge(s_st, (u - 1) * 16)
                            gp.dma_start(
                                out=exb[:, sl:sl + ownT],
                                in_=at_hbm[u * 128:(u + 1) * 128,
                                           pbc:pbc + ownT]).then_inc(s_ld, 16)
                            gp.wait_ge(s_ld, (u + 1) * 16)
                            gp.dma_start(
                                out=sh_attn[w, u * 128:(u + 1) * 128, :],
                                in_=exb[:, sl:sl + ownT]).then_inc(s_st, 16)
                        gp.wait_ge(s_st, hpc * 16)
                        gp.reg_mov(sv, MAGIC_ATTN)
                        gp.store(sh_flag[w, 0:1, 1:2], sv)
                        gp.reg_mov(pf, 0)
                        with gp.While(RuntimeValue(pf) != MAGIC_ATTN):
                            gp.reg_load(pf, sh_flag[1 - w, 0:1, 1:2])
                        for f in range(FIN):
                            uu = f % hpc
                            if (f // hpc) == w:
                                src = at_hbm[uu * 128:(uu + 1) * 128,
                                             obc:obc + ownT]
                            else:
                                src = sh_attn[1 - w, uu * 128:(uu + 1) * 128, :]
                            gp.dma_start(
                                out=attnF[:, f * ownT:(f + 1) * ownT],
                                in_=src).then_inc(s_af, 16)
                        gp.wait_ge(s_af, FIN * 16)
            barrier()

        if cfg.stop_after < 6:
            _finish(p, gp, out, cs, dbg, debug_taps, locals())
            return nc, dbg

        with ExitStack() as ph:
            sb = lambda n, sh, dt=F32: ph.enter_context(
                nc.sbuf_tensor(n, list(sh), dt))
            ps = lambda n, sh, dt=F32: ph.enter_context(
                nc.psum_tensor(n, list(sh), dt))
            w1b = [sb("w1b%d" % i, [128, FIN * 2 * 128], FP8) for i in range(2)]
            xn8t = sb("xn8t", [128, FIN * TGF], FP8)
            ffg = sb("ffg", [128, TGF], BF16)
            ff1 = sb("ff1", [128, TGF], BF16)
            gvt = sb("gvt", [128, TGF], BF16)
            gt1 = sb("gt1", [128, TGF], BF16)
            gt2 = sb("gt2", [128, TGF], BF16)
            prod = sb("prod", [128, TGF], BF16)
            act8 = sb("act8", [128, HC * TGF], FP8)
            wl8b = [sb("wl8b%d" % i, [128, D], FP8) for i in range(2)]
            ps_g = [ps("ps_g%d" % i, [128, TGF]) for i in range(2)]
            ps_o = [ps("ps_o%d" % i, [128, 512]) for i in range(NFG)]

            wg8m = wg8.rearrange("n m -> (n m)").rearrange(
                "(g b c) -> g b c", g=2, b=D)
            wl8m = wl8.rearrange("n m -> (n m)").rearrange(
                "(b c) -> b c", b=FF)
            nH = 0
            nO = 0
            for tg in range(NTG):
                tgs = slice(tg * TGF, (tg + 1) * TGF)
                p.wait(sy, "t_g1")
                for f in range(FIN):
                    p.inc(sy.dma_start(out=xn8t[:, f * TGF:(f + 1) * TGF],
                                       in_=xn8[f * 128:(f + 1) * 128, tgs]),
                          "d_x8", 16)
                for hc in range(HC):
                    bi = nH % 2
                    p.wait(sy, "t_g1", max(0, nH - 1))
                    for g in range(2):
                        for f in range(FIN):
                            p.inc(sy.dma_start(
                                out=w1b[bi][:, (f * 2 + g) * 128:
                                            (f * 2 + g + 1) * 128],
                                in_=wg8m[g, f * 128:(f + 1) * 128,
                                         hc * 128:(hc + 1) * 128]),
                                "d_w1", 16)
                    p.wait(te, "d_w1")
                    p.wait(te, "d_x8")
                    p.wait(te, "v_ffc", max(0, 2 * nH))
                    lg_ = None
                    for g in range(2):
                        for f in range(FIN):
                            lg_ = te.matmul(
                                ps_g[g][:],
                                w1b[bi][:, (f * 2 + g) * 128:
                                        (f * 2 + g + 1) * 128],
                                xn8t[:, f * TGF:(f + 1) * TGF],
                                start=(f == 0), stop=(f == FIN - 1))
                    p.inc(te.drain(), "t_g1")
                    p.wait(ve, "t_g1")
                    ve.tensor_copy(out=ffg[:], in_=ps_g[0][:])
                    ve.tensor_copy(out=ff1[:], in_=ps_g[1][:])
                    p.inc(ve.drain(), "v_ffc", 2)
                    ve.tensor_mul(gt1[:], ffg[:], ffg[:])
                    ve.tensor_mul(gt2[:], gt1[:], ffg[:])
                    ve.tensor_scalar(out=gt1[:], in0=gt2[:],
                                     scalar1=0.044677734375, scalar2=None,
                                     op0=ALU.mult)
                    ve.tensor_add(gt2[:], ffg[:], gt1[:])
                    ve.tensor_scalar(out=gt1[:], in0=gt2[:],
                                     scalar1=0.796875, scalar2=None,
                                     op0=ALU.mult)
                    p.inc(ve.drain(), "v_gel")
                    p.wait(sc, "v_gel")
                    sc.activation(out=gvt[:], in_=gt1[:], func=AF.Tanh)
                    p.inc(sc.drain(), "s_gelu")
                    p.wait(ve, "s_gelu")
                    ve.tensor_scalar(out=gt2[:], in0=gvt[:], scalar1=1.0,
                                     scalar2=0.5, op0=ALU.add, op1=ALU.mult)
                    ve.tensor_mul(gt1[:], ffg[:], gt2[:])
                    ve.tensor_mul(prod[:], gt1[:], ff1[:])
                    ve.tensor_copy(
                        out=act8[:, hc * TGF:(hc + 1) * TGF], in_=prod[:])
                    p.inc(ve.drain(), "v_ff")
                    nH += 1
                for t4 in range(TGF // 128):
                    ti = tg * (TGF // 128) + t4
                    p.wait(te, "v_fo", max(0, nO * NFG))
                    for hc in range(HC):
                        bi = hc % 2
                        p.wait(sy, "t_o", max(0, p.n("t_o") - 1))
                        p.inc(sy.dma_start(
                            out=wl8b[bi][:],
                            in_=wl8m[hc * 128:(hc + 1) * 128, :]), "d_wl", 16)
                        p.wait(te, "d_wl")
                        p.wait(te, "v_ff")
                        lo = None
                        for fg in range(NFG):
                            lo = te.matmul(
                                ps_o[fg][:],
                                act8[:, hc * TGF + t4 * 128:
                                     hc * TGF + (t4 + 1) * 128],
                                wl8b[bi][:, fg * 512:(fg + 1) * 512],
                                start=(hc == 0), stop=(hc == HC - 1))
                        if hc == HC - 1:
                            p.inc(te.drain(), "t_o")
                        else:
                            p.inc(lo, "t_o")
                    p.wait(ve, "t_o")
                    for fg in range(NFG):
                        p.inc(ve.tensor_copy(
                            out=ffnO[:, ti * D + fg * 512:
                                     ti * D + (fg + 1) * 512],
                            in_=ps_o[fg][:]), "v_fo")
                    nO += 1
            barrier()

        if cfg.stop_after < 7:
            _finish(p, gp, out, cs, dbg, debug_taps, locals())
            return nc, dbg

        with ExitStack() as ph:
            sb = lambda n, sh, dt=F32: ph.enter_context(
                nc.sbuf_tensor(n, list(sh), dt))
            ps = lambda n, sh, dt=F32: ph.enter_context(
                nc.psum_tensor(n, list(sh), dt))
            wo_bf = sb("wo_bf", [128, FIN * D], BF16)
            wof = [sb("wof%d" % i, [128, D]) for i in range(2)]
            xr = [sb("xr%d" % i, [128, D]) for i in range(2)]
            outb = [sb("outb%d" % i, [128, D]) for i in range(2)]
            ps_w = [ps("ps_w%d" % i, [128, 512]) for i in range(NFG)]

            for f in range(FIN):
                bi = f % 2
                p.wait(sy, "v_wo", max(0, f - 1))
                p.inc(sy.dma_start(out=wof[bi][:],
                                   in_=wo_t[f * 128:(f + 1) * 128, :]),
                      "d_wo", 16)
                p.wait(ve, "d_wo")
                ve.tensor_copy(out=wo_bf[:, f * D:(f + 1) * D],
                               in_=wof[bi][:])
                p.inc(ve.drain(), "v_wo")
            for tt in range(OTT):
                bi = tt % 2
                p.wait(sy, "v_out", max(0, (tt - 1) * NFG))
                p.inc(sy.dma_start(out=xr[bi][:],
                                   in_=xown[tt * 128:(tt + 1) * 128, :]),
                      "d_xr", 16)
                p.wait(te, "v_wo")
                p.wait(te, "v_out", tt * NFG)
                lw = None
                for f in range(FIN):
                    for fg in range(NFG):
                        lw = te.matmul(
                            ps_w[fg][:],
                            attnF[:, f * ownT + tt * 128:
                                  f * ownT + (tt + 1) * 128],
                            wo_bf[:, f * D + fg * 512:f * D + (fg + 1) * 512],
                            start=(f == 0), stop=(f == FIN - 1))
                p.inc(te.drain(), "t_wo")
                p.wait(ve, "t_wo")
                p.wait(ve, "d_xr")
                p.wait(ve, "g_out", max(0, (tt - 1) * 16))
                for fg in range(NFG):
                    fsl = slice(fg * 512, (fg + 1) * 512)
                    ve.tensor_add(outb[bi][:, fsl], ps_w[fg][:],
                                  xr[bi][:, fsl])
                    ve.tensor_add(
                        outb[bi][:, fsl], outb[bi][:, fsl],
                        ffnO[:, tt * D + fg * 512:tt * D + (fg + 1) * 512])
                    p.inc(ve.drain(), "v_out")
                p.wait(gp, "v_out")
                p.inc(gp.dma_start(out=out[tt * 128:(tt + 1) * 128, :],
                                   in_=outb[bi][:]), "g_out", 16)
            _finish(p, gp, out, cs, dbg, debug_taps, locals())
    return nc, dbg


def _finish(p, gp, out, cs, dbg, debug_taps, env):
    nc = p.nc
    for name in debug_taps:
        if name in dbg:
            continue
        tile = None
        m = re.match(r"^(tsum)(\d+)$", name)
        if m:
            tile = env[m.group(1)][int(m.group(2))]
        elif name in env:
            tile = env[name]
        if tile is None:
            continue
        shp = list(tile.shape)
        d = nc.dram_tensor("dbg_" + name, shp, tile.dtype,
                           kind="ExternalOutput").ap()
        dbg[name] = d
        p.inc(gp.dma_start(out=d, in_=tile[:]), "g_out", 16)
    p.wait(gp, "g_out")



def prep_inputs(cfg, x, Wq, Wk, Wv, Wo, w_gating, w_linear):
    B, S, D = cfg.B, cfg.S, cfg.D
    hpc, dh, cpb = cfg.hpc, cfg.dh, cfg.cpb
    tabs = _rope_tables(cfg)
    xf = np.ascontiguousarray(np.asarray(x, dtype=np.float32).reshape(B * S, D))
    wo_tv = np.ascontiguousarray(np.asarray(Wo, dtype=np.float32).T)
    wgv = np.ascontiguousarray(np.asarray(w_gating, dtype=np.float32))
    wlv = np.ascontiguousarray(np.asarray(w_linear, dtype=np.float32))
    Wq = np.asarray(Wq, dtype=np.float32)
    Wk = np.asarray(Wk, dtype=np.float32)
    Wv = np.asarray(Wv, dtype=np.float32)
    in_maps = []
    for c in range(cfg.n_cores):
        b = c // cpb
        hs = hpc * (c % 2)
        m = {}
        m["xb"] = np.ascontiguousarray(xf[b * S:(b + 1) * S])
        m["xown"] = np.ascontiguousarray(xf[c * cfg.ownT:(c + 1) * cfg.ownT])
        m["wq_t"] = np.ascontiguousarray(Wq[hs * dh:(hs + hpc) * dh, :].T)
        m["wk_t"] = np.ascontiguousarray(Wk[hs * dh:(hs + hpc) * dh, :].T)
        m["wv_t"] = np.ascontiguousarray(Wv[hs * dh:(hs + hpc) * dh, :].T)
        m["wo_t"] = wo_tv
        m["wg"] = wgv
        m["wl"] = wlv
        for k, v in tabs.items():
            m[k] = v
        in_maps.append(m)
    return in_maps


def kernel(L0_attn_hidden_states, L0_attn_Wq, L0_attn_Wk, L0_attn_Wv,
           L0_attn_Wo, L0_ffn_w_gating, L0_ffn_w_linear):
    cfg = Cfg()
    in_maps = prep_inputs(cfg, L0_attn_hidden_states, L0_attn_Wq,
                          L0_attn_Wk, L0_attn_Wv, L0_attn_Wo,
                          L0_ffn_w_gating, L0_ffn_w_linear)
    nc, _ = build_program(cfg)
    res = run_bass_kernel_spmd(nc, in_maps, core_ids=list(range(cfg.n_cores)))
    shards = [r["out"] for r in res.results]
    outv = np.concatenate(shards, axis=0).reshape(cfg.B, cfg.S, cfg.D)
    return outv


# version 07
# speedup vs baseline: 1.0874x; 1.0874x over previous
import math
import re
from contextlib import ExitStack

import numpy as np
import ml_dtypes

import concourse.bass as bass
import concourse.mybir as mybir
from concourse.bass_utils import run_bass_kernel_spmd
from concourse.expressions import RuntimeValue

F32 = mybir.dt.float32
BF16 = mybir.dt.bfloat16
FP8 = mybir.dt.float8e4
I32 = mybir.dt.int32
AF = mybir.ActivationFunctionType
ALU = mybir.AluOpType
AX = mybir.AxisListType

NBIG = -1.0e38
CSHIFT = 14.0
MAGIC_SENT = 0x0BAD0001
MAGIC_ATTN = 0x51CA0001


class Cfg:
    def __init__(self, B=2, S=4096, D=2048, H=16, FF=8192, n_cores=8,
                 bisect_iters=42, theta=10000.0, stop_after=99):
        self.B, self.S, self.D, self.H, self.FF = B, S, D, H, FF
        self.n_cores = n_cores
        self.dh = 128
        self.theta = theta
        self.bisect_iters = bisect_iters
        self.stop_after = stop_after
        assert D == H * self.dh
        self.cpb = n_cores // B
        assert self.cpb % 2 == 0
        self.hpc = H // 2
        self.ownT = B * S // n_cores
        self.heavy = int(0.1 * S)
        self.recent = int(0.1 * S)
        self.FIN = D // 128
        self.KC = S // 128
        self.CH = S // 512
        self.QT = S // 128
        self.QG = S // 512
        self.HC = FF // 128
        self.TT = S // 128
        self.TCOL = S // 512
        self.OTT = self.ownT // 128
        self.TGF = min(512, self.ownT)
        self.NTG = self.ownT // self.TGF
        self.HD2 = self.hpc * self.dh
        assert self.HD2 % 512 == 0
        self.HG = self.HD2 // 512
        self.UPG = min(self.hpc, 128 // self.KC)
        self.NBG = (self.hpc + self.UPG - 1) // self.UPG
        assert self.hpc % self.UPG == 0
        assert self.CH >= 1 and S % 512 == 0
        assert self.ownT % self.TGF == 0 and D % 512 == 0


def _rope_tables(cfg):
    dh = cfg.dh
    ex = np.arange(0, dh, 2, dtype=np.float64) / dh
    inv = 1.0 / (cfg.theta ** ex)
    pos = np.arange(cfg.S, dtype=np.float64)
    fr = inv[:, None] * pos[None, :]
    cos = np.cos(fr)
    sin = np.sin(fr)
    cos_full = np.concatenate([cos, cos], 0)
    sinpm = np.concatenate([-sin, sin], 0)
    qs = 1.0 / math.sqrt(dh)
    out = {}
    for name, tab in (("cq", cos_full * qs), ("sq", sinpm * qs),
                      ("ck", cos_full), ("sk", sinpm)):
        hi = tab.astype(np.float32)
        lo = (tab - hi.astype(np.float64)).astype(np.float32)
        out[name + "_hi"] = np.ascontiguousarray(hi)
        out[name + "_lo"] = np.ascontiguousarray(lo)
    return out


def _const_tables(cfg):
    c = {}
    c["ident_bf"] = np.eye(128, dtype=ml_dtypes.bfloat16)
    c["ident_f32"] = np.eye(128, dtype=np.float32)
    c["ones_col_bf"] = np.ones((128, 1), dtype=ml_dtypes.bfloat16)
    c["ones_row_f32"] = np.ones((1, 128), dtype=np.float32)
    p = np.arange(128, dtype=np.float32).reshape(128, 1)
    j = np.arange(512, dtype=np.float32)[None, :]
    big = np.float32(NBIG)
    for r in range(4):
        c["p1tri_%d" % r] = np.ascontiguousarray(np.where(
            j < 128 * r + p + 1, np.float32(0), big).astype(ml_dtypes.bfloat16))
    c["eps_col"] = np.full((128, 1), 1e-5, dtype=np.float32)
    c["negC_col"] = np.full((128, 1), -CSHIFT, dtype=np.float32)
    for rel in range(-512, 385, 128):
        cs_ = np.clip(rel + p, 0, 512)
        be_ = np.clip(rel + p + cfg.recent + 1, 0, 512)
        c["p3ca_%d" % rel] = np.ascontiguousarray(np.where(
            j >= cs_, np.float32(0), big).astype(ml_dtypes.bfloat16))
        c["p3ba_%d" % rel] = np.ascontiguousarray(np.where(
            (j >= cs_) & (j < be_), np.float32(0), big).astype(ml_dtypes.bfloat16))
    for ci in range(cfg.CH):
        m = np.zeros((128, cfg.CH), dtype=np.float32)
        m[:, ci] = 1.0
        c["oh_%d" % ci] = m
    upg = cfg.UPG
    pbg = upg * cfg.KC
    sel = np.zeros((pbg, upg), dtype=np.float32)
    for u in range(upg):
        sel[u * cfg.KC:(u + 1) * cfg.KC, u] = 1
    c["bis_sel"] = sel
    bc = np.zeros((upg, pbg), dtype=np.float32)
    for u in range(upg):
        bc[u, u * cfg.KC:(u + 1) * cfg.KC] = 1
    c["bis_bc"] = bc
    return c


class P:
    def __init__(self, nc, ctx):
        self.nc = nc
        self.ctx = ctx
        self.sems = {}
        self.cnt = {}

    def s(self, name):
        if name not in self.sems:
            self.sems[name] = self.ctx.enter_context(self.nc.semaphore(name))
            self.cnt[name] = 0
        return self.sems[name]

    def inc(self, instr, name, k=1):
        instr.then_inc(self.s(name), k)
        self.cnt[name] += k
        return self.cnt[name]

    def wait(self, eng, name, value=None):
        v = self.cnt.get(name, 0) if value is None else value
        self.s(name)
        if v > 0:
            eng.wait_ge(self.sems[name], v)

    def n(self, name):
        self.s(name)
        return self.cnt[name]


def build_program(cfg, debug_taps=(), hbm_taps=()):
    nc = bass.Bass("TRN2", target_bir_lowering=False, debug=False,
                   enable_partition_id=False, detect_race_conditions=False)

    def internal(name):
        return "ExternalOutput" if name in hbm_taps else "Internal"
    B, S, D, FF, dh = cfg.B, cfg.S, cfg.D, cfg.FF, cfg.dh
    hpc, FIN, KC, CH, QT, QG, HC = (cfg.hpc, cfg.FIN, cfg.KC, cfg.CH,
                                    cfg.QT, cfg.QG, cfg.HC)
    ownT, TT, TCOL, OTT, TGF, NTG = (cfg.ownT, cfg.TT, cfg.TCOL, cfg.OTT,
                                     cfg.TGF, cfg.NTG)
    HD2, HG = cfg.HD2, cfg.HG
    NFG = D // 512

    xb = nc.dram_tensor("xb", [S, D], F32, kind="ExternalInput").ap()
    xown = nc.dram_tensor("xown", [ownT, D], F32, kind="ExternalInput").ap()
    wq_t = nc.dram_tensor("wq_t", [D, HD2], F32, kind="ExternalInput").ap()
    wk_t = nc.dram_tensor("wk_t", [D, HD2], F32, kind="ExternalInput").ap()
    wv_t = nc.dram_tensor("wv_t", [D, HD2], F32, kind="ExternalInput").ap()
    wo_t = nc.dram_tensor("wo_t", [D, D], F32, kind="ExternalInput").ap()
    wg = nc.dram_tensor("wg", [2, D, FF], F32, kind="ExternalInput").ap()
    wl = nc.dram_tensor("wl", [FF, D], F32, kind="ExternalInput").ap()
    tabs = {}
    for nme in ("cq_hi", "cq_lo", "sq_hi", "sq_lo",
                "ck_hi", "ck_lo", "sk_hi", "sk_lo"):
        tabs[nme] = nc.dram_tensor(nme, [128, S], F32, kind="ExternalInput").ap()
    out = nc.dram_tensor("out", [ownT, D], F32, kind="ExternalOutput").ap()

    wg8 = nc.dram_tensor("wg8", [2 * D * FF // 4096, 4096], FP8,
                         kind="Internal").ap()
    wl8 = nc.dram_tensor("wl8", [FF * D // 4096, 4096], FP8,
                         kind="Internal").ap()
    wqkv16 = nc.dram_tensor("wqkv16", [3 * D * HD2 // 4096, 4096], BF16,
                            kind="Internal").ap()
    xn8 = nc.dram_tensor("xn8", [FIN * 128, ownT], FP8,
                         kind=internal("xn8")).ap()
    q_hbm = nc.dram_tensor("q_hbm", [hpc * 128, S], BF16,
                           kind=internal("q_hbm")).ap()
    k_hbm = nc.dram_tensor("k_hbm", [hpc * 128, S], BF16,
                           kind=internal("k_hbm")).ap()
    v_hbm = nc.dram_tensor("v_hbm", [128, hpc * TT * 128], BF16,
                           kind=internal("v_hbm")).ap()
    at_hbm = nc.dram_tensor("at_hbm", [hpc * 128, S], BF16,
                            kind=internal("at_hbm")).ap()
    sh_attn = nc.dram_tensor("sh_attn", [2, hpc * 128, ownT], BF16,
                             kind="Internal", addr_space="Shared").ap()
    sh_flag = nc.dram_tensor("sh_flag", [2, 1, 32], I32,
                             kind="Internal", addr_space="Shared").ap()

    dbg = {}
    consts = _const_tables(cfg)
    cth = {k: nc.inline_tensor(np.ascontiguousarray(v), "c_" + k)
           for k, v in consts.items()}

    outer = ExitStack()
    p = P(nc, outer)
    sy, te, ve, sc, gp = nc.sync, nc.tensor, nc.vector, nc.scalar, nc.gpsimd

    def barrier():
        nc.all_engine_barrier()

    def arm_parity(reg, mod):
        for arm in gp.switch_core_id(n=128):
            try:
                lg = arm.logical % 8
            except Exception:
                lg = 0
            gp.reg_mov(reg, lg % mod)
        gp.snap(reg, min_val=0, max_val=mod - 1)

    with outer:
        cs = {}
        for k, v in consts.items():
            t = outer.enter_context(
                nc.sbuf_tensor("cs_" + k, list(v.shape),
                               mybir.dt.from_np(v.dtype)))
            p.inc(sy.dma_start(out=t[:], in_=cth[k].ap()), "d_c", 16)
            cs[k] = t

        tsum = [outer.enter_context(
            nc.sbuf_tensor("tsum%d" % g, [128, 128], F32))
            for g in range(cfg.NBG)]
        Aadd = outer.enter_context(
            nc.sbuf_tensor("Aadd", [128, hpc * KC], F32))
        Amin = outer.enter_context(
            nc.sbuf_tensor("Amin", [128, hpc * KC], F32))
        Abias = outer.enter_context(
            nc.sbuf_tensor("Abias", [128, hpc * KC], F32))
        lntap = outer.enter_context(nc.sbuf_tensor("lntap", [128, 8], F32))
        for g in range(cfg.NBG):
            ve.memset(tsum[g][:], 0.0)

        if not getattr(cfg, "dbg_no_sentinel", False):
            with gp.register("oc0") as oc0:
                arm_parity(oc0, 2)
                with gp.register("sv0") as sv0:
                    gp.reg_mov(sv0, MAGIC_SENT)
                    for s in range(2):
                        with gp.If_eq(oc0, s):
                            gp.store(sh_flag[s, 0:1, 1:2], sv0)

        with ExitStack() as ph:
            qb = [ph.enter_context(nc.sbuf_tensor("qb%d" % i, [128, 4096], F32))
                  for i in range(2)]
            qb8 = [ph.enter_context(nc.sbuf_tensor("qb8%d" % i, [128, 4096],
                                                   FP8))
                   for i in range(2)]
            qb16 = [ph.enter_context(nc.sbuf_tensor("qb16%d" % i, [128, 4096],
                                                    BF16))
                    for i in range(2)]
            wgf = wg.rearrange("a b c -> (a b c)").rearrange(
                "(n p m) -> n p m", p=128, m=4096)
            wlf = wl.rearrange("a b -> (a b)").rearrange(
                "(n p m) -> n p m", p=128, m=4096)
            wg8d = wg8.rearrange("(n p) m -> n p m", p=128)
            wl8d = wl8.rearrange("(n p) m -> n p m", p=128)
            wqkvd = wqkv16.rearrange("(n p) m -> n p m", p=128)
            nqkv = D * HD2 // (128 * 4096)
            srcs16 = []
            for wi, w_src in enumerate((wq_t, wk_t, wv_t)):
                wsf = w_src.rearrange("a b -> (a b)").rearrange(
                    "(n p m) -> n p m", p=128, m=4096)
                for i in range(nqkv):
                    srcs16.append((wsf[i], wqkvd[wi * nqkv + i]))
            ntile_g = 2 * D * FF // (128 * 4096)
            ntile_l = FF * D // (128 * 4096)
            n16 = len(srcs16)
            nW = 0 if getattr(cfg, "dbg_no_phasew", False) else \
                n16 + ntile_g + ntile_l
            if getattr(cfg, "dbg_phasew_nops", False):
                nW = 0
                for e_ in (ve, sc, te, gp, sy):
                    for _ in range(1500):
                        e_.nop()
            wmode = getattr(cfg, "dbg_phasew_mode", "full")
            for i in range(nW):
                bi = i % 2
                if i < n16:
                    srcA, dstA = srcs16[i]
                    stg = qb16[bi]
                elif i < n16 + ntile_g:
                    srcA, dstA = wgf[i - n16], wg8d[i - n16]
                    stg = qb8[bi]
                else:
                    srcA = wlf[i - n16 - ntile_g]
                    dstA = wl8d[i - n16 - ntile_g]
                    stg = qb8[bi]
                if wmode in ("full", "load"):
                    if wmode == "load":
                        p.wait(sy, "d_q", max(0, (i - 1) * 16))
                    else:
                        p.wait(sy, "g_q8", max(0, (i - 1) * 16))
                    p.inc(sy.dma_start(out=qb[bi][:], in_=srcA), "d_q", 16)
                if wmode in ("full", "cast"):
                    eng = ve if i % 2 == 0 else sc
                    p.wait(eng, "d_q")
                    if eng is ve:
                        ve.tensor_copy(out=stg[:], in_=qb[bi][:])
                    else:
                        sc.copy(out=stg[:], in_=qb[bi][:])
                    p.inc(eng.drain(), "c_q8")
                if wmode in ("full", "store"):
                    sub = getattr(cfg, "dbg_store_subset", "")
                    skip = (sub == "16" and i >= n16) or \
                           (sub == "g" and not (n16 <= i < n16 + ntile_g)) or \
                           (sub == "l" and i < n16 + ntile_g)
                    if sub == "16x":
                        dstA = wqkvd[i % nqkv]
                        stg = qb16[bi]
                        skip = False
                    if not skip:
                        p.wait(gp, "c_q8")
                        p.inc(gp.dma_start(out=dstA, in_=stg[:]), "g_q8", 16)
            p.wait(gp, "g_q8")
            if getattr(cfg, "dbg_zero_qb", False):
                p.wait(ve, "g_q8")
                for t_ in qb + qb8 + qb16:
                    ve.memset(t_[:], 0.0)
                ve.drain()
            barrier()

        if "wdump" in debug_taps:
            dwd = nc.dram_tensor("dbg_wdump", [128, 1664], F32,
                                 kind="ExternalOutput").ap()
            dbg["wdump"] = dwd
            p.inc(gp.dma_start(out=dwd[:, 0:128],
                               in_=cth["ident_f32"].ap()), "g_wd", 16)
            p.inc(gp.dma_start(out=dwd[:, 128:640],
                               in_=tabs["cq_hi"][:, 0:512]), "g_wd", 16)
            p.inc(gp.dma_start(out=dwd[:, 640:1152],
                               in_=xb[0:128, 0:512]), "g_wd", 16)
            p.inc(gp.dma_start(out=dwd[:, 1152:1664],
                               in_=wo_t[0:128, 0:512]), "g_wd", 16)
            p.wait(gp, "g_wd")
            barrier()

        with ExitStack() as ph:
            sb = lambda n, sh, dt=F32: ph.enter_context(
                nc.sbuf_tensor(n, list(sh), dt))
            ps = lambda n, sh, dt=F32: ph.enter_context(
                nc.psum_tensor(n, list(sh), dt))
            wqkb = [sb("wqkb%d" % i, [128, FIN * HD2], BF16) for i in range(2)]
            x8s = sb("x8s", [128, 128], FP8)
            xf = [sb("xf%d" % i, [128, D]) for i in range(2)]
            xnf = sb("xnf", [128, D])
            xnb = sb("xnb", [128, D], BF16)
            xsq = sb("xsq", [128, D])
            smn = sb("smn", [128, 1])
            ssq = sb("ssq", [128, 1])
            mean = sb("mean", [128, 1])
            ex2 = sb("ex2", [128, 1])
            t1v = sb("t1v", [128, 1])
            var1 = sb("var1", [128, 1])
            sd1 = sb("sd1", [128, 1])
            rstd1 = sb("rstd1", [128, 1])
            nmr1 = sb("nmr1", [128, 1])
            xnTc = sb("xnTc", [128, FIN * 512], BF16)
            thi = sb("thi", [128, 512])
            tb4 = [sb("tb4_%d" % i, [128, 512]) for i in range(4)]
            qraw = sb("qraw", [128, 512], BF16)
            kraw = sb("kraw", [128, 512], BF16)
            qrot = sb("qrot", [128, 512], BF16)
            krot = sb("krot", [128, 512], BF16)
            rt1 = sb("rt1", [128, 512], BF16)
            qcol = sb("qcol", [128, 512], BF16)
            kcol = sb("kcol", [128, 512], BF16)
            vcol = sb("vcol", [128, HD2], BF16)
            ps_tb = [ps("ps_tb%d" % i, [128, 128], BF16) for i in range(2)]
            ps_tf = [ps("ps_tf%d" % i, [128, 128]) for i in range(2)]
            ps_q = ps("ps_q", [128, 512])
            ps_k = ps("ps_k", [128, 512])
            ps_v = [ps("ps_v%d" % i, [128, 512]) for i in range(HG)]

            wqkvv = wqkv16.rearrange("a b -> (a b)").rearrange(
                "(w d h) -> w d h", w=3, d=D)

            def load_w(widx, tcol):
                gi = 3 * tcol + widx
                bi = gi % 2
                p.wait(sy, "t_wq", max(0, gi - 1))
                for f in range(FIN):
                    p.inc(sy.dma_start(
                        out=wqkb[bi][:, f * HD2:(f + 1) * HD2],
                        in_=wqkvv[widx, f * 128:(f + 1) * 128, :]), "d_wq", 16)
                return bi

            def ln_tile(src_ap, dst_f32, dst_bf=None):
                bi = p.n("d_x") // 16 % 2
                p.wait(sy, "s_ln", max(0, p.n("s_ln") - 1))
                p.inc(sy.dma_start(out=xf[bi][:], in_=src_ap), "d_x", 16)
                if getattr(cfg, "dbg_no_ln", False):
                    p.wait(ve, "d_x")
                    p.wait(ve, "t_tp")
                    ve.tensor_copy(out=dst_f32[:], in_=xf[bi][:])
                    p.inc(ve.drain(), "s_ln")
                    if dst_bf is not None:
                        ve.tensor_copy(out=dst_bf[:], in_=dst_f32[:])
                        p.inc(ve.drain(), "v_xnb")
                    return
                p.wait(sc, "d_x")
                p.wait(sc, "v_sq", p.n("v_sq"))
                sc.activation(out=xsq[:], in_=xf[bi][:], func=AF.Square,
                              accum_out=ssq[:])
                p.inc(sc.drain(), "s_sq")
                p.wait(ve, "d_x")
                ve.reduce_sum(smn[:], xf[bi][:], axis=AX.X)
                ve.drain()
                p.wait(ve, "s_sq")
                p.inc(ve.tensor_scalar(out=ex2[:], in0=ssq[:],
                                       scalar1=1.0 / D, scalar2=None,
                                       op0=ALU.mult), "v_sq")
                ve.tensor_scalar(out=mean[:], in0=smn[:], scalar1=1.0 / D,
                                 scalar2=None, op0=ALU.mult)
                ve.drain()
                ve.tensor_mul(t1v[:], mean[:], mean[:])
                ve.drain()
                ve.tensor_sub(var1[:], ex2[:], t1v[:])
                p.inc(ve.drain(), "v_ln")
                p.wait(sc, "v_ln")
                sc.activation(out=sd1[:], in_=var1[:], func=AF.Sqrt,
                              bias=cs["eps_col"][:, 0:1])
                p.inc(sc.drain(), "s_sd")
                p.wait(ve, "s_sd")
                ve.reciprocal(out=rstd1[:], in_=sd1[:])
                ve.drain()
                ve.tensor_mul(nmr1[:], mean[:], rstd1[:])
                ve.drain()
                ve.tensor_scalar(out=nmr1[:], in0=nmr1[:], scalar1=-1.0,
                                 scalar2=None, op0=ALU.mult)
                p.inc(ve.drain(), "v_rs")
                p.wait(sc, "v_rs")
                p.wait(sc, "t_tp")
                sc.activation(out=dst_f32[:], in_=xf[bi][:],
                              func=AF.Identity, bias=nmr1[:],
                              scale=rstd1[:])
                p.inc(sc.drain(), "s_ln")
                if dst_bf is not None:
                    p.wait(ve, "s_ln")
                    ve.tensor_copy(out=dst_bf[:], in_=dst_f32[:])
                    p.inc(ve.drain(), "v_xnb")
                if "lntap" in debug_taps and p.n("s_ln") == 1:
                    p.wait(gp, "v_rs")
                    for j, tt_ in enumerate((smn, ssq, mean, ex2, var1, sd1,
                                             rstd1, nmr1)):
                        p.inc(gp.dma_start(out=lntap[:, j:j + 1],
                                           in_=tt_[:]), "g_lt", 16)

            def transpose128(src, dst, f32, ready):
                n = p.n("t_tp")
                pb = n % 2
                pt = ps_tf[pb] if f32 else ps_tb[pb]
                p.wait(te, ready)
                p.wait(te, "v_tp", max(0, n - 1))
                te.transpose(pt[:, 0:128], src,
                             cs["ident_f32" if f32 else "ident_bf"][:])
                p.inc(te.drain(), "t_tp")
                p.wait(ve, "t_tp")
                ve.tensor_copy(out=dst, in_=pt[:, 0:128])
                p.inc(ve.drain(), "v_tp")

            for tcol in range(TCOL):
                tsl = slice(tcol * 512, (tcol + 1) * 512)
                no_qkv = getattr(cfg, "dbg_no_qkv", False)
                no_rope = getattr(cfg, "dbg_no_rope", False)
                for i, nme in enumerate(("cq", "sq", "ck", "sk")):
                    if no_qkv or no_rope:
                        break
                    p.wait(sy, "v_rope", max(0, 2 * hpc * tcol))
                    p.wait(sy, "v_tab", max(0, 4 * tcol + i))
                    p.inc(sy.dma_start(out=thi[:],
                                       in_=tabs[nme + "_hi"][:, tsl]),
                          "d_tab", 16)
                    p.inc(sy.dma_start(out=tb4[i][:],
                                       in_=tabs[nme + "_lo"][:, tsl]),
                          "d_tab", 16)
                    p.wait(ve, "d_tab")
                    p.inc(ve.tensor_add(tb4[i][:], thi[:], tb4[i][:]), "v_tab")

                if not no_qkv:
                    p.wait(ve, "t_qk", max(0, 2 * hpc * tcol))
                    p.wait(ve, "t_v", max(0, 4 * HG * tcol))
                for t4 in range(4):
                    ti = tcol * 4 + t4
                    ln_tile(xb[ti * 128:(ti + 1) * 128, :], xnf, xnb)
                    for f in range(FIN):
                        transpose128(
                            xnb[:, f * 128:(f + 1) * 128],
                            xnTc[:, f * 512 + t4 * 128:f * 512 + (t4 + 1) * 128],
                            f32=False, ready="v_xnb")

                if tcol == 0 and "xnTc" in debug_taps:
                    dxnTc = nc.dram_tensor("dbg_xnTc", [128, FIN * 512], BF16,
                                           kind="ExternalOutput").ap()
                    dxnb = nc.dram_tensor("dbg_xnb", [128, D], BF16,
                                          kind="ExternalOutput").ap()
                    dbg["xnTc"] = dxnTc
                    dbg["xnb"] = dxnb
                    tapeng = gp if getattr(cfg, "dbg_tap_gp", True) else sy
                    p.wait(tapeng, "v_tp")
                    p.wait(tapeng, "v_xnb")
                    p.inc(tapeng.dma_start(out=dxnTc, in_=xnTc[:]),
                          "g_dbg", 16)
                    p.inc(tapeng.dma_start(out=dxnb, in_=xnb[:]),
                          "g_dbg", 16)
                    p.wait(te, "g_dbg")

                if no_qkv:
                    continue
                bq = load_w(0, tcol)
                bk = load_w(1, tcol)
                bv = load_w(2, tcol)
                dwq = p.n("d_wq")
                for widx, (bw, hbmt) in enumerate(((bq, q_hbm), (bk, k_hbm))):
                    p.wait(te, "d_wq", dwq - (2 - widx) * 16 * FIN)
                    p.wait(te, "v_tp")
                    raw = qraw if widx == 0 else kraw
                    rot = qrot if widx == 0 else krot
                    colt = qcol if widx == 0 else kcol
                    ti0, ti1 = (0, 1) if widx == 0 else (2, 3)
                    for h in range(hpc):
                        pdst = ps_q if h % 2 == 0 else ps_k
                        p.wait(te, "v_qk", max(0, p.n("t_qk") - 1))
                        lastm = None
                        for f in range(FIN):
                            lastm = te.matmul(
                                pdst[:],
                                wqkb[bw][:, f * HD2 + h * dh:
                                         f * HD2 + (h + 1) * dh],
                                xnTc[:, f * 512:(f + 1) * 512],
                                start=(f == 0), stop=(f == FIN - 1))
                        p.inc(te.drain(), "t_qk")
                        p.wait(ve, "t_qk")
                        p.wait(ve, "g_rot", p.n("g_rot"))
                        ve.tensor_copy(out=raw[:], in_=pdst[:])
                        p.inc(ve.drain(), "v_qk")
                        if no_rope:
                            p.wait(ve, "g_qk", p.n("g_qk"))
                            ve.tensor_copy(out=colt[:], in_=raw[:])
                            p.inc(ve.drain(), "v_rope")
                            p.wait(gp, "v_rope")
                            p.inc(gp.dma_start(
                                out=hbmt[h * 128:(h + 1) * 128, tsl],
                                in_=colt[:]), "g_qk", 16)
                            continue
                        p.wait(gp, "v_qk")
                        p.inc(gp.dma_start(out=rot[0:64, :],
                                           in_=raw[64:128, :]), "g_rot", 16)
                        p.inc(gp.dma_start(out=rot[64:128, :],
                                           in_=raw[0:64, :]), "g_rot", 16)
                        p.wait(ve, "v_tab")
                        p.wait(ve, "g_rot")
                        p.wait(ve, "g_qk", p.n("g_qk"))
                        ve.tensor_mul(rt1[:], raw[:], tb4[ti0][:])
                        ve.tensor_mul(colt[:], rot[:], tb4[ti1][:])
                        ve.tensor_add(colt[:], rt1[:], colt[:])
                        p.inc(ve.drain(), "v_rope")
                        p.wait(gp, "v_rope")
                        p.inc(gp.dma_start(
                            out=hbmt[h * 128:(h + 1) * 128, tsl],
                            in_=colt[:]), "g_qk", 16)
                    p.inc(te.nop(), "t_wq")

                p.wait(te, "d_wq", dwq)
                for hg in range(HG):
                    for t4 in range(4):
                        ti = tcol * 4 + t4
                        p.wait(te, "v_v", p.n("t_v"))
                        lastv = None
                        for f in range(FIN):
                            lastv = te.matmul(
                                ps_v[hg][:],
                                xnTc[:, f * 512 + t4 * 128:
                                     f * 512 + (t4 + 1) * 128],
                                wqkb[bv][:, f * HD2 + hg * 512:
                                         f * HD2 + (hg + 1) * 512],
                                start=(f == 0), stop=(f == FIN - 1))
                        p.inc(te.drain(), "t_v")
                        p.wait(ve, "t_v")
                        p.wait(ve, "g_v", p.n("g_v"))
                        ve.tensor_copy(
                            out=vcol[:, hg * 512:(hg + 1) * 512],
                            in_=ps_v[hg][:])
                        p.inc(ve.drain(), "v_v")
                        p.wait(gp, "v_v")
                        for hh in range(4):
                            u = hg * 4 + hh
                            p.inc(gp.dma_start(
                                out=v_hbm[:, (u * TT + ti) * 128:
                                          (u * TT + ti + 1) * 128],
                                in_=vcol[:, hg * 512 + hh * 128:
                                         hg * 512 + (hh + 1) * 128]),
                                "g_v", 16)
                p.inc(te.nop(), "t_wq")

            for tt in range(0 if getattr(cfg, "dbg_no_xn8", False) else OTT):
                ln_tile(xown[tt * 128:(tt + 1) * 128, :], xnf, None)
                for f in range(FIN):
                    p.wait(ve, "g_x8", p.n("g_x8"))
                    transpose128(xnf[:, f * 128:(f + 1) * 128],
                                 x8s[:, 0:128], f32=True, ready="s_ln")
                    p.wait(gp, "v_tp")
                    p.inc(gp.dma_start(
                        out=xn8[f * 128:(f + 1) * 128,
                                tt * 128:(tt + 1) * 128],
                        in_=x8s[:, 0:128]), "g_x8", 16)
            p.wait(gp, "g_qk")
            p.wait(gp, "g_v")
            p.wait(gp, "g_x8")
            barrier()

        if cfg.stop_after < 2:
            _finish(p, gp, out, cs, dbg, debug_taps, locals())
            return nc, dbg

        with ExitStack() as ph:
            sb = lambda n, sh, dt=F32: ph.enter_context(
                nc.sbuf_tensor(n, list(sh), dt))
            ps = lambda n, sh, dt=F32: ph.enter_context(
                nc.psum_tensor(n, list(sh), dt))
            qTh = [sb("qTh%d" % i, [128, S], BF16) for i in range(2)]
            kTh = [sb("kTh%d" % i, [128, S], BF16) for i in range(2)]
            Ebuf = sb("Ebuf", [128, S], BF16)
            lacc = sb("lacc", [128, max(CH, 1)])
            ltot = sb("ltot", [128, 1])
            invl = sb("invl", [128, 1])
            ohw = [sb("ohw%d" % i, [128, CH], BF16) for i in range(8)]
            smask = sb("smask", [128, 512])
            tmp_sb = sb("tmp_sb", [max(CH, 1), 512])
            ps_s = [ps("ps_s%d" % i, [128, 512]) for i in range(4)]
            ps_tmp = ps("ps_tmp", [max(CH, 1), 512])

            nEx = 0
            nCS = 0
            for u in range(hpc):
                ub = u % 2
                p.wait(sy, "t_p12", max(0, u - 1))
                p.inc(sy.dma_start(out=qTh[ub][:],
                                   in_=q_hbm[u * 128:(u + 1) * 128, :]),
                      "d_qk1", 16)
                p.inc(sy.dma_start(out=kTh[ub][:],
                                   in_=k_hbm[u * 128:(u + 1) * 128, :]),
                      "d_qk1", 16)
                p.wait(te, "d_qk1")
                for qt in range(QT):
                    nch = qt // 4 + 1
                    qsl = slice(qt * 128, (qt + 1) * 128)
                    base = p.n("t_s1")
                    qend = base + nch
                    for c in range(nch):
                        bi = nEx % 4
                        p.wait(te, "p1_free", max(0, nEx - 3))
                        p.inc(te.matmul(ps_s[bi][:], qTh[ub][:, qsl],
                                        kTh[ub][:, c * 512:(c + 1) * 512],
                                        start=True, stop=True), "t_s1")
                        nEx += 1
                    p.inc(te.drain(), "t_s1", 3)
                    for c in range(nch):
                        bi = (nEx - nch + c) % 4
                        vflush = min(base + c + 3, qend + 1)
                        diag = (c == qt // 4)
                        if diag:
                            p.wait(ve, "t_s1", vflush)
                            ve.tensor_add(
                                smask[:], ps_s[bi][:],
                                cs["p1tri_%d" % (qt % 4)][:])
                            p.inc(ve.drain(), "v_m1")
                            p.wait(sc, "v_m1")
                            src = smask[:]
                        else:
                            p.wait(sc, "t_s1", vflush)
                            src = ps_s[bi][:]
                        p.inc(sc.activation(
                            out=Ebuf[:, c * 512:(c + 1) * 512], in_=src,
                            func=AF.Exp, bias=cs["negC_col"][:, 0:1],
                            accum_out=lacc[:, c:c + 1]), "p1_free")
                    p.inc(sc.drain(), "l_rdy")
                    p.wait(ve, "l_rdy")
                    ve.reduce_sum(ltot[:], lacc[:, 0:nch], axis=AX.X)
                    ve.drain()
                    ve.reciprocal(out=invl[:], in_=ltot[:])
                    ve.drain()
                    p.wait(ve, "t_cs", max(0, nCS))
                    for c in range(nch):
                        ve.tensor_scalar(out=ohw[c % 8][:],
                                         in0=cs["oh_%d" % c][:],
                                         scalar1=invl[:], scalar2=None,
                                         op0=ALU.mult)
                    p.inc(ve.drain(), "v_oh", nch)
                    p.wait(te, "v_oh")
                    p.wait(te, "l_rdy")
                    for c in range(nch):
                        mcs = te.matmul(ps_tmp[:], ohw[c % 8][:],
                                        Ebuf[:, c * 512:(c + 1) * 512],
                                        start=(qt == 0 and c == 0),
                                        stop=(qt == QT - 1 and c == nch - 1))
                        if qt == QT - 1 and c == nch - 1:
                            p.inc(te.drain(), "t_cs")
                        else:
                            p.inc(mcs, "t_cs")
                        nCS += 1
                p.wait(ve, "t_cs")
                ve.tensor_copy(out=tmp_sb[:], in_=ps_tmp[:])
                p.inc(ve.drain(), "v_ts")
                p.wait(gp, "v_ts")
                g = u // cfg.UPG
                ul = u % cfg.UPG
                for cch in range(max(CH, 1)):
                    p.inc(gp.dma_start(
                        out=tsum[g][ul * KC + cch * 4:
                                    ul * KC + (cch + 1) * 4, :],
                        in_=tmp_sb[cch:cch + 1, :]), "g_ts", 16)
                p.wait(te, "g_ts")
                p.inc(te.nop(), "t_p12")
            barrier()

        if cfg.stop_after < 3:
            _finish(p, gp, out, cs, dbg, debug_taps, locals())
            return nc, dbg

        with ExitStack() as ph:
            sb = lambda n, sh, dt=F32: ph.enter_context(
                nc.sbuf_tensor(n, list(sh), dt))
            upg = cfg.UPG
            PBG = upg * KC
            b_ge = sb("b_ge", [128, 128])
            b_cnt = sb("b_cnt", [128, 1])
            b_c4 = sb("b_c4", [upg, 1])
            b_L = sb("b_L", [upg, 1])
            b_H = sb("b_H", [upg, 1])
            b_m = sb("b_m", [upg, 1])
            b_sel = sb("b_sel", [upg, 1])
            b_nsel = sb("b_nsel", [upg, 1])
            b_t1 = sb("b_t1", [upg, 1])
            b_T = sb("b_T", [128, 1])
            h01 = sb("h01", [128, 128])
            ps_b = ph.enter_context(nc.psum_tensor("ps_b", [128, 512], F32))

            for g in range(cfg.NBG):
                p.wait(te, "v_A", g)
                ve.memset(b_L[:], 0.0)
                p.inc(ve.memset(b_H[:], float(S + 1)), "v_bi")

                def thresh_bcast(src_col):
                    p.wait(te, "v_bi")
                    te.matmul(ps_b[0:PBG, 0:1], cs["bis_bc"][:],
                              src_col, start=True, stop=True)
                    p.inc(te.drain(), "t_bT")
                    p.wait(ve, "t_bT")
                    p.inc(ve.tensor_copy(out=b_T[0:PBG, :],
                                         in_=ps_b[0:PBG, 0:1]), "v_bT")
                    ve.drain()

                for it in range(cfg.bisect_iters):
                    ve.tensor_add(b_m[:], b_L[:], b_H[:])
                    ve.drain()
                    ve.tensor_scalar(out=b_m[:], in0=b_m[:],
                                     scalar1=0.5, scalar2=None,
                                     op0=ALU.mult)
                    p.inc(ve.drain(), "v_bi")
                    thresh_bcast(b_m[:])
                    ve.tensor_scalar(out=b_ge[0:PBG, :],
                                     in0=tsum[g][0:PBG, :],
                                     scalar1=b_T[0:PBG, :], scalar2=None,
                                     op0=ALU.is_ge)
                    ve.drain()
                    ve.reduce_sum(b_cnt[0:PBG, :], b_ge[0:PBG, :],
                                  axis=AX.X)
                    p.inc(ve.drain(), "v_bc")
                    p.wait(te, "v_bc")
                    te.matmul(ps_b[0:upg, 128:129],
                              cs["bis_sel"][0:PBG, :],
                              b_cnt[0:PBG, :], start=True, stop=True)
                    p.inc(te.drain(), "t_bc")
                    p.wait(ve, "t_bc")
                    ve.tensor_copy(out=b_c4[:], in_=ps_b[0:upg, 128:129])
                    ve.drain()
                    ve.tensor_scalar(out=b_sel[:], in0=b_c4[:],
                                     scalar1=float(cfg.heavy), scalar2=None,
                                     op0=ALU.is_ge)
                    ve.drain()
                    ve.tensor_scalar(out=b_nsel[:], in0=b_sel[:],
                                     scalar1=-1.0, scalar2=1.0,
                                     op0=ALU.mult, op1=ALU.add)
                    ve.tensor_sub(b_t1[:], b_m[:], b_L[:])
                    ve.drain()
                    ve.tensor_mul(b_t1[:], b_t1[:], b_sel[:])
                    ve.drain()
                    ve.tensor_add(b_L[:], b_L[:], b_t1[:])
                    ve.tensor_sub(b_t1[:], b_m[:], b_H[:])
                    ve.drain()
                    ve.tensor_mul(b_t1[:], b_t1[:], b_nsel[:])
                    ve.drain()
                    ve.tensor_add(b_H[:], b_H[:], b_t1[:])
                    p.inc(ve.drain(), "v_bi")
                thresh_bcast(b_L[:])
                ve.tensor_scalar(out=h01[0:PBG, :],
                                 in0=tsum[g][0:PBG, :],
                                 scalar1=b_T[0:PBG, :], scalar2=None,
                                 op0=ALU.is_ge)
                p.inc(ve.drain(), "v_h01")
                p.wait(te, "v_h01")
                te.transpose(ps_b[:, 256:256 + PBG], h01[0:PBG, :],
                             cs["ident_f32"][0:PBG, 0:PBG])
                p.inc(te.drain(), "t_h01")
                p.wait(ve, "t_h01")
                asl = slice(g * PBG, (g + 1) * PBG)
                ve.tensor_scalar(out=Aadd[:, asl],
                                 in0=ps_b[:, 256:256 + PBG],
                                 scalar1=1.0e38, scalar2=NBIG,
                                 op0=ALU.mult, op1=ALU.add)
                ve.tensor_scalar(out=Amin[:, asl],
                                 in0=ps_b[:, 256:256 + PBG],
                                 scalar1=2.0e38, scalar2=NBIG,
                                 op0=ALU.mult, op1=ALU.add)
                ve.drain()
                ve.tensor_scalar(out=Abias[:, asl], in0=Aadd[:, asl],
                                 scalar1=-CSHIFT, scalar2=None,
                                 op0=ALU.add)
                p.inc(ve.drain(), "v_A")
            barrier()

        if cfg.stop_after < 4:
            _finish(p, gp, out, cs, dbg, debug_taps, locals())
            return nc, dbg

        with ExitStack() as ph:
            sb = lambda n, sh, dt=F32: ph.enter_context(
                nc.sbuf_tensor(n, list(sh), dt))
            ps = lambda n, sh, dt=F32: ph.enter_context(
                nc.psum_tensor(n, list(sh), dt))
            qTh = [sb("qTh3%d" % i, [128, S], BF16) for i in range(2)]
            kTh = [sb("kTh3%d" % i, [128, S], BF16) for i in range(2)]
            vh = [sb("vh%d" % i, [128, TT * 128], BF16) for i in range(2)]
            E2 = [sb("E2_%d" % i, [128, 512], BF16) for i in range(4)]
            tcaus = sb("tcaus", [128, 512])
            tband = sb("tband", [128, 512])
            smask3 = sb("smask3", [128, 512])
            l2sb = sb("l2sb", [1, 512])
            invl2 = sb("invl2", [1, 512])
            il2bc = sb("il2bc", [128, 512])
            atq = sb("atq", [128, 512], BF16)
            ps_s3 = [ps("ps_s3%d" % i, [128, 512]) for i in range(4)]
            ps_pv = ps("ps_pv", [128, 512])
            ps_l2 = ps("ps_l2", [1, 512])
            ps_bc = ps("ps_bc", [128, 512])

            nK = 0
            nQG = 0
            for u in range(hpc):
                ub = u % 2
                p.wait(sy, "t_p3", max(0, (u - 1) * QG))
                p.inc(sy.dma_start(out=qTh[ub][:],
                                   in_=q_hbm[u * 128:(u + 1) * 128, :]),
                      "d_qk3", 16)
                p.inc(sy.dma_start(out=kTh[ub][:],
                                   in_=k_hbm[u * 128:(u + 1) * 128, :]),
                      "d_qk3", 16)
                p.wait(gp, "t_p3", max(0, (u - 1) * QG))
                p.inc(gp.dma_start(
                    out=vh[ub][:],
                    in_=v_hbm[:, u * TT * 128:(u + 1) * TT * 128]),
                    "g_v3", 16)
                p.wait(te, "d_qk3")
                p.wait(te, "g_v3")
                for qg in range(QG):
                    qsl = slice(qg * 512, (qg + 1) * 512)
                    kmax = (qg * 512 + 511) // 128
                    for g0 in range(0, kmax + 1, 4):
                        grp = list(range(g0, min(g0 + 4, kmax + 1)))
                        sbase = p.n("t_s3")
                        gend = sbase + len(grp)
                        for j, kc in enumerate(grp):
                            bi = (nK + j) % 4
                            p.wait(te, "p3_free", max(0, nK + j - 3))
                            p.inc(te.matmul(
                                ps_s3[bi][:],
                                kTh[ub][:, kc * 128:(kc + 1) * 128],
                                qTh[ub][:, qsl],
                                start=True, stop=True), "t_s3")
                        p.inc(te.drain(), "t_s3", 3)
                        for j, kc in enumerate(grp):
                            bi = (nK + j) % 4
                            vfl = min(sbase + j + 3, gend + 1)
                            rel = 128 * kc - 512 * qg
                            acol = Amin[:, u * KC + kc:u * KC + kc + 1]
                            abias = Abias[:, u * KC + kc:u * KC + kc + 1]
                            p.wait(sc, "t_pv", max(0, nK + j - 3))
                            if rel < -(cfg.recent + 127):
                                p.wait(sc, "t_s3", vfl)
                                sc.activation(out=E2[bi][:],
                                              in_=ps_s3[bi][:],
                                              func=AF.Exp, bias=abias)
                                sc.sem_inc(p.s("p3_free"), 1)
                                p.cnt["p3_free"] += 1
                            else:
                                p.wait(ve, "t_s3", vfl)
                                ve.tensor_add(tcaus[:], ps_s3[bi][:],
                                              cs["p3ca_%d" % rel][:])
                                p.inc(ve.tensor_add(tband[:], ps_s3[bi][:],
                                                    cs["p3ba_%d" % rel][:]),
                                      "p3_free")
                                ve.scalar_tensor_tensor(
                                    out=smask3[:], in0=tcaus[:], scalar=acol,
                                    in1=tband[:], op0=ALU.min, op1=ALU.max)
                                p.inc(ve.drain(), "v_m3")
                                p.wait(sc, "v_m3")
                                sc.activation(out=E2[bi][:], in_=smask3[:],
                                              func=AF.Exp,
                                              bias=cs["negC_col"][:, 0:1])
                        p.inc(sc.drain(), "s_e2", len(grp))
                        p.wait(te, "s_e2")
                        for j, kc in enumerate(grp):
                            bi = (nK + j) % 4
                            te.matmul(ps_pv[:],
                                      vh[ub][:, kc * 128:(kc + 1) * 128],
                                      E2[bi][:], start=(kc == 0),
                                      stop=(kc == kmax))
                            ml2 = te.matmul(ps_l2[:], cs["ones_col_bf"][:],
                                            E2[bi][:], start=(kc == 0),
                                            stop=(kc == kmax))
                            if kc == kmax:
                                p.inc(te.drain(), "t_pv")
                            else:
                                p.inc(ml2, "t_pv")
                        nK += len(grp)
                    p.wait(ve, "t_pv")
                    ve.tensor_copy(out=l2sb[:], in_=ps_l2[:])
                    ve.drain()
                    ve.reciprocal(out=invl2[:], in_=l2sb[:])
                    p.inc(ve.drain(), "v_il2")
                    p.wait(te, "v_il2")
                    te.matmul(ps_bc[:], cs["ones_row_f32"][:], invl2[:],
                              start=True, stop=True)
                    p.inc(te.drain(), "t_bc3")
                    p.wait(ve, "t_bc3")
                    ve.tensor_copy(out=il2bc[:], in_=ps_bc[:])
                    ve.drain()
                    p.wait(ve, "g_at", max(0, p.n("g_at") - 16))
                    ve.tensor_mul(atq[:], ps_pv[:], il2bc[:])
                    p.inc(ve.drain(), "v_at")
                    p.wait(gp, "v_at")
                    p.inc(gp.dma_start(
                        out=at_hbm[u * 128:(u + 1) * 128, qsl],
                        in_=atq[:]), "g_at", 16)
                    nQG += 1
                    p.wait(te, "v_at", nQG)
                    p.inc(te.nop(), "t_p3")
            p.wait(gp, "g_at")
            barrier()

        if cfg.stop_after < 5:
            _finish(p, gp, out, cs, dbg, debug_taps, locals())
            return nc, dbg

        ffnO = outer.enter_context(
            nc.sbuf_tensor("ffnO", [128, OTT * D], BF16, side="right"))
        attnF = outer.enter_context(
            nc.sbuf_tensor("attnF", [128, FIN * ownT], BF16, side="right"))
        with ExitStack() as ph:
            exb = ph.enter_context(
                nc.sbuf_tensor("exb", [128, 2 * ownT], BF16))
            s_ld = p.s("x_ld")
            s_st = p.s("x_st")
            s_af = p.s("x_af")
            with gp.register("oc") as oc, gp.register("sv") as sv, \
                    gp.register("pf") as pf:
                arm_parity(oc, cfg.cpb)
                for jj in range(cfg.cpb):
                    with gp.If_eq(oc, jj):
                        w = jj % 2
                        pbc = (jj ^ 1) * ownT
                        obc = jj * ownT
                        for u in range(hpc):
                            sl = (u % 2) * ownT
                            if u >= 2:
                                gp.wait_ge(s_st, (u - 1) * 16)
                            gp.dma_start(
                                out=exb[:, sl:sl + ownT],
                                in_=at_hbm[u * 128:(u + 1) * 128,
                                           pbc:pbc + ownT]).then_inc(s_ld, 16)
                            gp.wait_ge(s_ld, (u + 1) * 16)
                            gp.dma_start(
                                out=sh_attn[w, u * 128:(u + 1) * 128, :],
                                in_=exb[:, sl:sl + ownT]).then_inc(s_st, 16)
                        gp.wait_ge(s_st, hpc * 16)
                        gp.reg_mov(sv, MAGIC_ATTN)
                        gp.store(sh_flag[w, 0:1, 1:2], sv)
                        gp.reg_mov(pf, 0)
                        with gp.While(RuntimeValue(pf) != MAGIC_ATTN):
                            gp.reg_load(pf, sh_flag[1 - w, 0:1, 1:2])
                        for f in range(FIN):
                            uu = f % hpc
                            if (f // hpc) == w:
                                src = at_hbm[uu * 128:(uu + 1) * 128,
                                             obc:obc + ownT]
                            else:
                                src = sh_attn[1 - w, uu * 128:(uu + 1) * 128, :]
                            gp.dma_start(
                                out=attnF[:, f * ownT:(f + 1) * ownT],
                                in_=src).then_inc(s_af, 16)
                        gp.wait_ge(s_af, FIN * 16)
            barrier()

        if cfg.stop_after < 6:
            _finish(p, gp, out, cs, dbg, debug_taps, locals())
            return nc, dbg

        with ExitStack() as ph:
            sb = lambda n, sh, dt=F32: ph.enter_context(
                nc.sbuf_tensor(n, list(sh), dt))
            ps = lambda n, sh, dt=F32: ph.enter_context(
                nc.psum_tensor(n, list(sh), dt))
            w1b = [sb("w1b%d" % i, [128, FIN * 2 * 128], FP8) for i in range(2)]
            xn8t = sb("xn8t", [128, FIN * TGF], FP8)
            ffg = sb("ffg", [128, TGF], BF16)
            ff1 = sb("ff1", [128, TGF], BF16)
            gvt = sb("gvt", [128, TGF], BF16)
            gt1 = sb("gt1", [128, TGF], BF16)
            gt2 = sb("gt2", [128, TGF], BF16)
            prod = sb("prod", [128, TGF], BF16)
            act8 = sb("act8", [128, HC * TGF], FP8)
            wl8b = [sb("wl8b%d" % i, [128, D], FP8) for i in range(2)]
            ps_g = [ps("ps_g%d" % i, [128, TGF]) for i in range(2)]
            ps_o = [ps("ps_o%d" % i, [128, 512]) for i in range(NFG)]

            wg8m = wg8.rearrange("n m -> (n m)").rearrange(
                "(g b c) -> g b c", g=2, b=D)
            wl8m = wl8.rearrange("n m -> (n m)").rearrange(
                "(b c) -> b c", b=FF)
            nH = 0
            nO = 0
            for tg in range(NTG):
                tgs = slice(tg * TGF, (tg + 1) * TGF)
                p.wait(sy, "t_g1")
                for f in range(FIN):
                    p.inc(sy.dma_start(out=xn8t[:, f * TGF:(f + 1) * TGF],
                                       in_=xn8[f * 128:(f + 1) * 128, tgs]),
                          "d_x8", 16)
                for hc in range(HC):
                    bi = nH % 2
                    p.wait(sy, "t_g1", max(0, nH - 1))
                    for g in range(2):
                        for f in range(FIN):
                            p.inc(sy.dma_start(
                                out=w1b[bi][:, (f * 2 + g) * 128:
                                            (f * 2 + g + 1) * 128],
                                in_=wg8m[g, f * 128:(f + 1) * 128,
                                         hc * 128:(hc + 1) * 128]),
                                "d_w1", 16)
                    p.wait(te, "d_w1")
                    p.wait(te, "d_x8")
                    p.wait(te, "v_ffc", max(0, 2 * nH))
                    lg_ = None
                    for g in range(2):
                        for f in range(FIN):
                            lg_ = te.matmul(
                                ps_g[g][:],
                                w1b[bi][:, (f * 2 + g) * 128:
                                        (f * 2 + g + 1) * 128],
                                xn8t[:, f * TGF:(f + 1) * TGF],
                                start=(f == 0), stop=(f == FIN - 1))
                    p.inc(te.drain(), "t_g1")
                    p.wait(ve, "t_g1")
                    ve.tensor_copy(out=ffg[:], in_=ps_g[0][:])
                    ve.tensor_copy(out=ff1[:], in_=ps_g[1][:])
                    p.inc(ve.drain(), "v_ffc", 2)
                    ve.tensor_mul(gt1[:], ffg[:], ffg[:])
                    ve.tensor_mul(gt2[:], gt1[:], ffg[:])
                    ve.tensor_scalar(out=gt1[:], in0=gt2[:],
                                     scalar1=0.044677734375, scalar2=None,
                                     op0=ALU.mult)
                    ve.tensor_add(gt2[:], ffg[:], gt1[:])
                    ve.tensor_scalar(out=gt1[:], in0=gt2[:],
                                     scalar1=0.796875, scalar2=None,
                                     op0=ALU.mult)
                    p.inc(ve.drain(), "v_gel")
                    p.wait(sc, "v_gel")
                    sc.activation(out=gvt[:], in_=gt1[:], func=AF.Tanh)
                    p.inc(sc.drain(), "s_gelu")
                    p.wait(ve, "s_gelu")
                    ve.tensor_scalar(out=gt2[:], in0=gvt[:], scalar1=1.0,
                                     scalar2=0.5, op0=ALU.add, op1=ALU.mult)
                    ve.tensor_mul(gt1[:], ffg[:], gt2[:])
                    ve.tensor_mul(prod[:], gt1[:], ff1[:])
                    ve.tensor_copy(
                        out=act8[:, hc * TGF:(hc + 1) * TGF], in_=prod[:])
                    p.inc(ve.drain(), "v_ff")
                    nH += 1
                for t4 in range(TGF // 128):
                    ti = tg * (TGF // 128) + t4
                    p.wait(te, "v_fo", max(0, nO * NFG))
                    for hc in range(HC):
                        bi = hc % 2
                        p.wait(sy, "t_o", max(0, p.n("t_o") - 1))
                        p.inc(sy.dma_start(
                            out=wl8b[bi][:],
                            in_=wl8m[hc * 128:(hc + 1) * 128, :]), "d_wl", 16)
                        p.wait(te, "d_wl")
                        p.wait(te, "v_ff")
                        lo = None
                        for fg in range(NFG):
                            lo = te.matmul(
                                ps_o[fg][:],
                                act8[:, hc * TGF + t4 * 128:
                                     hc * TGF + (t4 + 1) * 128],
                                wl8b[bi][:, fg * 512:(fg + 1) * 512],
                                start=(hc == 0), stop=(hc == HC - 1))
                        if hc == HC - 1:
                            p.inc(te.drain(), "t_o")
                        else:
                            p.inc(lo, "t_o")
                    p.wait(ve, "t_o")
                    for fg in range(NFG):
                        p.inc(ve.tensor_copy(
                            out=ffnO[:, ti * D + fg * 512:
                                     ti * D + (fg + 1) * 512],
                            in_=ps_o[fg][:]), "v_fo")
                    nO += 1
            barrier()

        if cfg.stop_after < 7:
            _finish(p, gp, out, cs, dbg, debug_taps, locals())
            return nc, dbg

        with ExitStack() as ph:
            sb = lambda n, sh, dt=F32: ph.enter_context(
                nc.sbuf_tensor(n, list(sh), dt))
            ps = lambda n, sh, dt=F32: ph.enter_context(
                nc.psum_tensor(n, list(sh), dt))
            wo_bf = sb("wo_bf", [128, FIN * D], BF16)
            wof = [sb("wof%d" % i, [128, D]) for i in range(2)]
            xr = [sb("xr%d" % i, [128, D]) for i in range(2)]
            outb = [sb("outb%d" % i, [128, D]) for i in range(2)]
            ps_w = [ps("ps_w%d" % i, [128, 512]) for i in range(NFG)]

            for f in range(FIN):
                bi = f % 2
                p.wait(sy, "v_wo", max(0, f - 1))
                p.inc(sy.dma_start(out=wof[bi][:],
                                   in_=wo_t[f * 128:(f + 1) * 128, :]),
                      "d_wo", 16)
                p.wait(ve, "d_wo")
                ve.tensor_copy(out=wo_bf[:, f * D:(f + 1) * D],
                               in_=wof[bi][:])
                p.inc(ve.drain(), "v_wo")
            for tt in range(OTT):
                bi = tt % 2
                p.wait(sy, "v_out", max(0, (tt - 1) * NFG))
                p.inc(sy.dma_start(out=xr[bi][:],
                                   in_=xown[tt * 128:(tt + 1) * 128, :]),
                      "d_xr", 16)
                p.wait(te, "v_wo")
                p.wait(te, "v_out", tt * NFG)
                lw = None
                for f in range(FIN):
                    for fg in range(NFG):
                        lw = te.matmul(
                            ps_w[fg][:],
                            attnF[:, f * ownT + tt * 128:
                                  f * ownT + (tt + 1) * 128],
                            wo_bf[:, f * D + fg * 512:f * D + (fg + 1) * 512],
                            start=(f == 0), stop=(f == FIN - 1))
                p.inc(te.drain(), "t_wo")
                p.wait(ve, "t_wo")
                p.wait(ve, "d_xr")
                p.wait(ve, "g_out", max(0, (tt - 1) * 16))
                for fg in range(NFG):
                    fsl = slice(fg * 512, (fg + 1) * 512)
                    ve.tensor_add(outb[bi][:, fsl], ps_w[fg][:],
                                  xr[bi][:, fsl])
                    ve.tensor_add(
                        outb[bi][:, fsl], outb[bi][:, fsl],
                        ffnO[:, tt * D + fg * 512:tt * D + (fg + 1) * 512])
                    p.inc(ve.drain(), "v_out")
                p.wait(gp, "v_out")
                p.inc(gp.dma_start(out=out[tt * 128:(tt + 1) * 128, :],
                                   in_=outb[bi][:]), "g_out", 16)
            _finish(p, gp, out, cs, dbg, debug_taps, locals())
    return nc, dbg


def _finish(p, gp, out, cs, dbg, debug_taps, env):
    nc = p.nc
    for name in debug_taps:
        if name in dbg:
            continue
        tile = None
        m = re.match(r"^(tsum)(\d+)$", name)
        if m:
            tile = env[m.group(1)][int(m.group(2))]
        elif name in env:
            tile = env[name]
        if tile is None:
            continue
        shp = list(tile.shape)
        d = nc.dram_tensor("dbg_" + name, shp, tile.dtype,
                           kind="ExternalOutput").ap()
        dbg[name] = d
        p.inc(gp.dma_start(out=d, in_=tile[:]), "g_out", 16)
    p.wait(gp, "g_out")



def prep_inputs(cfg, x, Wq, Wk, Wv, Wo, w_gating, w_linear):
    B, S, D = cfg.B, cfg.S, cfg.D
    hpc, dh, cpb = cfg.hpc, cfg.dh, cfg.cpb
    tabs = _rope_tables(cfg)
    xf = np.ascontiguousarray(np.asarray(x, dtype=np.float32).reshape(B * S, D))
    wo_tv = np.ascontiguousarray(np.asarray(Wo, dtype=np.float32).T)
    wgv = np.ascontiguousarray(np.asarray(w_gating, dtype=np.float32))
    wlv = np.ascontiguousarray(np.asarray(w_linear, dtype=np.float32))
    Wq = np.asarray(Wq, dtype=np.float32)
    Wk = np.asarray(Wk, dtype=np.float32)
    Wv = np.asarray(Wv, dtype=np.float32)
    in_maps = []
    for c in range(cfg.n_cores):
        b = c // cpb
        hs = hpc * (c % 2)
        m = {}
        m["xb"] = np.ascontiguousarray(xf[b * S:(b + 1) * S])
        m["xown"] = np.ascontiguousarray(xf[c * cfg.ownT:(c + 1) * cfg.ownT])
        m["wq_t"] = np.ascontiguousarray(Wq[hs * dh:(hs + hpc) * dh, :].T)
        m["wk_t"] = np.ascontiguousarray(Wk[hs * dh:(hs + hpc) * dh, :].T)
        m["wv_t"] = np.ascontiguousarray(Wv[hs * dh:(hs + hpc) * dh, :].T)
        m["wo_t"] = wo_tv
        m["wg"] = wgv
        m["wl"] = wlv
        for k, v in tabs.items():
            m[k] = v
        in_maps.append(m)
    return in_maps


def kernel(L0_attn_hidden_states, L0_attn_Wq, L0_attn_Wk, L0_attn_Wv,
           L0_attn_Wo, L0_ffn_w_gating, L0_ffn_w_linear):
    cfg = Cfg()
    in_maps = prep_inputs(cfg, L0_attn_hidden_states, L0_attn_Wq,
                          L0_attn_Wk, L0_attn_Wv, L0_attn_Wo,
                          L0_ffn_w_gating, L0_ffn_w_linear)
    nc, _ = build_program(cfg)
    res = run_bass_kernel_spmd(nc, in_maps, core_ids=list(range(cfg.n_cores)))
    shards = [r["out"] for r in res.results]
    outv = np.concatenate(shards, axis=0).reshape(cfg.B, cfg.S, cfg.D)
    return outv


# version 08 (final)
# speedup vs baseline: 1.2373x; 1.1379x over previous
import math
import re
from contextlib import ExitStack

import numpy as np
import ml_dtypes

import concourse.bass as bass
import concourse.mybir as mybir
from concourse.bass_utils import run_bass_kernel_spmd
from concourse.expressions import RuntimeValue

F32 = mybir.dt.float32
BF16 = mybir.dt.bfloat16
FP8 = mybir.dt.float8e4
I32 = mybir.dt.int32
AF = mybir.ActivationFunctionType
ALU = mybir.AluOpType
AX = mybir.AxisListType

NBIG = -1.0e38
CSHIFT = 14.0
MAGIC_SENT = 0x0BAD0001
MAGIC_ATTN = 0x51CA0001


class Cfg:
    def __init__(self, B=2, S=4096, D=2048, H=16, FF=8192, n_cores=8,
                 bisect_iters=42, theta=10000.0, stop_after=99):
        self.B, self.S, self.D, self.H, self.FF = B, S, D, H, FF
        self.n_cores = n_cores
        self.dh = 128
        self.theta = theta
        self.bisect_iters = bisect_iters
        self.stop_after = stop_after
        assert D == H * self.dh
        self.cpb = n_cores // B
        assert self.cpb % 2 == 0
        self.hpc = H // 2
        self.ownT = B * S // n_cores
        self.heavy = int(0.1 * S)
        self.recent = int(0.1 * S)
        self.FIN = D // 128
        self.KC = S // 128
        self.CH = S // 512
        self.QT = S // 128
        self.QG = S // 512
        self.HC = FF // 128
        self.TT = S // 128
        self.TCOL = S // 512
        self.OTT = self.ownT // 128
        self.TGF = min(512, self.ownT)
        self.NTG = self.ownT // self.TGF
        self.HD2 = self.hpc * self.dh
        assert self.HD2 % 512 == 0
        self.HG = self.HD2 // 512
        self.UPG = min(self.hpc, 128 // self.KC)
        self.NBG = (self.hpc + self.UPG - 1) // self.UPG
        assert self.hpc % self.UPG == 0
        assert self.CH >= 1 and S % 512 == 0
        assert self.ownT % self.TGF == 0 and D % 512 == 0


def _rope_tables(cfg):
    dh = cfg.dh
    ex = np.arange(0, dh, 2, dtype=np.float64) / dh
    inv = 1.0 / (cfg.theta ** ex)
    pos = np.arange(cfg.S, dtype=np.float64)
    fr = inv[:, None] * pos[None, :]
    cos = np.cos(fr)
    sin = np.sin(fr)
    cos_full = np.concatenate([cos, cos], 0)
    sinpm = np.concatenate([-sin, sin], 0)
    qs = 1.0 / math.sqrt(dh)
    out = {}
    for name, tab in (("cq", cos_full * qs), ("sq", sinpm * qs),
                      ("ck", cos_full), ("sk", sinpm)):
        hi = tab.astype(np.float32)
        lo = (tab - hi.astype(np.float64)).astype(np.float32)
        out[name + "_hi"] = np.ascontiguousarray(hi)
        out[name + "_lo"] = np.ascontiguousarray(lo)
    return out


def _const_tables(cfg):
    c = {}
    c["ident_bf"] = np.eye(128, dtype=ml_dtypes.bfloat16)
    c["ident_f32"] = np.eye(128, dtype=np.float32)
    c["ones_col_bf"] = np.ones((128, 1), dtype=ml_dtypes.bfloat16)
    c["ones_row_f32"] = np.ones((1, 128), dtype=np.float32)
    p = np.arange(128, dtype=np.float32).reshape(128, 1)
    j = np.arange(512, dtype=np.float32)[None, :]
    big = np.float32(NBIG)
    for r in range(4):
        c["p1tri_%d" % r] = np.ascontiguousarray(np.where(
            j < 128 * r + p + 1, np.float32(0), big).astype(ml_dtypes.bfloat16))
    c["eps_col"] = np.full((128, 1), 1e-5, dtype=np.float32)
    c["negC_col"] = np.full((128, 1), -CSHIFT, dtype=np.float32)
    for rel in range(-512, 385, 128):
        cs_ = np.clip(rel + p, 0, 512)
        be_ = np.clip(rel + p + cfg.recent + 1, 0, 512)
        c["p3ca_%d" % rel] = np.ascontiguousarray(np.where(
            j >= cs_, np.float32(0), big).astype(ml_dtypes.bfloat16))
        c["p3ba_%d" % rel] = np.ascontiguousarray(np.where(
            (j >= cs_) & (j < be_), np.float32(0), big).astype(ml_dtypes.bfloat16))
    for ci in range(cfg.CH):
        m = np.zeros((128, cfg.CH), dtype=np.float32)
        m[:, ci] = 1.0
        c["oh_%d" % ci] = m
    upg = cfg.UPG
    pbg = upg * cfg.KC
    sel = np.zeros((pbg, upg), dtype=np.float32)
    for u in range(upg):
        sel[u * cfg.KC:(u + 1) * cfg.KC, u] = 1
    c["bis_sel"] = sel
    bc = np.zeros((upg, pbg), dtype=np.float32)
    for u in range(upg):
        bc[u, u * cfg.KC:(u + 1) * cfg.KC] = 1
    c["bis_bc"] = bc
    return c


class P:
    def __init__(self, nc, ctx):
        self.nc = nc
        self.ctx = ctx
        self.sems = {}
        self.cnt = {}

    def s(self, name):
        if name not in self.sems:
            self.sems[name] = self.ctx.enter_context(self.nc.semaphore(name))
            self.cnt[name] = 0
        return self.sems[name]

    def inc(self, instr, name, k=1):
        instr.then_inc(self.s(name), k)
        self.cnt[name] += k
        return self.cnt[name]

    def wait(self, eng, name, value=None):
        v = self.cnt.get(name, 0) if value is None else value
        self.s(name)
        if v > 0:
            eng.wait_ge(self.sems[name], v)

    def n(self, name):
        self.s(name)
        return self.cnt[name]


def build_program(cfg, debug_taps=(), hbm_taps=()):
    nc = bass.Bass("TRN2", target_bir_lowering=False, debug=False,
                   enable_partition_id=False, detect_race_conditions=False)

    def internal(name):
        return "ExternalOutput" if name in hbm_taps else "Internal"
    B, S, D, FF, dh = cfg.B, cfg.S, cfg.D, cfg.FF, cfg.dh
    hpc, FIN, KC, CH, QT, QG, HC = (cfg.hpc, cfg.FIN, cfg.KC, cfg.CH,
                                    cfg.QT, cfg.QG, cfg.HC)
    ownT, TT, TCOL, OTT, TGF, NTG = (cfg.ownT, cfg.TT, cfg.TCOL, cfg.OTT,
                                     cfg.TGF, cfg.NTG)
    HD2, HG = cfg.HD2, cfg.HG
    NFG = D // 512

    xb = nc.dram_tensor("xb", [S, D], F32, kind="ExternalInput").ap()
    xown = nc.dram_tensor("xown", [ownT, D], F32, kind="ExternalInput").ap()
    wq_t = nc.dram_tensor("wq_t", [D, HD2], F32, kind="ExternalInput").ap()
    wk_t = nc.dram_tensor("wk_t", [D, HD2], F32, kind="ExternalInput").ap()
    wv_t = nc.dram_tensor("wv_t", [D, HD2], F32, kind="ExternalInput").ap()
    wo_t = nc.dram_tensor("wo_t", [D, D], F32, kind="ExternalInput").ap()
    wg = nc.dram_tensor("wg", [2, D, FF], F32, kind="ExternalInput").ap()
    wl = nc.dram_tensor("wl", [FF, D], F32, kind="ExternalInput").ap()
    tabs = {}
    for nme in ("cq_hi", "cq_lo", "sq_hi", "sq_lo",
                "ck_hi", "ck_lo", "sk_hi", "sk_lo"):
        tabs[nme] = nc.dram_tensor(nme, [128, S], F32, kind="ExternalInput").ap()
    out = nc.dram_tensor("out", [ownT, D], F32, kind="ExternalOutput").ap()

    wg8 = nc.dram_tensor("wg8", [2 * D * FF // 4096, 4096], FP8,
                         kind="Internal").ap()
    wl8 = nc.dram_tensor("wl8", [FF * D // 4096, 4096], FP8,
                         kind="Internal").ap()
    wqkv16 = nc.dram_tensor("wqkv16", [3 * D * HD2 // 4096, 4096], BF16,
                            kind="Internal").ap()
    xn8 = nc.dram_tensor("xn8", [FIN * 128, ownT], FP8,
                         kind=internal("xn8")).ap()
    q_hbm = nc.dram_tensor("q_hbm", [hpc * 128, S], BF16,
                           kind=internal("q_hbm")).ap()
    k_hbm = nc.dram_tensor("k_hbm", [hpc * 128, S], BF16,
                           kind=internal("k_hbm")).ap()
    v_hbm = nc.dram_tensor("v_hbm", [128, hpc * TT * 128], BF16,
                           kind=internal("v_hbm")).ap()
    at_hbm = nc.dram_tensor("at_hbm", [hpc * 128, S], BF16,
                            kind=internal("at_hbm")).ap()
    sh_attn = nc.dram_tensor("sh_attn", [2, hpc * 128, ownT], BF16,
                             kind="Internal", addr_space="Shared").ap()
    sh_flag = nc.dram_tensor("sh_flag", [2, 1, 32], I32,
                             kind="Internal", addr_space="Shared").ap()

    dbg = {}
    consts = _const_tables(cfg)
    cth = {k: nc.inline_tensor(np.ascontiguousarray(v), "c_" + k)
           for k, v in consts.items()}

    outer = ExitStack()
    p = P(nc, outer)
    sy, te, ve, sc, gp = nc.sync, nc.tensor, nc.vector, nc.scalar, nc.gpsimd

    def barrier():
        nc.all_engine_barrier()

    def arm_parity(reg, mod):
        for arm in gp.switch_core_id(n=128):
            try:
                lg = arm.logical % 8
            except Exception:
                lg = 0
            gp.reg_mov(reg, lg % mod)
        gp.snap(reg, min_val=0, max_val=mod - 1)

    with outer:
        cs = {}
        for k, v in consts.items():
            t = outer.enter_context(
                nc.sbuf_tensor("cs_" + k, list(v.shape),
                               mybir.dt.from_np(v.dtype)))
            p.inc(sy.dma_start(out=t[:], in_=cth[k].ap()), "d_c", 16)
            cs[k] = t

        tsum = [outer.enter_context(
            nc.sbuf_tensor("tsum%d" % g, [128, 128], F32))
            for g in range(cfg.NBG)]
        Aadd = outer.enter_context(
            nc.sbuf_tensor("Aadd", [128, hpc * KC], F32))
        Amin = outer.enter_context(
            nc.sbuf_tensor("Amin", [128, hpc * KC], F32))
        Abias = outer.enter_context(
            nc.sbuf_tensor("Abias", [128, hpc * KC], F32))
        lntap = outer.enter_context(nc.sbuf_tensor("lntap", [128, 8], F32))
        for g in range(cfg.NBG):
            ve.memset(tsum[g][:], 0.0)

        if not getattr(cfg, "dbg_no_sentinel", False):
            with gp.register("oc0") as oc0:
                arm_parity(oc0, 2)
                with gp.register("sv0") as sv0:
                    gp.reg_mov(sv0, MAGIC_SENT)
                    for s in range(2):
                        with gp.If_eq(oc0, s):
                            gp.store(sh_flag[s, 0:1, 1:2], sv0)

        with ExitStack() as ph:
            qb = [ph.enter_context(nc.sbuf_tensor("qb%d" % i, [128, 4096], F32))
                  for i in range(2)]
            qb8 = [ph.enter_context(nc.sbuf_tensor("qb8%d" % i, [128, 4096],
                                                   FP8))
                   for i in range(2)]
            qb16 = [ph.enter_context(nc.sbuf_tensor("qb16%d" % i, [128, 4096],
                                                    BF16))
                    for i in range(2)]
            wgf = wg.rearrange("a b c -> (a b c)").rearrange(
                "(n p m) -> n p m", p=128, m=4096)
            wlf = wl.rearrange("a b -> (a b)").rearrange(
                "(n p m) -> n p m", p=128, m=4096)
            wg8d = wg8.rearrange("(n p) m -> n p m", p=128)
            wl8d = wl8.rearrange("(n p) m -> n p m", p=128)
            wqkvd = wqkv16.rearrange("(n p) m -> n p m", p=128)
            nqkv = D * HD2 // (128 * 4096)
            srcs16 = []
            for wi, w_src in enumerate((wq_t, wk_t, wv_t)):
                wsf = w_src.rearrange("a b -> (a b)").rearrange(
                    "(n p m) -> n p m", p=128, m=4096)
                for i in range(nqkv):
                    srcs16.append((wsf[i], wqkvd[wi * nqkv + i]))
            ntile_g = 2 * D * FF // (128 * 4096)
            ntile_l = FF * D // (128 * 4096)
            n16 = len(srcs16)
            nW = 0 if getattr(cfg, "dbg_no_phasew", False) else \
                n16 + ntile_g + ntile_l
            if getattr(cfg, "dbg_phasew_nops", False):
                nW = 0
                for e_ in (ve, sc, te, gp, sy):
                    for _ in range(1500):
                        e_.nop()
            wmode = getattr(cfg, "dbg_phasew_mode", "full")
            for i in range(nW):
                bi = i % 2
                if i < n16:
                    srcA, dstA = srcs16[i]
                    stg = qb16[bi]
                elif i < n16 + ntile_g:
                    srcA, dstA = wgf[i - n16], wg8d[i - n16]
                    stg = qb8[bi]
                else:
                    srcA = wlf[i - n16 - ntile_g]
                    dstA = wl8d[i - n16 - ntile_g]
                    stg = qb8[bi]
                if wmode in ("full", "load"):
                    if wmode == "load":
                        p.wait(sy, "d_q", max(0, (i - 1) * 16))
                    else:
                        p.wait(sy, "g_q8", max(0, (i - 1) * 16))
                    p.inc(sy.dma_start(out=qb[bi][:], in_=srcA), "d_q", 16)
                if wmode in ("full", "cast"):
                    eng = ve if i % 2 == 0 else sc
                    p.wait(eng, "d_q")
                    if eng is ve:
                        ve.tensor_copy(out=stg[:], in_=qb[bi][:])
                    else:
                        sc.copy(out=stg[:], in_=qb[bi][:])
                    p.inc(eng.drain(), "c_q8")
                if wmode in ("full", "store"):
                    sub = getattr(cfg, "dbg_store_subset", "")
                    skip = (sub == "16" and i >= n16) or \
                           (sub == "g" and not (n16 <= i < n16 + ntile_g)) or \
                           (sub == "l" and i < n16 + ntile_g)
                    if sub == "16x":
                        dstA = wqkvd[i % nqkv]
                        stg = qb16[bi]
                        skip = False
                    if not skip:
                        p.wait(gp, "c_q8")
                        p.inc(gp.dma_start(out=dstA, in_=stg[:]), "g_q8", 16)
            p.wait(gp, "g_q8")
            if getattr(cfg, "dbg_zero_qb", False):
                p.wait(ve, "g_q8")
                for t_ in qb + qb8 + qb16:
                    ve.memset(t_[:], 0.0)
                ve.drain()
            barrier()

        if "wdump" in debug_taps:
            dwd = nc.dram_tensor("dbg_wdump", [128, 1664], F32,
                                 kind="ExternalOutput").ap()
            dbg["wdump"] = dwd
            p.inc(gp.dma_start(out=dwd[:, 0:128],
                               in_=cth["ident_f32"].ap()), "g_wd", 16)
            p.inc(gp.dma_start(out=dwd[:, 128:640],
                               in_=tabs["cq_hi"][:, 0:512]), "g_wd", 16)
            p.inc(gp.dma_start(out=dwd[:, 640:1152],
                               in_=xb[0:128, 0:512]), "g_wd", 16)
            p.inc(gp.dma_start(out=dwd[:, 1152:1664],
                               in_=wo_t[0:128, 0:512]), "g_wd", 16)
            p.wait(gp, "g_wd")
            barrier()

        with ExitStack() as ph:
            sb = lambda n, sh, dt=F32: ph.enter_context(
                nc.sbuf_tensor(n, list(sh), dt))
            ps = lambda n, sh, dt=F32: ph.enter_context(
                nc.psum_tensor(n, list(sh), dt))
            wqkb = [sb("wqkb%d" % i, [128, FIN * HD2], BF16) for i in range(2)]
            x8s = sb("x8s", [128, 128], FP8)
            xf = [sb("xf%d" % i, [128, D]) for i in range(2)]
            xnf = sb("xnf", [128, D])
            xnb = sb("xnb", [128, D], BF16)
            xsq = sb("xsq", [128, D])
            smn = sb("smn", [128, 1])
            ssq = sb("ssq", [128, 1])
            mean = sb("mean", [128, 1])
            ex2 = sb("ex2", [128, 1])
            t1v = sb("t1v", [128, 1])
            var1 = sb("var1", [128, 1])
            sd1 = sb("sd1", [128, 1])
            rstd1 = sb("rstd1", [128, 1])
            nmr1 = sb("nmr1", [128, 1])
            xnTc = sb("xnTc", [128, FIN * 512], BF16)
            thi = sb("thi", [128, 512])
            tb4 = [sb("tb4_%d" % i, [128, 512]) for i in range(4)]
            qraw = sb("qraw", [128, 512], BF16)
            kraw = sb("kraw", [128, 512], BF16)
            qrot = sb("qrot", [128, 512], BF16)
            krot = sb("krot", [128, 512], BF16)
            rt1 = sb("rt1", [128, 512], BF16)
            qcol = sb("qcol", [128, 512], BF16)
            kcol = sb("kcol", [128, 512], BF16)
            vcol = sb("vcol", [128, HD2], BF16)
            ps_tb = [ps("ps_tb%d" % i, [128, 128], BF16) for i in range(2)]
            ps_tf = [ps("ps_tf%d" % i, [128, 128]) for i in range(2)]
            ps_q = ps("ps_q", [128, 512])
            ps_k = ps("ps_k", [128, 512])
            ps_v = [ps("ps_v%d" % i, [128, 512]) for i in range(HG)]

            wqkvv = wqkv16.rearrange("a b -> (a b)").rearrange(
                "(w d h) -> w d h", w=3, d=D)

            def load_w(widx, tcol):
                gi = 3 * tcol + widx
                bi = gi % 2
                p.wait(sy, "t_wq", max(0, gi - 1))
                for f in range(FIN):
                    p.inc(sy.dma_start(
                        out=wqkb[bi][:, f * HD2:(f + 1) * HD2],
                        in_=wqkvv[widx, f * 128:(f + 1) * 128, :]), "d_wq", 16)
                return bi

            def ln_tile(src_ap, dst_f32, dst_bf=None):
                bi = p.n("d_x") // 16 % 2
                p.wait(sy, "s_ln", max(0, p.n("s_ln") - 1))
                p.inc(sy.dma_start(out=xf[bi][:], in_=src_ap), "d_x", 16)
                if getattr(cfg, "dbg_no_ln", False):
                    p.wait(ve, "d_x")
                    p.wait(ve, "t_tp")
                    ve.tensor_copy(out=dst_f32[:], in_=xf[bi][:])
                    p.inc(ve.drain(), "s_ln")
                    if dst_bf is not None:
                        ve.tensor_copy(out=dst_bf[:], in_=dst_f32[:])
                        p.inc(ve.drain(), "v_xnb")
                    return
                p.wait(sc, "d_x")
                p.wait(sc, "v_sq", p.n("v_sq"))
                sc.activation(out=xsq[:], in_=xf[bi][:], func=AF.Square,
                              accum_out=ssq[:])
                p.inc(sc.drain(), "s_sq")
                p.wait(ve, "d_x")
                ve.reduce_sum(smn[:], xf[bi][:], axis=AX.X)
                ve.drain()
                p.wait(ve, "s_sq")
                p.inc(ve.tensor_scalar(out=ex2[:], in0=ssq[:],
                                       scalar1=1.0 / D, scalar2=None,
                                       op0=ALU.mult), "v_sq")
                ve.tensor_scalar(out=mean[:], in0=smn[:], scalar1=1.0 / D,
                                 scalar2=None, op0=ALU.mult)
                ve.drain()
                ve.tensor_mul(t1v[:], mean[:], mean[:])
                ve.drain()
                ve.tensor_sub(var1[:], ex2[:], t1v[:])
                p.inc(ve.drain(), "v_ln")
                p.wait(sc, "v_ln")
                sc.activation(out=sd1[:], in_=var1[:], func=AF.Sqrt,
                              bias=cs["eps_col"][:, 0:1])
                p.inc(sc.drain(), "s_sd")
                p.wait(ve, "s_sd")
                ve.reciprocal(out=rstd1[:], in_=sd1[:])
                ve.drain()
                ve.tensor_mul(nmr1[:], mean[:], rstd1[:])
                ve.drain()
                ve.tensor_scalar(out=nmr1[:], in0=nmr1[:], scalar1=-1.0,
                                 scalar2=None, op0=ALU.mult)
                p.inc(ve.drain(), "v_rs")
                p.wait(sc, "v_rs")
                p.wait(sc, "t_tp")
                sc.activation(out=dst_f32[:], in_=xf[bi][:],
                              func=AF.Identity, bias=nmr1[:],
                              scale=rstd1[:])
                p.inc(sc.drain(), "s_ln")
                if dst_bf is not None:
                    p.wait(ve, "s_ln")
                    ve.tensor_copy(out=dst_bf[:], in_=dst_f32[:])
                    p.inc(ve.drain(), "v_xnb")
                if "lntap" in debug_taps and p.n("s_ln") == 1:
                    p.wait(gp, "v_rs")
                    for j, tt_ in enumerate((smn, ssq, mean, ex2, var1, sd1,
                                             rstd1, nmr1)):
                        p.inc(gp.dma_start(out=lntap[:, j:j + 1],
                                           in_=tt_[:]), "g_lt", 16)

            def transpose128(src, dst, f32, ready):
                n = p.n("t_tp")
                pb = n % 2
                pt = ps_tf[pb] if f32 else ps_tb[pb]
                p.wait(te, ready)
                p.wait(te, "v_tp", max(0, n - 1))
                te.transpose(pt[:, 0:128], src,
                             cs["ident_f32" if f32 else "ident_bf"][:])
                p.inc(te.drain(), "t_tp")
                p.wait(ve, "t_tp")
                ve.tensor_copy(out=dst, in_=pt[:, 0:128])
                p.inc(ve.drain(), "v_tp")

            for tcol in range(TCOL):
                tsl = slice(tcol * 512, (tcol + 1) * 512)
                no_qkv = getattr(cfg, "dbg_no_qkv", False)
                no_rope = getattr(cfg, "dbg_no_rope", False)
                for i, nme in enumerate(("cq", "sq", "ck", "sk")):
                    if no_qkv or no_rope:
                        break
                    p.wait(sy, "v_rope", max(0, 2 * hpc * tcol))
                    p.wait(sy, "v_tab", max(0, 4 * tcol + i))
                    p.inc(sy.dma_start(out=thi[:],
                                       in_=tabs[nme + "_hi"][:, tsl]),
                          "d_tab", 16)
                    p.inc(sy.dma_start(out=tb4[i][:],
                                       in_=tabs[nme + "_lo"][:, tsl]),
                          "d_tab", 16)
                    p.wait(ve, "d_tab")
                    p.inc(ve.tensor_add(tb4[i][:], thi[:], tb4[i][:]), "v_tab")

                if not no_qkv:
                    p.wait(ve, "t_qk", max(0, 2 * hpc * tcol))
                    p.wait(ve, "t_v", max(0, 4 * HG * tcol))
                for t4 in range(4):
                    ti = tcol * 4 + t4
                    ln_tile(xb[ti * 128:(ti + 1) * 128, :], xnf, xnb)
                    for f in range(FIN):
                        transpose128(
                            xnb[:, f * 128:(f + 1) * 128],
                            xnTc[:, f * 512 + t4 * 128:f * 512 + (t4 + 1) * 128],
                            f32=False, ready="v_xnb")

                if tcol == 0 and "xnTc" in debug_taps:
                    dxnTc = nc.dram_tensor("dbg_xnTc", [128, FIN * 512], BF16,
                                           kind="ExternalOutput").ap()
                    dxnb = nc.dram_tensor("dbg_xnb", [128, D], BF16,
                                          kind="ExternalOutput").ap()
                    dbg["xnTc"] = dxnTc
                    dbg["xnb"] = dxnb
                    tapeng = gp if getattr(cfg, "dbg_tap_gp", True) else sy
                    p.wait(tapeng, "v_tp")
                    p.wait(tapeng, "v_xnb")
                    p.inc(tapeng.dma_start(out=dxnTc, in_=xnTc[:]),
                          "g_dbg", 16)
                    p.inc(tapeng.dma_start(out=dxnb, in_=xnb[:]),
                          "g_dbg", 16)
                    p.wait(te, "g_dbg")

                if no_qkv:
                    continue
                bq = load_w(0, tcol)
                bk = load_w(1, tcol)
                bv = load_w(2, tcol)
                dwq = p.n("d_wq")
                for widx, (bw, hbmt) in enumerate(((bq, q_hbm), (bk, k_hbm))):
                    p.wait(te, "d_wq", dwq - (2 - widx) * 16 * FIN)
                    p.wait(te, "v_tp")
                    raw = qraw if widx == 0 else kraw
                    rot = qrot if widx == 0 else krot
                    colt = qcol if widx == 0 else kcol
                    ti0, ti1 = (0, 1) if widx == 0 else (2, 3)
                    for h in range(hpc):
                        pdst = ps_q if h % 2 == 0 else ps_k
                        p.wait(te, "v_qk", max(0, p.n("t_qk") - 1))
                        lastm = None
                        for f in range(FIN):
                            lastm = te.matmul(
                                pdst[:],
                                wqkb[bw][:, f * HD2 + h * dh:
                                         f * HD2 + (h + 1) * dh],
                                xnTc[:, f * 512:(f + 1) * 512],
                                start=(f == 0), stop=(f == FIN - 1))
                        p.inc(te.drain(), "t_qk")
                        p.wait(ve, "t_qk")
                        p.wait(ve, "g_rot", p.n("g_rot"))
                        ve.tensor_copy(out=raw[:], in_=pdst[:])
                        p.inc(ve.drain(), "v_qk")
                        if no_rope:
                            p.wait(ve, "g_qk", p.n("g_qk"))
                            ve.tensor_copy(out=colt[:], in_=raw[:])
                            p.inc(ve.drain(), "v_rope")
                            p.wait(gp, "v_rope")
                            p.inc(gp.dma_start(
                                out=hbmt[h * 128:(h + 1) * 128, tsl],
                                in_=colt[:]), "g_qk", 16)
                            continue
                        p.wait(gp, "v_qk")
                        p.inc(gp.dma_start(out=rot[0:64, :],
                                           in_=raw[64:128, :]), "g_rot", 16)
                        p.inc(gp.dma_start(out=rot[64:128, :],
                                           in_=raw[0:64, :]), "g_rot", 16)
                        p.wait(ve, "v_tab")
                        p.wait(ve, "g_rot")
                        p.wait(ve, "g_qk", p.n("g_qk"))
                        ve.tensor_mul(rt1[:], raw[:], tb4[ti0][:])
                        ve.tensor_mul(colt[:], rot[:], tb4[ti1][:])
                        ve.tensor_add(colt[:], rt1[:], colt[:])
                        p.inc(ve.drain(), "v_rope")
                        p.wait(gp, "v_rope")
                        p.inc(gp.dma_start(
                            out=hbmt[h * 128:(h + 1) * 128, tsl],
                            in_=colt[:]), "g_qk", 16)
                    p.inc(te.nop(), "t_wq")

                p.wait(te, "d_wq", dwq)
                for hg in range(HG):
                    for t4 in range(4):
                        ti = tcol * 4 + t4
                        p.wait(te, "v_v", p.n("t_v"))
                        lastv = None
                        for f in range(FIN):
                            lastv = te.matmul(
                                ps_v[hg][:],
                                xnTc[:, f * 512 + t4 * 128:
                                     f * 512 + (t4 + 1) * 128],
                                wqkb[bv][:, f * HD2 + hg * 512:
                                         f * HD2 + (hg + 1) * 512],
                                start=(f == 0), stop=(f == FIN - 1))
                        p.inc(te.drain(), "t_v")
                        p.wait(ve, "t_v")
                        p.wait(ve, "g_v", p.n("g_v"))
                        ve.tensor_copy(
                            out=vcol[:, hg * 512:(hg + 1) * 512],
                            in_=ps_v[hg][:])
                        p.inc(ve.drain(), "v_v")
                        p.wait(gp, "v_v")
                        for hh in range(4):
                            u = hg * 4 + hh
                            p.inc(gp.dma_start(
                                out=v_hbm[:, (u * TT + ti) * 128:
                                          (u * TT + ti + 1) * 128],
                                in_=vcol[:, hg * 512 + hh * 128:
                                         hg * 512 + (hh + 1) * 128]),
                                "g_v", 16)
                p.inc(te.nop(), "t_wq")

            for tt in range(0 if getattr(cfg, "dbg_no_xn8", False) else OTT):
                ln_tile(xown[tt * 128:(tt + 1) * 128, :], xnf, None)
                for f in range(FIN):
                    p.wait(ve, "g_x8", p.n("g_x8"))
                    transpose128(xnf[:, f * 128:(f + 1) * 128],
                                 x8s[:, 0:128], f32=True, ready="s_ln")
                    p.wait(gp, "v_tp")
                    p.inc(gp.dma_start(
                        out=xn8[f * 128:(f + 1) * 128,
                                tt * 128:(tt + 1) * 128],
                        in_=x8s[:, 0:128]), "g_x8", 16)
            p.wait(gp, "g_qk")
            p.wait(gp, "g_v")
            p.wait(gp, "g_x8")
            barrier()

        if cfg.stop_after < 2:
            _finish(p, gp, out, cs, dbg, debug_taps, locals())
            return nc, dbg

        with ExitStack() as ph:
            sb = lambda n, sh, dt=F32: ph.enter_context(
                nc.sbuf_tensor(n, list(sh), dt))
            ps = lambda n, sh, dt=F32: ph.enter_context(
                nc.psum_tensor(n, list(sh), dt))
            qTh = [sb("qTh%d" % i, [128, S], BF16) for i in range(2)]
            kTh = [sb("kTh%d" % i, [128, S], BF16) for i in range(2)]
            Ebuf = [sb("Ebuf%d" % i, [128, S], BF16) for i in range(2)]
            lacc = [sb("lacc%d" % i, [128, max(CH, 1)]) for i in range(2)]
            ltot = sb("ltot", [128, 1])
            invl = sb("invl", [128, 1])
            ohw = [sb("ohw%d" % i, [128, CH], BF16) for i in range(8)]
            smask = [sb("smask%d" % i, [128, 512]) for i in range(2)]
            tmp_sb = sb("tmp_sb", [max(CH, 1), 512])
            ps_s = [ps("ps_s%d" % i, [128, 512]) for i in range(4)]
            ps_tmp = ps("ps_tmp", [max(CH, 1), 512])

            nEx = 0
            nCS = 0
            for u in range(hpc):
                ub = u % 2
                p.wait(sy, "t_p12", max(0, u - 1))
                p.inc(sy.dma_start(out=qTh[ub][:],
                                   in_=q_hbm[u * 128:(u + 1) * 128, :]),
                      "d_qk1", 16)
                p.inc(sy.dma_start(out=kTh[ub][:],
                                   in_=k_hbm[u * 128:(u + 1) * 128, :]),
                      "d_qk1", 16)
                p.wait(te, "d_qk1")

                def emit_colsums(pv_):
                    p.wait(te, "v_oh", pv_["voh"])
                    p.wait(te, "l_rdy", pv_["lrdy"])
                    nonlocal nCS
                    qt_, eb_ = pv_["qt"], pv_["eb"]
                    for c in range(pv_["nch"]):
                        mcs = te.matmul(ps_tmp[:], ohw[c % 8][:],
                                        Ebuf[eb_][:, c * 512:(c + 1) * 512],
                                        start=(qt_ == 0 and c == 0),
                                        stop=(qt_ == QT - 1 and
                                              c == pv_["nch"] - 1))
                        if qt_ == QT - 1 and c == pv_["nch"] - 1:
                            p.inc(te.drain(), "t_cs")
                        else:
                            p.inc(mcs, "t_cs")
                        nCS += 1

                prev = None
                for qt in range(QT):
                    nch = qt // 4 + 1
                    eb = qt % 2
                    qsl = slice(qt * 128, (qt + 1) * 128)
                    base = p.n("t_s1")
                    qend = base + nch
                    for c in range(nch):
                        bi = nEx % 4
                        p.wait(te, "p1_free", max(0, nEx - 3))
                        p.inc(te.matmul(ps_s[bi][:], qTh[ub][:, qsl],
                                        kTh[ub][:, c * 512:(c + 1) * 512],
                                        start=True, stop=True), "t_s1")
                        nEx += 1
                    p.inc(te.drain(), "t_s1", 3)
                    if prev is not None:
                        emit_colsums(prev)
                    for c in range(nch):
                        bi = (nEx - nch + c) % 4
                        vflush = min(base + c + 3, qend + 1)
                        diag = (c == nch - 1)
                        if diag:
                            p.wait(ve, "t_s1", vflush)
                            ve.tensor_add(
                                smask[eb][:], ps_s[bi][:],
                                cs["p1tri_%d" % (qt % 4)][:])
                            p.inc(ve.drain(), "v_m1")
                            p.wait(sc, "v_m1")
                            src = smask[eb][:]
                        else:
                            p.wait(sc, "t_s1", vflush)
                            src = ps_s[bi][:]
                        p.inc(sc.activation(
                            out=Ebuf[eb][:, c * 512:(c + 1) * 512], in_=src,
                            func=AF.Exp, bias=cs["negC_col"][:, 0:1],
                            accum_out=lacc[eb][:, c:c + 1]), "p1_free")
                    p.inc(sc.drain(), "l_rdy")
                    p.wait(ve, "l_rdy")
                    ve.reduce_sum(ltot[:], lacc[eb][:, 0:nch], axis=AX.X)
                    ve.drain()
                    ve.reciprocal(out=invl[:], in_=ltot[:])
                    ve.drain()
                    p.wait(ve, "t_cs", max(0, nCS))
                    for c in range(nch):
                        ve.tensor_scalar(out=ohw[c % 8][:],
                                         in0=cs["oh_%d" % c][:],
                                         scalar1=invl[:], scalar2=None,
                                         op0=ALU.mult)
                    p.inc(ve.drain(), "v_oh", nch)
                    prev = {"qt": qt, "nch": nch, "eb": eb,
                            "voh": p.n("v_oh"), "lrdy": p.n("l_rdy")}
                emit_colsums(prev)
                p.wait(ve, "t_cs")
                ve.tensor_copy(out=tmp_sb[:], in_=ps_tmp[:])
                p.inc(ve.drain(), "v_ts")
                p.wait(gp, "v_ts")
                g = u // cfg.UPG
                ul = u % cfg.UPG
                for cch in range(max(CH, 1)):
                    p.inc(gp.dma_start(
                        out=tsum[g][ul * KC + cch * 4:
                                    ul * KC + (cch + 1) * 4, :],
                        in_=tmp_sb[cch:cch + 1, :]), "g_ts", 16)
                p.wait(te, "g_ts")
                p.inc(te.nop(), "t_p12")
            barrier()

        if cfg.stop_after < 3:
            _finish(p, gp, out, cs, dbg, debug_taps, locals())
            return nc, dbg

        with ExitStack() as ph:
            sb = lambda n, sh, dt=F32: ph.enter_context(
                nc.sbuf_tensor(n, list(sh), dt))
            upg = cfg.UPG
            PBG = upg * KC
            b_ge = sb("b_ge", [128, 128])
            b_cnt = sb("b_cnt", [128, 1])
            b_c4 = sb("b_c4", [upg, 1])
            b_L = sb("b_L", [upg, 1])
            b_H = sb("b_H", [upg, 1])
            b_m = sb("b_m", [upg, 1])
            b_sel = sb("b_sel", [upg, 1])
            b_nsel = sb("b_nsel", [upg, 1])
            b_t1 = sb("b_t1", [upg, 1])
            b_T = sb("b_T", [128, 1])
            h01 = sb("h01", [128, 128])
            ps_b = ph.enter_context(nc.psum_tensor("ps_b", [128, 512], F32))

            for g in range(cfg.NBG):
                p.wait(te, "v_A", g)
                ve.memset(b_L[:], 0.0)
                p.inc(ve.memset(b_H[:], float(S + 1)), "v_bi")

                def thresh_bcast(src_col):
                    p.wait(te, "v_bi")
                    te.matmul(ps_b[0:PBG, 0:1], cs["bis_bc"][:],
                              src_col, start=True, stop=True)
                    p.inc(te.drain(), "t_bT")
                    p.wait(ve, "t_bT")
                    p.inc(ve.tensor_copy(out=b_T[0:PBG, :],
                                         in_=ps_b[0:PBG, 0:1]), "v_bT")
                    ve.drain()

                for it in range(cfg.bisect_iters):
                    ve.tensor_add(b_m[:], b_L[:], b_H[:])
                    ve.drain()
                    ve.tensor_scalar(out=b_m[:], in0=b_m[:],
                                     scalar1=0.5, scalar2=None,
                                     op0=ALU.mult)
                    p.inc(ve.drain(), "v_bi")
                    thresh_bcast(b_m[:])
                    ve.tensor_scalar(out=b_ge[0:PBG, :],
                                     in0=tsum[g][0:PBG, :],
                                     scalar1=b_T[0:PBG, :], scalar2=None,
                                     op0=ALU.is_ge)
                    ve.drain()
                    ve.reduce_sum(b_cnt[0:PBG, :], b_ge[0:PBG, :],
                                  axis=AX.X)
                    p.inc(ve.drain(), "v_bc")
                    p.wait(te, "v_bc")
                    te.matmul(ps_b[0:upg, 128:129],
                              cs["bis_sel"][0:PBG, :],
                              b_cnt[0:PBG, :], start=True, stop=True)
                    p.inc(te.drain(), "t_bc")
                    p.wait(ve, "t_bc")
                    ve.tensor_copy(out=b_c4[:], in_=ps_b[0:upg, 128:129])
                    ve.drain()
                    ve.tensor_scalar(out=b_sel[:], in0=b_c4[:],
                                     scalar1=float(cfg.heavy), scalar2=None,
                                     op0=ALU.is_ge)
                    ve.drain()
                    ve.tensor_scalar(out=b_nsel[:], in0=b_sel[:],
                                     scalar1=-1.0, scalar2=1.0,
                                     op0=ALU.mult, op1=ALU.add)
                    ve.tensor_sub(b_t1[:], b_m[:], b_L[:])
                    ve.drain()
                    ve.tensor_mul(b_t1[:], b_t1[:], b_sel[:])
                    ve.drain()
                    ve.tensor_add(b_L[:], b_L[:], b_t1[:])
                    ve.tensor_sub(b_t1[:], b_m[:], b_H[:])
                    ve.drain()
                    ve.tensor_mul(b_t1[:], b_t1[:], b_nsel[:])
                    ve.drain()
                    ve.tensor_add(b_H[:], b_H[:], b_t1[:])
                    p.inc(ve.drain(), "v_bi")
                thresh_bcast(b_L[:])
                ve.tensor_scalar(out=h01[0:PBG, :],
                                 in0=tsum[g][0:PBG, :],
                                 scalar1=b_T[0:PBG, :], scalar2=None,
                                 op0=ALU.is_ge)
                p.inc(ve.drain(), "v_h01")
                p.wait(te, "v_h01")
                te.transpose(ps_b[:, 256:256 + PBG], h01[0:PBG, :],
                             cs["ident_f32"][0:PBG, 0:PBG])
                p.inc(te.drain(), "t_h01")
                p.wait(ve, "t_h01")
                asl = slice(g * PBG, (g + 1) * PBG)
                ve.tensor_scalar(out=Aadd[:, asl],
                                 in0=ps_b[:, 256:256 + PBG],
                                 scalar1=1.0e38, scalar2=NBIG,
                                 op0=ALU.mult, op1=ALU.add)
                ve.tensor_scalar(out=Amin[:, asl],
                                 in0=ps_b[:, 256:256 + PBG],
                                 scalar1=2.0e38, scalar2=NBIG,
                                 op0=ALU.mult, op1=ALU.add)
                ve.drain()
                ve.tensor_scalar(out=Abias[:, asl], in0=Aadd[:, asl],
                                 scalar1=-CSHIFT, scalar2=None,
                                 op0=ALU.add)
                p.inc(ve.drain(), "v_A")
            barrier()

        if cfg.stop_after < 4:
            _finish(p, gp, out, cs, dbg, debug_taps, locals())
            return nc, dbg

        with ExitStack() as ph:
            sb = lambda n, sh, dt=F32: ph.enter_context(
                nc.sbuf_tensor(n, list(sh), dt))
            ps = lambda n, sh, dt=F32: ph.enter_context(
                nc.psum_tensor(n, list(sh), dt))
            qTh = [sb("qTh3%d" % i, [128, S], BF16) for i in range(2)]
            kTh = [sb("kTh3%d" % i, [128, S], BF16) for i in range(2)]
            vh = [sb("vh%d" % i, [128, TT * 128], BF16) for i in range(2)]
            E2 = [sb("E2_%d" % i, [128, 512], BF16) for i in range(4)]
            tcaus = sb("tcaus", [128, 512])
            tband = sb("tband", [128, 512])
            smask3 = sb("smask3", [128, 512])
            l2sb = sb("l2sb", [1, 512])
            invl2 = sb("invl2", [1, 512])
            il2bc = sb("il2bc", [128, 512])
            atq = sb("atq", [128, 512], BF16)
            ps_s3 = [ps("ps_s3%d" % i, [128, 512]) for i in range(4)]
            ps_pv = ps("ps_pv", [128, 512])
            ps_l2 = ps("ps_l2", [1, 512])
            ps_bc = ps("ps_bc", [128, 512])

            nK = 0
            nQG = 0
            for u in range(hpc):
                ub = u % 2
                p.wait(sy, "t_p3", max(0, (u - 1) * QG))
                p.inc(sy.dma_start(out=qTh[ub][:],
                                   in_=q_hbm[u * 128:(u + 1) * 128, :]),
                      "d_qk3", 16)
                p.inc(sy.dma_start(out=kTh[ub][:],
                                   in_=k_hbm[u * 128:(u + 1) * 128, :]),
                      "d_qk3", 16)
                p.wait(gp, "t_p3", max(0, (u - 1) * QG))
                p.inc(gp.dma_start(
                    out=vh[ub][:],
                    in_=v_hbm[:, u * TT * 128:(u + 1) * TT * 128]),
                    "g_v3", 16)
                p.wait(te, "d_qk3")
                p.wait(te, "g_v3")
                for qg in range(QG):
                    qsl = slice(qg * 512, (qg + 1) * 512)
                    kmax = (qg * 512 + 511) // 128
                    for g0 in range(0, kmax + 1, 4):
                        grp = list(range(g0, min(g0 + 4, kmax + 1)))
                        sbase = p.n("t_s3")
                        gend = sbase + len(grp)
                        for j, kc in enumerate(grp):
                            bi = (nK + j) % 4
                            p.wait(te, "p3_free", max(0, nK + j - 3))
                            p.inc(te.matmul(
                                ps_s3[bi][:],
                                kTh[ub][:, kc * 128:(kc + 1) * 128],
                                qTh[ub][:, qsl],
                                start=True, stop=True), "t_s3")
                        p.inc(te.drain(), "t_s3", 3)
                        for j, kc in enumerate(grp):
                            bi = (nK + j) % 4
                            vfl = min(sbase + j + 3, gend + 1)
                            rel = 128 * kc - 512 * qg
                            acol = Amin[:, u * KC + kc:u * KC + kc + 1]
                            abias = Abias[:, u * KC + kc:u * KC + kc + 1]
                            p.wait(sc, "t_pv", max(0, nK + j - 3))
                            if rel < -(cfg.recent + 127):
                                p.wait(sc, "t_s3", vfl)
                                sc.activation(out=E2[bi][:],
                                              in_=ps_s3[bi][:],
                                              func=AF.Exp, bias=abias)
                                sc.sem_inc(p.s("p3_free"), 1)
                                p.cnt["p3_free"] += 1
                            else:
                                p.wait(ve, "t_s3", vfl)
                                ve.tensor_add(tcaus[:], ps_s3[bi][:],
                                              cs["p3ca_%d" % rel][:])
                                p.inc(ve.tensor_add(tband[:], ps_s3[bi][:],
                                                    cs["p3ba_%d" % rel][:]),
                                      "p3_free")
                                ve.scalar_tensor_tensor(
                                    out=smask3[:], in0=tcaus[:], scalar=acol,
                                    in1=tband[:], op0=ALU.min, op1=ALU.max)
                                p.inc(ve.drain(), "v_m3")
                                p.wait(sc, "v_m3")
                                sc.activation(out=E2[bi][:], in_=smask3[:],
                                              func=AF.Exp,
                                              bias=cs["negC_col"][:, 0:1])
                        p.inc(sc.drain(), "s_e2", len(grp))
                        p.wait(te, "s_e2")
                        for j, kc in enumerate(grp):
                            bi = (nK + j) % 4
                            te.matmul(ps_pv[:],
                                      vh[ub][:, kc * 128:(kc + 1) * 128],
                                      E2[bi][:], start=(kc == 0),
                                      stop=(kc == kmax))
                            ml2 = te.matmul(ps_l2[:], cs["ones_col_bf"][:],
                                            E2[bi][:], start=(kc == 0),
                                            stop=(kc == kmax))
                            if kc == kmax:
                                p.inc(te.drain(), "t_pv")
                            else:
                                p.inc(ml2, "t_pv")
                        nK += len(grp)
                    p.wait(ve, "t_pv")
                    ve.tensor_copy(out=l2sb[:], in_=ps_l2[:])
                    ve.drain()
                    ve.reciprocal(out=invl2[:], in_=l2sb[:])
                    p.inc(ve.drain(), "v_il2")
                    p.wait(te, "v_il2")
                    te.matmul(ps_bc[:], cs["ones_row_f32"][:], invl2[:],
                              start=True, stop=True)
                    p.inc(te.drain(), "t_bc3")
                    p.wait(ve, "t_bc3")
                    ve.tensor_copy(out=il2bc[:], in_=ps_bc[:])
                    ve.drain()
                    p.wait(ve, "g_at", max(0, p.n("g_at") - 16))
                    ve.tensor_mul(atq[:], ps_pv[:], il2bc[:])
                    p.inc(ve.drain(), "v_at")
                    p.wait(gp, "v_at")
                    p.inc(gp.dma_start(
                        out=at_hbm[u * 128:(u + 1) * 128, qsl],
                        in_=atq[:]), "g_at", 16)
                    nQG += 1
                    p.wait(te, "v_at", nQG)
                    p.inc(te.nop(), "t_p3")
            p.wait(gp, "g_at")
            barrier()

        if cfg.stop_after < 5:
            _finish(p, gp, out, cs, dbg, debug_taps, locals())
            return nc, dbg

        ffnO = outer.enter_context(
            nc.sbuf_tensor("ffnO", [128, OTT * D], BF16, side="right"))
        attnF = outer.enter_context(
            nc.sbuf_tensor("attnF", [128, FIN * ownT], BF16, side="right"))
        with ExitStack() as ph:
            exb = ph.enter_context(
                nc.sbuf_tensor("exb", [128, 2 * ownT], BF16))
            s_ld = p.s("x_ld")
            s_st = p.s("x_st")
            s_af = p.s("x_af")
            with gp.register("oc") as oc, gp.register("sv") as sv, \
                    gp.register("pf") as pf:
                arm_parity(oc, cfg.cpb)
                for jj in range(cfg.cpb):
                    with gp.If_eq(oc, jj):
                        w = jj % 2
                        pbc = (jj ^ 1) * ownT
                        obc = jj * ownT
                        for u in range(hpc):
                            sl = (u % 2) * ownT
                            if u >= 2:
                                gp.wait_ge(s_st, (u - 1) * 16)
                            gp.dma_start(
                                out=exb[:, sl:sl + ownT],
                                in_=at_hbm[u * 128:(u + 1) * 128,
                                           pbc:pbc + ownT]).then_inc(s_ld, 16)
                            gp.wait_ge(s_ld, (u + 1) * 16)
                            gp.dma_start(
                                out=sh_attn[w, u * 128:(u + 1) * 128, :],
                                in_=exb[:, sl:sl + ownT]).then_inc(s_st, 16)
                        gp.wait_ge(s_st, hpc * 16)
                        gp.reg_mov(sv, MAGIC_ATTN)
                        gp.store(sh_flag[w, 0:1, 1:2], sv)
                        gp.reg_mov(pf, 0)
                        with gp.While(RuntimeValue(pf) != MAGIC_ATTN):
                            gp.reg_load(pf, sh_flag[1 - w, 0:1, 1:2])
                        for f in range(FIN):
                            uu = f % hpc
                            if (f // hpc) == w:
                                src = at_hbm[uu * 128:(uu + 1) * 128,
                                             obc:obc + ownT]
                            else:
                                src = sh_attn[1 - w, uu * 128:(uu + 1) * 128, :]
                            gp.dma_start(
                                out=attnF[:, f * ownT:(f + 1) * ownT],
                                in_=src).then_inc(s_af, 16)
                        gp.wait_ge(s_af, FIN * 16)
            barrier()

        if cfg.stop_after < 6:
            _finish(p, gp, out, cs, dbg, debug_taps, locals())
            return nc, dbg

        with ExitStack() as ph:
            sb = lambda n, sh, dt=F32: ph.enter_context(
                nc.sbuf_tensor(n, list(sh), dt))
            ps = lambda n, sh, dt=F32: ph.enter_context(
                nc.psum_tensor(n, list(sh), dt))
            w1b = [sb("w1b%d" % i, [128, FIN * 2 * 128], FP8) for i in range(2)]
            xn8t = sb("xn8t", [128, FIN * TGF], FP8)
            ffg = sb("ffg", [128, TGF], BF16)
            ff1 = sb("ff1", [128, TGF], BF16)
            gvt = sb("gvt", [128, TGF], BF16)
            gt1 = sb("gt1", [128, TGF], BF16)
            gt2 = sb("gt2", [128, TGF], BF16)
            prod = sb("prod", [128, TGF], BF16)
            act8 = sb("act8", [128, HC * TGF], FP8)
            wl8b = [sb("wl8b%d" % i, [128, D], FP8) for i in range(2)]
            ps_g = [ps("ps_g%d" % i, [128, TGF]) for i in range(2)]
            ps_o = [ps("ps_o%d" % i, [128, 512]) for i in range(NFG)]

            wg8m = wg8.rearrange("n m -> (n m)").rearrange(
                "(g b c) -> g b c", g=2, b=D)
            wl8m = wl8.rearrange("n m -> (n m)").rearrange(
                "(b c) -> b c", b=FF)
            nH = 0
            nO = 0
            for tg in range(NTG):
                tgs = slice(tg * TGF, (tg + 1) * TGF)
                p.wait(sy, "t_g1")
                for f in range(FIN):
                    p.inc(sy.dma_start(out=xn8t[:, f * TGF:(f + 1) * TGF],
                                       in_=xn8[f * 128:(f + 1) * 128, tgs]),
                          "d_x8", 16)
                for hc in range(HC):
                    bi = nH % 2
                    p.wait(sy, "t_g1", max(0, nH - 1))
                    p.wait(gp, "t_g1", max(0, nH - 1))
                    for g in range(2):
                        eng_w = sy if g == 0 else gp
                        sem_w = "d_w1" if g == 0 else "g_w1"
                        for f in range(FIN):
                            p.inc(eng_w.dma_start(
                                out=w1b[bi][:, (f * 2 + g) * 128:
                                            (f * 2 + g + 1) * 128],
                                in_=wg8m[g, f * 128:(f + 1) * 128,
                                         hc * 128:(hc + 1) * 128]),
                                sem_w, 16)
                    p.wait(te, "d_w1")
                    p.wait(te, "g_w1")
                    p.wait(te, "d_x8")
                    p.wait(te, "v_ffc", max(0, 2 * nH))
                    lg_ = None
                    for g in range(2):
                        for f in range(FIN):
                            lg_ = te.matmul(
                                ps_g[g][:],
                                w1b[bi][:, (f * 2 + g) * 128:
                                        (f * 2 + g + 1) * 128],
                                xn8t[:, f * TGF:(f + 1) * TGF],
                                start=(f == 0), stop=(f == FIN - 1))
                    p.inc(te.drain(), "t_g1")
                    p.wait(ve, "t_g1")
                    ve.tensor_copy(out=ffg[:], in_=ps_g[0][:])
                    ve.tensor_copy(out=ff1[:], in_=ps_g[1][:])
                    p.inc(ve.drain(), "v_ffc", 2)
                    ve.tensor_mul(gt1[:], ffg[:], ffg[:])
                    ve.tensor_mul(gt2[:], gt1[:], ffg[:])
                    ve.tensor_scalar(out=gt1[:], in0=gt2[:],
                                     scalar1=0.044677734375, scalar2=None,
                                     op0=ALU.mult)
                    ve.tensor_add(gt2[:], ffg[:], gt1[:])
                    ve.tensor_scalar(out=gt1[:], in0=gt2[:],
                                     scalar1=0.796875, scalar2=None,
                                     op0=ALU.mult)
                    p.inc(ve.drain(), "v_gel")
                    p.wait(sc, "v_gel")
                    sc.activation(out=gvt[:], in_=gt1[:], func=AF.Tanh)
                    p.inc(sc.drain(), "s_gelu")
                    p.wait(ve, "s_gelu")
                    ve.tensor_scalar(out=gt2[:], in0=gvt[:], scalar1=1.0,
                                     scalar2=0.5, op0=ALU.add, op1=ALU.mult)
                    ve.tensor_mul(gt1[:], ffg[:], gt2[:])
                    ve.tensor_mul(prod[:], gt1[:], ff1[:])
                    ve.tensor_copy(
                        out=act8[:, hc * TGF:(hc + 1) * TGF], in_=prod[:])
                    p.inc(ve.drain(), "v_ff")
                    nH += 1
                for t4 in range(TGF // 128):
                    ti = tg * (TGF // 128) + t4
                    p.wait(te, "v_fo", max(0, nO * NFG))
                    for hc in range(HC):
                        bi = hc % 2
                        p.wait(sy, "t_o", max(0, p.n("t_o") - 1))
                        p.inc(sy.dma_start(
                            out=wl8b[bi][:],
                            in_=wl8m[hc * 128:(hc + 1) * 128, :]), "d_wl", 16)
                        p.wait(te, "d_wl")
                        p.wait(te, "v_ff")
                        lo = None
                        for fg in range(NFG):
                            lo = te.matmul(
                                ps_o[fg][:],
                                act8[:, hc * TGF + t4 * 128:
                                     hc * TGF + (t4 + 1) * 128],
                                wl8b[bi][:, fg * 512:(fg + 1) * 512],
                                start=(hc == 0), stop=(hc == HC - 1))
                        if hc == HC - 1:
                            p.inc(te.drain(), "t_o")
                        else:
                            p.inc(lo, "t_o")
                    p.wait(ve, "t_o")
                    for fg in range(NFG):
                        p.inc(ve.tensor_copy(
                            out=ffnO[:, ti * D + fg * 512:
                                     ti * D + (fg + 1) * 512],
                            in_=ps_o[fg][:]), "v_fo")
                    nO += 1
            barrier()

        if cfg.stop_after < 7:
            _finish(p, gp, out, cs, dbg, debug_taps, locals())
            return nc, dbg

        with ExitStack() as ph:
            sb = lambda n, sh, dt=F32: ph.enter_context(
                nc.sbuf_tensor(n, list(sh), dt))
            ps = lambda n, sh, dt=F32: ph.enter_context(
                nc.psum_tensor(n, list(sh), dt))
            wo_bf = sb("wo_bf", [128, FIN * D], BF16)
            wof = [sb("wof%d" % i, [128, D]) for i in range(2)]
            xr = [sb("xr%d" % i, [128, D]) for i in range(2)]
            outb = [sb("outb%d" % i, [128, D]) for i in range(2)]
            ps_w = [ps("ps_w%d" % i, [128, 512]) for i in range(NFG)]

            for f in range(FIN):
                bi = f % 2
                p.wait(sy, "v_wo", max(0, f - 1))
                p.inc(sy.dma_start(out=wof[bi][:],
                                   in_=wo_t[f * 128:(f + 1) * 128, :]),
                      "d_wo", 16)
                p.wait(ve, "d_wo")
                ve.tensor_copy(out=wo_bf[:, f * D:(f + 1) * D],
                               in_=wof[bi][:])
                p.inc(ve.drain(), "v_wo")
            for tt in range(OTT):
                bi = tt % 2
                p.wait(sy, "v_out", max(0, (tt - 1) * NFG))
                p.inc(sy.dma_start(out=xr[bi][:],
                                   in_=xown[tt * 128:(tt + 1) * 128, :]),
                      "d_xr", 16)
                p.wait(te, "v_wo")
                p.wait(te, "v_out", tt * NFG)
                lw = None
                for f in range(FIN):
                    for fg in range(NFG):
                        lw = te.matmul(
                            ps_w[fg][:],
                            attnF[:, f * ownT + tt * 128:
                                  f * ownT + (tt + 1) * 128],
                            wo_bf[:, f * D + fg * 512:f * D + (fg + 1) * 512],
                            start=(f == 0), stop=(f == FIN - 1))
                p.inc(te.drain(), "t_wo")
                p.wait(ve, "t_wo")
                p.wait(ve, "d_xr")
                p.wait(ve, "g_out", max(0, (tt - 1) * 16))
                for fg in range(NFG):
                    fsl = slice(fg * 512, (fg + 1) * 512)
                    ve.tensor_add(outb[bi][:, fsl], ps_w[fg][:],
                                  xr[bi][:, fsl])
                    ve.tensor_add(
                        outb[bi][:, fsl], outb[bi][:, fsl],
                        ffnO[:, tt * D + fg * 512:tt * D + (fg + 1) * 512])
                    p.inc(ve.drain(), "v_out")
                p.wait(gp, "v_out")
                p.inc(gp.dma_start(out=out[tt * 128:(tt + 1) * 128, :],
                                   in_=outb[bi][:]), "g_out", 16)
            _finish(p, gp, out, cs, dbg, debug_taps, locals())
    return nc, dbg


def _finish(p, gp, out, cs, dbg, debug_taps, env):
    nc = p.nc
    for name in debug_taps:
        if name in dbg:
            continue
        tile = None
        m = re.match(r"^(tsum)(\d+)$", name)
        if m:
            tile = env[m.group(1)][int(m.group(2))]
        elif name in env:
            tile = env[name]
        if tile is None:
            continue
        shp = list(tile.shape)
        d = nc.dram_tensor("dbg_" + name, shp, tile.dtype,
                           kind="ExternalOutput").ap()
        dbg[name] = d
        p.inc(gp.dma_start(out=d, in_=tile[:]), "g_out", 16)
    p.wait(gp, "g_out")



def prep_inputs(cfg, x, Wq, Wk, Wv, Wo, w_gating, w_linear):
    B, S, D = cfg.B, cfg.S, cfg.D
    hpc, dh, cpb = cfg.hpc, cfg.dh, cfg.cpb
    tabs = _rope_tables(cfg)
    xf = np.ascontiguousarray(np.asarray(x, dtype=np.float32).reshape(B * S, D))
    wo_tv = np.ascontiguousarray(np.asarray(Wo, dtype=np.float32).T)
    wgv = np.ascontiguousarray(np.asarray(w_gating, dtype=np.float32))
    wlv = np.ascontiguousarray(np.asarray(w_linear, dtype=np.float32))
    Wq = np.asarray(Wq, dtype=np.float32)
    Wk = np.asarray(Wk, dtype=np.float32)
    Wv = np.asarray(Wv, dtype=np.float32)
    in_maps = []
    for c in range(cfg.n_cores):
        b = c // cpb
        hs = hpc * (c % 2)
        m = {}
        m["xb"] = np.ascontiguousarray(xf[b * S:(b + 1) * S])
        m["xown"] = np.ascontiguousarray(xf[c * cfg.ownT:(c + 1) * cfg.ownT])
        m["wq_t"] = np.ascontiguousarray(Wq[hs * dh:(hs + hpc) * dh, :].T)
        m["wk_t"] = np.ascontiguousarray(Wk[hs * dh:(hs + hpc) * dh, :].T)
        m["wv_t"] = np.ascontiguousarray(Wv[hs * dh:(hs + hpc) * dh, :].T)
        m["wo_t"] = wo_tv
        m["wg"] = wgv
        m["wl"] = wlv
        for k, v in tabs.items():
            m[k] = v
        in_maps.append(m)
    return in_maps


def kernel(L0_attn_hidden_states, L0_attn_Wq, L0_attn_Wk, L0_attn_Wv,
           L0_attn_Wo, L0_ffn_w_gating, L0_ffn_w_linear):
    cfg = Cfg()
    in_maps = prep_inputs(cfg, L0_attn_hidden_states, L0_attn_Wq,
                          L0_attn_Wk, L0_attn_Wv, L0_attn_Wo,
                          L0_ffn_w_gating, L0_ffn_w_linear)
    nc, _ = build_program(cfg)
    res = run_bass_kernel_spmd(nc, in_maps, core_ids=list(range(cfg.n_cores)))
    shards = [r["out"] for r in res.results]
    outv = np.concatenate(shards, axis=0).reshape(cfg.B, cfg.S, cfg.D)
    return outv


# version 13
# speedup vs baseline: 1.2734x; 1.0291x over previous
import math
import re
from contextlib import ExitStack

import numpy as np
import ml_dtypes

import concourse.bass as bass
import concourse.mybir as mybir
from concourse.bass_utils import run_bass_kernel_spmd
from concourse.expressions import RuntimeValue

F32 = mybir.dt.float32
BF16 = mybir.dt.bfloat16
FP8 = mybir.dt.float8e4
I32 = mybir.dt.int32
AF = mybir.ActivationFunctionType
ALU = mybir.AluOpType
AX = mybir.AxisListType

NBIG = -1.0e38
CSHIFT = 14.0
MAGIC_SENT = 0x0BAD0001
MAGIC_ATTN = 0x51CA0001


class Cfg:
    def __init__(self, B=2, S=4096, D=2048, H=16, FF=8192, n_cores=8,
                 bisect_iters=42, theta=10000.0, stop_after=99):
        self.B, self.S, self.D, self.H, self.FF = B, S, D, H, FF
        self.n_cores = n_cores
        self.dh = 128
        self.theta = theta
        self.bisect_iters = bisect_iters
        self.stop_after = stop_after
        assert D == H * self.dh
        self.cpb = n_cores // B
        assert self.cpb % 2 == 0
        self.hpc = H // 2
        self.ownT = B * S // n_cores
        self.heavy = int(0.1 * S)
        self.recent = int(0.1 * S)
        self.FIN = D // 128
        self.KC = S // 128
        self.CH = S // 512
        self.QT = S // 128
        self.QG = S // 512
        self.HC = FF // 128
        self.TT = S // 128
        self.TCOL = S // 512
        self.OTT = self.ownT // 128
        self.TGF = min(512, self.ownT)
        self.NTG = self.ownT // self.TGF
        self.HD2 = self.hpc * self.dh
        assert self.HD2 % 512 == 0
        self.HG = self.HD2 // 512
        self.UPG = min(self.hpc, 128 // self.KC)
        self.NBG = (self.hpc + self.UPG - 1) // self.UPG
        assert self.hpc % self.UPG == 0
        assert self.CH >= 1 and S % 512 == 0
        assert self.ownT % self.TGF == 0 and D % 512 == 0


def _rope_tables(cfg):
    dh = cfg.dh
    ex = np.arange(0, dh, 2, dtype=np.float64) / dh
    inv = 1.0 / (cfg.theta ** ex)
    pos = np.arange(cfg.S, dtype=np.float64)
    fr = inv[:, None] * pos[None, :]
    cos = np.cos(fr)
    sin = np.sin(fr)
    cos_full = np.concatenate([cos, cos], 0)
    sinpm = np.concatenate([-sin, sin], 0)
    qs = 1.0 / math.sqrt(dh)
    out = {}
    for name, tab in (("cq", cos_full * qs), ("sq", sinpm * qs),
                      ("ck", cos_full), ("sk", sinpm)):
        hi = tab.astype(np.float32)
        lo = (tab - hi.astype(np.float64)).astype(np.float32)
        out[name + "_hi"] = np.ascontiguousarray(hi)
        out[name + "_lo"] = np.ascontiguousarray(lo)
    return out


def _const_tables(cfg):
    c = {}
    c["ident_bf"] = np.eye(128, dtype=ml_dtypes.bfloat16)
    c["ident_f32"] = np.eye(128, dtype=np.float32)
    c["ones_col_bf"] = np.ones((128, 1), dtype=ml_dtypes.bfloat16)
    c["ones_row_f32"] = np.ones((1, 128), dtype=np.float32)
    p = np.arange(128, dtype=np.float32).reshape(128, 1)
    j = np.arange(512, dtype=np.float32)[None, :]
    big = np.float32(NBIG)
    for r in range(4):
        c["p1tri_%d" % r] = np.ascontiguousarray(np.where(
            j < 128 * r + p + 1, np.float32(0), big).astype(ml_dtypes.bfloat16))
    c["eps_col"] = np.full((128, 1), 1e-5, dtype=np.float32)
    c["negC_col"] = np.full((128, 1), -CSHIFT, dtype=np.float32)
    for rel in range(-512, 385, 128):
        cs_ = np.clip(rel + p, 0, 512)
        be_ = np.clip(rel + p + cfg.recent + 1, 0, 512)
        c["p3ca_%d" % rel] = np.ascontiguousarray(np.where(
            j >= cs_, np.float32(0), big).astype(ml_dtypes.bfloat16))
        c["p3ba_%d" % rel] = np.ascontiguousarray(np.where(
            (j >= cs_) & (j < be_), np.float32(0), big).astype(ml_dtypes.bfloat16))
    for ci in range(cfg.CH):
        m = np.zeros((128, cfg.CH), dtype=np.float32)
        m[:, ci] = 1.0
        c["oh_%d" % ci] = m
    upg = cfg.UPG
    pbg = upg * cfg.KC
    sel = np.zeros((pbg, upg), dtype=np.float32)
    for u in range(upg):
        sel[u * cfg.KC:(u + 1) * cfg.KC, u] = 1
    c["bis_sel"] = sel
    bc = np.zeros((upg, pbg), dtype=np.float32)
    for u in range(upg):
        bc[u, u * cfg.KC:(u + 1) * cfg.KC] = 1
    c["bis_bc"] = bc
    return c


class P:
    def __init__(self, nc, ctx):
        self.nc = nc
        self.ctx = ctx
        self.sems = {}
        self.cnt = {}

    def s(self, name):
        if name not in self.sems:
            self.sems[name] = self.ctx.enter_context(self.nc.semaphore(name))
            self.cnt[name] = 0
        return self.sems[name]

    def inc(self, instr, name, k=1):
        instr.then_inc(self.s(name), k)
        self.cnt[name] += k
        return self.cnt[name]

    def wait(self, eng, name, value=None):
        v = self.cnt.get(name, 0) if value is None else value
        self.s(name)
        if v > 0:
            eng.wait_ge(self.sems[name], v)

    def n(self, name):
        self.s(name)
        return self.cnt[name]


def build_program(cfg, debug_taps=(), hbm_taps=()):
    nc = bass.Bass("TRN2", target_bir_lowering=False, debug=False,
                   enable_partition_id=False, detect_race_conditions=False)

    def internal(name):
        return "ExternalOutput" if name in hbm_taps else "Internal"
    B, S, D, FF, dh = cfg.B, cfg.S, cfg.D, cfg.FF, cfg.dh
    hpc, FIN, KC, CH, QT, QG, HC = (cfg.hpc, cfg.FIN, cfg.KC, cfg.CH,
                                    cfg.QT, cfg.QG, cfg.HC)
    ownT, TT, TCOL, OTT, TGF, NTG = (cfg.ownT, cfg.TT, cfg.TCOL, cfg.OTT,
                                     cfg.TGF, cfg.NTG)
    HD2, HG = cfg.HD2, cfg.HG
    NFG = D // 512

    xb = nc.dram_tensor("xb", [S, D], F32, kind="ExternalInput").ap()
    xown = nc.dram_tensor("xown", [ownT, D], F32, kind="ExternalInput").ap()
    wq_t = nc.dram_tensor("wq_t", [D, HD2], F32, kind="ExternalInput").ap()
    wk_t = nc.dram_tensor("wk_t", [D, HD2], F32, kind="ExternalInput").ap()
    wv_t = nc.dram_tensor("wv_t", [D, HD2], F32, kind="ExternalInput").ap()
    wo_t = nc.dram_tensor("wo_t", [D, D], F32, kind="ExternalInput").ap()
    wg = nc.dram_tensor("wg", [2, D, FF], F32, kind="ExternalInput").ap()
    wl = nc.dram_tensor("wl", [FF, D], F32, kind="ExternalInput").ap()
    tabs = {}
    for nme in ("cq_hi", "cq_lo", "sq_hi", "sq_lo",
                "ck_hi", "ck_lo", "sk_hi", "sk_lo"):
        tabs[nme] = nc.dram_tensor(nme, [128, S], F32, kind="ExternalInput").ap()
    out = nc.dram_tensor("out", [ownT, D], F32, kind="ExternalOutput").ap()

    wg8 = nc.dram_tensor("wg8", [2 * D * FF // 4096, 4096], FP8,
                         kind="Internal").ap()
    wl8 = nc.dram_tensor("wl8", [FF * D // 4096, 4096], FP8,
                         kind="Internal").ap()
    wqkv16 = nc.dram_tensor("wqkv16", [3 * D * HD2 // 4096, 4096], BF16,
                            kind="Internal").ap()
    xn8 = nc.dram_tensor("xn8", [FIN * 128, ownT], FP8,
                         kind=internal("xn8")).ap()
    q_hbm = nc.dram_tensor("q_hbm", [hpc * 128, S], BF16,
                           kind=internal("q_hbm")).ap()
    k_hbm = nc.dram_tensor("k_hbm", [hpc * 128, S], BF16,
                           kind=internal("k_hbm")).ap()
    v_hbm = nc.dram_tensor("v_hbm", [128, hpc * TT * 128], BF16,
                           kind=internal("v_hbm")).ap()
    at_hbm = nc.dram_tensor("at_hbm", [hpc * 128, S], BF16,
                            kind=internal("at_hbm")).ap()
    sh_attn = nc.dram_tensor("sh_attn", [2, hpc * 128, ownT], BF16,
                             kind="Internal", addr_space="Shared").ap()
    sh_flag = nc.dram_tensor("sh_flag", [2, 1, 32], I32,
                             kind="Internal", addr_space="Shared").ap()

    dbg = {}
    consts = _const_tables(cfg)
    cth = {k: nc.inline_tensor(np.ascontiguousarray(v), "c_" + k)
           for k, v in consts.items()}

    outer = ExitStack()
    p = P(nc, outer)
    sy, te, ve, sc, gp = nc.sync, nc.tensor, nc.vector, nc.scalar, nc.gpsimd

    def barrier():
        nc.all_engine_barrier()

    def arm_parity(reg, mod):
        for arm in gp.switch_core_id(n=128):
            try:
                lg = arm.logical % 8
            except Exception:
                lg = 0
            gp.reg_mov(reg, lg % mod)
        gp.snap(reg, min_val=0, max_val=mod - 1)

    with outer:
        cs = {}
        for k, v in consts.items():
            t = outer.enter_context(
                nc.sbuf_tensor("cs_" + k, list(v.shape),
                               mybir.dt.from_np(v.dtype)))
            p.inc(sy.dma_start(out=t[:], in_=cth[k].ap()), "d_c", 16)
            cs[k] = t

        tsum = [outer.enter_context(
            nc.sbuf_tensor("tsum%d" % g, [128, 128], F32))
            for g in range(cfg.NBG)]
        Aadd = outer.enter_context(
            nc.sbuf_tensor("Aadd", [128, hpc * KC], F32))
        Amin = outer.enter_context(
            nc.sbuf_tensor("Amin", [128, hpc * KC], F32))
        Abias = outer.enter_context(
            nc.sbuf_tensor("Abias", [128, hpc * KC], F32))
        lntap = outer.enter_context(nc.sbuf_tensor("lntap", [128, 8], F32))
        for g in range(cfg.NBG):
            ve.memset(tsum[g][:], 0.0)

        if not getattr(cfg, "dbg_no_sentinel", False):
            with gp.register("oc0") as oc0:
                arm_parity(oc0, 2)
                with gp.register("sv0") as sv0:
                    gp.reg_mov(sv0, MAGIC_SENT)
                    for s in range(2):
                        with gp.If_eq(oc0, s):
                            gp.store(sh_flag[s, 0:1, 1:2], sv0)

        with ExitStack() as ph:
            qb = [ph.enter_context(nc.sbuf_tensor("qb%d" % i, [128, 4096], F32))
                  for i in range(2)]
            qb8 = [ph.enter_context(nc.sbuf_tensor("qb8%d" % i, [128, 4096],
                                                   FP8))
                   for i in range(2)]
            qb16 = [ph.enter_context(nc.sbuf_tensor("qb16%d" % i, [128, 4096],
                                                    BF16))
                    for i in range(2)]
            wgf = wg.rearrange("a b c -> (a b c)").rearrange(
                "(n p m) -> n p m", p=128, m=4096)
            wlf = wl.rearrange("a b -> (a b)").rearrange(
                "(n p m) -> n p m", p=128, m=4096)
            wg8d = wg8.rearrange("(n p) m -> n p m", p=128)
            wl8d = wl8.rearrange("(n p) m -> n p m", p=128)
            wqkvd = wqkv16.rearrange("(n p) m -> n p m", p=128)
            nqkv = D * HD2 // (128 * 4096)
            srcs16 = []
            for wi, w_src in enumerate((wq_t, wk_t, wv_t)):
                wsf = w_src.rearrange("a b -> (a b)").rearrange(
                    "(n p m) -> n p m", p=128, m=4096)
                for i in range(nqkv):
                    srcs16.append((wsf[i], wqkvd[wi * nqkv + i]))
            ntile_g = 2 * D * FF // (128 * 4096)
            ntile_l = FF * D // (128 * 4096)
            n16 = len(srcs16)
            nW = 0 if getattr(cfg, "dbg_no_phasew", False) else \
                n16 + ntile_g + ntile_l
            if getattr(cfg, "dbg_phasew_nops", False):
                nW = 0
                for e_ in (ve, sc, te, gp, sy):
                    for _ in range(1500):
                        e_.nop()
            wmode = getattr(cfg, "dbg_phasew_mode", "full")
            for i in range(nW):
                bi = i % 2
                if i < n16:
                    srcA, dstA = srcs16[i]
                    stg = qb16[bi]
                elif i < n16 + ntile_g:
                    srcA, dstA = wgf[i - n16], wg8d[i - n16]
                    stg = qb8[bi]
                else:
                    srcA = wlf[i - n16 - ntile_g]
                    dstA = wl8d[i - n16 - ntile_g]
                    stg = qb8[bi]
                if wmode in ("full", "load"):
                    if wmode == "load":
                        p.wait(sy, "d_q", max(0, (i - 1) * 16))
                    else:
                        p.wait(sy, "g_q8", max(0, (i - 1) * 16))
                    p.inc(sy.dma_start(out=qb[bi][:], in_=srcA), "d_q", 16)
                if wmode in ("full", "cast"):
                    eng = ve if i % 2 == 0 else sc
                    p.wait(eng, "d_q")
                    if eng is ve:
                        ve.tensor_copy(out=stg[:], in_=qb[bi][:])
                    else:
                        sc.copy(out=stg[:], in_=qb[bi][:])
                    p.inc(eng.drain(), "c_q8")
                if wmode in ("full", "store"):
                    sub = getattr(cfg, "dbg_store_subset", "")
                    skip = (sub == "16" and i >= n16) or \
                           (sub == "g" and not (n16 <= i < n16 + ntile_g)) or \
                           (sub == "l" and i < n16 + ntile_g)
                    if sub == "16x":
                        dstA = wqkvd[i % nqkv]
                        stg = qb16[bi]
                        skip = False
                    if not skip:
                        p.wait(gp, "c_q8")
                        p.inc(gp.dma_start(out=dstA, in_=stg[:]), "g_q8", 16)
            p.wait(gp, "g_q8")
            if getattr(cfg, "dbg_zero_qb", False):
                p.wait(ve, "g_q8")
                for t_ in qb + qb8 + qb16:
                    ve.memset(t_[:], 0.0)
                ve.drain()
            barrier()

        if "wdump" in debug_taps:
            dwd = nc.dram_tensor("dbg_wdump", [128, 1664], F32,
                                 kind="ExternalOutput").ap()
            dbg["wdump"] = dwd
            p.inc(gp.dma_start(out=dwd[:, 0:128],
                               in_=cth["ident_f32"].ap()), "g_wd", 16)
            p.inc(gp.dma_start(out=dwd[:, 128:640],
                               in_=tabs["cq_hi"][:, 0:512]), "g_wd", 16)
            p.inc(gp.dma_start(out=dwd[:, 640:1152],
                               in_=xb[0:128, 0:512]), "g_wd", 16)
            p.inc(gp.dma_start(out=dwd[:, 1152:1664],
                               in_=wo_t[0:128, 0:512]), "g_wd", 16)
            p.wait(gp, "g_wd")
            barrier()

        with ExitStack() as ph:
            sb = lambda n, sh, dt=F32: ph.enter_context(
                nc.sbuf_tensor(n, list(sh), dt))
            ps = lambda n, sh, dt=F32: ph.enter_context(
                nc.psum_tensor(n, list(sh), dt))
            wqkb = [sb("wqkb%d" % i, [128, FIN * HD2], BF16) for i in range(2)]
            x8s = sb("x8s", [128, 128], FP8)
            xf = [sb("xf%d" % i, [128, D]) for i in range(2)]
            xnf = sb("xnf", [128, D])
            xnb = sb("xnb", [128, D], BF16)
            xsq = sb("xsq", [128, D])
            smn = sb("smn", [128, 1])
            ssq = sb("ssq", [128, 1])
            mean = sb("mean", [128, 1])
            ex2 = sb("ex2", [128, 1])
            t1v = sb("t1v", [128, 1])
            var1 = sb("var1", [128, 1])
            sd1 = sb("sd1", [128, 1])
            rstd1 = sb("rstd1", [128, 1])
            nmr1 = sb("nmr1", [128, 1])
            xnTc = sb("xnTc", [128, FIN * 512], BF16)
            thi = sb("thi", [128, 512])
            tb4 = [sb("tb4_%d" % i, [128, 512]) for i in range(4)]
            qraw = sb("qraw", [128, 512], BF16)
            kraw = sb("kraw", [128, 512], BF16)
            qrot = sb("qrot", [128, 512], BF16)
            krot = sb("krot", [128, 512], BF16)
            rt1 = sb("rt1", [128, 512], BF16)
            qcol = sb("qcol", [128, 512], BF16)
            kcol = sb("kcol", [128, 512], BF16)
            vcol = sb("vcol", [128, HD2], BF16)
            ps_tb = [ps("ps_tb%d" % i, [128, 128], BF16) for i in range(2)]
            ps_tf = [ps("ps_tf%d" % i, [128, 128]) for i in range(2)]
            ps_q = ps("ps_q", [128, 512])
            ps_k = ps("ps_k", [128, 512])
            ps_v = [ps("ps_v%d" % i, [128, 512]) for i in range(HG)]

            wqkvv = wqkv16.rearrange("a b -> (a b)").rearrange(
                "(w d h) -> w d h", w=3, d=D)

            def load_w(widx, tcol):
                gi = 3 * tcol + widx
                bi = gi % 2
                p.wait(sy, "t_wq", max(0, gi - 1))
                for f in range(FIN):
                    p.inc(sy.dma_start(
                        out=wqkb[bi][:, f * HD2:(f + 1) * HD2],
                        in_=wqkvv[widx, f * 128:(f + 1) * 128, :]), "d_wq", 16)
                return bi

            def ln_tile(src_ap, dst_f32, dst_bf=None):
                bi = p.n("d_x") // 16 % 2
                p.wait(sy, "s_ln", max(0, p.n("s_ln") - 1))
                p.inc(sy.dma_start(out=xf[bi][:], in_=src_ap), "d_x", 16)
                if getattr(cfg, "dbg_no_ln", False):
                    p.wait(ve, "d_x")
                    p.wait(ve, "t_tp")
                    ve.tensor_copy(out=dst_f32[:], in_=xf[bi][:])
                    p.inc(ve.drain(), "s_ln")
                    if dst_bf is not None:
                        ve.tensor_copy(out=dst_bf[:], in_=dst_f32[:])
                        p.inc(ve.drain(), "v_xnb")
                    return
                p.wait(sc, "d_x")
                p.wait(sc, "v_sq", p.n("v_sq"))
                sc.activation(out=xsq[:], in_=xf[bi][:], func=AF.Square,
                              accum_out=ssq[:])
                p.inc(sc.drain(), "s_sq")
                p.wait(ve, "d_x")
                ve.reduce_sum(smn[:], xf[bi][:], axis=AX.X)
                ve.drain()
                p.wait(ve, "s_sq")
                p.inc(ve.tensor_scalar(out=ex2[:], in0=ssq[:],
                                       scalar1=1.0 / D, scalar2=None,
                                       op0=ALU.mult), "v_sq")
                ve.tensor_scalar(out=mean[:], in0=smn[:], scalar1=1.0 / D,
                                 scalar2=None, op0=ALU.mult)
                ve.drain()
                ve.tensor_mul(t1v[:], mean[:], mean[:])
                ve.drain()
                ve.tensor_sub(var1[:], ex2[:], t1v[:])
                p.inc(ve.drain(), "v_ln")
                p.wait(sc, "v_ln")
                sc.activation(out=sd1[:], in_=var1[:], func=AF.Sqrt,
                              bias=cs["eps_col"][:, 0:1])
                p.inc(sc.drain(), "s_sd")
                p.wait(ve, "s_sd")
                ve.reciprocal(out=rstd1[:], in_=sd1[:])
                ve.drain()
                ve.tensor_mul(nmr1[:], mean[:], rstd1[:])
                ve.drain()
                ve.tensor_scalar(out=nmr1[:], in0=nmr1[:], scalar1=-1.0,
                                 scalar2=None, op0=ALU.mult)
                p.inc(ve.drain(), "v_rs")
                p.wait(sc, "v_rs")
                p.wait(sc, "t_tp")
                sc.activation(out=dst_f32[:], in_=xf[bi][:],
                              func=AF.Identity, bias=nmr1[:],
                              scale=rstd1[:])
                p.inc(sc.drain(), "s_ln")
                if dst_bf is not None:
                    p.wait(ve, "s_ln")
                    ve.tensor_copy(out=dst_bf[:], in_=dst_f32[:])
                    p.inc(ve.drain(), "v_xnb")
                if "lntap" in debug_taps and p.n("s_ln") == 1:
                    p.wait(gp, "v_rs")
                    for j, tt_ in enumerate((smn, ssq, mean, ex2, var1, sd1,
                                             rstd1, nmr1)):
                        p.inc(gp.dma_start(out=lntap[:, j:j + 1],
                                           in_=tt_[:]), "g_lt", 16)

            def transpose128(src, dst, f32, ready):
                n = p.n("t_tp")
                pb = n % 2
                pt = ps_tf[pb] if f32 else ps_tb[pb]
                p.wait(te, ready)
                p.wait(te, "v_tp", max(0, n - 1))
                te.transpose(pt[:, 0:128], src,
                             cs["ident_f32" if f32 else "ident_bf"][:])
                p.inc(te.drain(), "t_tp")
                p.wait(ve, "t_tp")
                ve.tensor_copy(out=dst, in_=pt[:, 0:128])
                p.inc(ve.drain(), "v_tp")

            for tcol in range(TCOL):
                tsl = slice(tcol * 512, (tcol + 1) * 512)
                no_qkv = getattr(cfg, "dbg_no_qkv", False)
                no_rope = getattr(cfg, "dbg_no_rope", False)
                for i, nme in enumerate(("cq", "sq", "ck", "sk")):
                    if no_qkv or no_rope:
                        break
                    p.wait(sy, "v_rope", max(0, 2 * hpc * tcol))
                    p.wait(sy, "v_tab", max(0, 4 * tcol + i))
                    p.inc(sy.dma_start(out=thi[:],
                                       in_=tabs[nme + "_hi"][:, tsl]),
                          "d_tab", 16)
                    p.inc(sy.dma_start(out=tb4[i][:],
                                       in_=tabs[nme + "_lo"][:, tsl]),
                          "d_tab", 16)
                    p.wait(ve, "d_tab")
                    p.inc(ve.tensor_add(tb4[i][:], thi[:], tb4[i][:]), "v_tab")

                if not no_qkv:
                    p.wait(ve, "t_qk", max(0, 2 * hpc * tcol))
                    p.wait(ve, "t_v", max(0, 4 * HG * tcol))
                for t4 in range(4):
                    ti = tcol * 4 + t4
                    ln_tile(xb[ti * 128:(ti + 1) * 128, :], xnf, xnb)
                    for f in range(FIN):
                        transpose128(
                            xnb[:, f * 128:(f + 1) * 128],
                            xnTc[:, f * 512 + t4 * 128:f * 512 + (t4 + 1) * 128],
                            f32=False, ready="v_xnb")

                if tcol == 0 and "xnTc" in debug_taps:
                    dxnTc = nc.dram_tensor("dbg_xnTc", [128, FIN * 512], BF16,
                                           kind="ExternalOutput").ap()
                    dxnb = nc.dram_tensor("dbg_xnb", [128, D], BF16,
                                          kind="ExternalOutput").ap()
                    dbg["xnTc"] = dxnTc
                    dbg["xnb"] = dxnb
                    tapeng = gp if getattr(cfg, "dbg_tap_gp", True) else sy
                    p.wait(tapeng, "v_tp")
                    p.wait(tapeng, "v_xnb")
                    p.inc(tapeng.dma_start(out=dxnTc, in_=xnTc[:]),
                          "g_dbg", 16)
                    p.inc(tapeng.dma_start(out=dxnb, in_=xnb[:]),
                          "g_dbg", 16)
                    p.wait(te, "g_dbg")

                if no_qkv:
                    continue
                bq = load_w(0, tcol)
                bk = load_w(1, tcol)
                bv = load_w(2, tcol)
                dwq = p.n("d_wq")
                for widx, (bw, hbmt) in enumerate(((bq, q_hbm), (bk, k_hbm))):
                    p.wait(te, "d_wq", dwq - (2 - widx) * 16 * FIN)
                    p.wait(te, "v_tp")
                    raw = qraw if widx == 0 else kraw
                    rot = qrot if widx == 0 else krot
                    colt = qcol if widx == 0 else kcol
                    ti0, ti1 = (0, 1) if widx == 0 else (2, 3)
                    for h in range(hpc):
                        pdst = ps_q if h % 2 == 0 else ps_k
                        p.wait(te, "v_qk", max(0, p.n("t_qk") - 1))
                        lastm = None
                        for f in range(FIN):
                            lastm = te.matmul(
                                pdst[:],
                                wqkb[bw][:, f * HD2 + h * dh:
                                         f * HD2 + (h + 1) * dh],
                                xnTc[:, f * 512:(f + 1) * 512],
                                start=(f == 0), stop=(f == FIN - 1))
                        p.inc(te.drain(), "t_qk")
                        p.wait(ve, "t_qk")
                        p.wait(ve, "g_rot", p.n("g_rot"))
                        ve.tensor_copy(out=raw[:], in_=pdst[:])
                        p.inc(ve.drain(), "v_qk")
                        if no_rope:
                            p.wait(ve, "g_qk", p.n("g_qk"))
                            ve.tensor_copy(out=colt[:], in_=raw[:])
                            p.inc(ve.drain(), "v_rope")
                            p.wait(gp, "v_rope")
                            p.inc(gp.dma_start(
                                out=hbmt[h * 128:(h + 1) * 128, tsl],
                                in_=colt[:]), "g_qk", 16)
                            continue
                        p.wait(gp, "v_qk")
                        p.inc(gp.dma_start(out=rot[0:64, :],
                                           in_=raw[64:128, :]), "g_rot", 16)
                        p.inc(gp.dma_start(out=rot[64:128, :],
                                           in_=raw[0:64, :]), "g_rot", 16)
                        p.wait(ve, "v_tab")
                        p.wait(ve, "g_rot")
                        p.wait(ve, "g_qk", p.n("g_qk"))
                        ve.tensor_mul(rt1[:], raw[:], tb4[ti0][:])
                        ve.tensor_mul(colt[:], rot[:], tb4[ti1][:])
                        ve.tensor_add(colt[:], rt1[:], colt[:])
                        p.inc(ve.drain(), "v_rope")
                        p.wait(gp, "v_rope")
                        p.inc(gp.dma_start(
                            out=hbmt[h * 128:(h + 1) * 128, tsl],
                            in_=colt[:]), "g_qk", 16)
                    p.inc(te.nop(), "t_wq")

                p.wait(te, "d_wq", dwq)
                for hg in range(HG):
                    for t4 in range(4):
                        ti = tcol * 4 + t4
                        p.wait(te, "v_v", p.n("t_v"))
                        lastv = None
                        for f in range(FIN):
                            lastv = te.matmul(
                                ps_v[hg][:],
                                xnTc[:, f * 512 + t4 * 128:
                                     f * 512 + (t4 + 1) * 128],
                                wqkb[bv][:, f * HD2 + hg * 512:
                                         f * HD2 + (hg + 1) * 512],
                                start=(f == 0), stop=(f == FIN - 1))
                        p.inc(te.drain(), "t_v")
                        p.wait(ve, "t_v")
                        p.wait(ve, "g_v", p.n("g_v"))
                        ve.tensor_copy(
                            out=vcol[:, hg * 512:(hg + 1) * 512],
                            in_=ps_v[hg][:])
                        p.inc(ve.drain(), "v_v")
                        p.wait(gp, "v_v")
                        for hh in range(4):
                            u = hg * 4 + hh
                            p.inc(gp.dma_start(
                                out=v_hbm[:, (u * TT + ti) * 128:
                                          (u * TT + ti + 1) * 128],
                                in_=vcol[:, hg * 512 + hh * 128:
                                         hg * 512 + (hh + 1) * 128]),
                                "g_v", 16)
                p.inc(te.nop(), "t_wq")

            for tt in range(0 if getattr(cfg, "dbg_no_xn8", False) else OTT):
                ln_tile(xown[tt * 128:(tt + 1) * 128, :], xnf, None)
                for f in range(FIN):
                    p.wait(ve, "g_x8", p.n("g_x8"))
                    transpose128(xnf[:, f * 128:(f + 1) * 128],
                                 x8s[:, 0:128], f32=True, ready="s_ln")
                    p.wait(gp, "v_tp")
                    p.inc(gp.dma_start(
                        out=xn8[f * 128:(f + 1) * 128,
                                tt * 128:(tt + 1) * 128],
                        in_=x8s[:, 0:128]), "g_x8", 16)
            p.wait(gp, "g_qk")
            p.wait(gp, "g_v")
            p.wait(gp, "g_x8")
            barrier()

        if cfg.stop_after < 2:
            _finish(p, gp, out, cs, dbg, debug_taps, locals())
            return nc, dbg

        with ExitStack() as ph:
            sb = lambda n, sh, dt=F32: ph.enter_context(
                nc.sbuf_tensor(n, list(sh), dt))
            ps = lambda n, sh, dt=F32: ph.enter_context(
                nc.psum_tensor(n, list(sh), dt))
            qTh = [sb("qTh%d" % i, [128, S], BF16) for i in range(2)]
            kTh = [sb("kTh%d" % i, [128, S], BF16) for i in range(2)]
            Ebuf = [sb("Ebuf%d" % i, [128, S], BF16) for i in range(2)]
            lacc = [sb("lacc%d" % i, [128, max(CH, 1)]) for i in range(2)]
            ltot = sb("ltot", [128, 1])
            invl = sb("invl", [128, 1])
            ohw = [sb("ohw%d" % i, [128, CH], BF16) for i in range(8)]
            smask = [sb("smask%d" % i, [128, 512]) for i in range(2)]
            tmp_sb = sb("tmp_sb", [max(CH, 1), 512])
            ps_s = [ps("ps_s%d" % i, [128, 512]) for i in range(4)]
            ps_tmp = ps("ps_tmp", [max(CH, 1), 512])

            nEx = 0
            nCS = 0
            for u in range(hpc):
                ub = u % 2
                p.wait(sy, "t_p12", max(0, u - 1))
                p.inc(sy.dma_start(out=qTh[ub][:],
                                   in_=q_hbm[u * 128:(u + 1) * 128, :]),
                      "d_qk1", 16)
                p.inc(sy.dma_start(out=kTh[ub][:],
                                   in_=k_hbm[u * 128:(u + 1) * 128, :]),
                      "d_qk1", 16)
                p.wait(te, "d_qk1")

                def emit_colsums(pv_):
                    p.wait(te, "v_oh", pv_["voh"])
                    p.wait(te, "l_rdy", pv_["lrdy"])
                    nonlocal nCS
                    qt_, eb_ = pv_["qt"], pv_["eb"]
                    for c in range(pv_["nch"]):
                        mcs = te.matmul(ps_tmp[:], ohw[c % 8][:],
                                        Ebuf[eb_][:, c * 512:(c + 1) * 512],
                                        start=(qt_ == 0 and c == 0),
                                        stop=(qt_ == QT - 1 and
                                              c == pv_["nch"] - 1))
                        if qt_ == QT - 1 and c == pv_["nch"] - 1:
                            p.inc(te.drain(), "t_cs")
                        else:
                            p.inc(mcs, "t_cs")
                        nCS += 1

                prev = None
                for qt in range(QT):
                    nch = qt // 4 + 1
                    eb = qt % 2
                    qsl = slice(qt * 128, (qt + 1) * 128)
                    base = p.n("t_s1")
                    qend = base + nch
                    for c in range(nch):
                        bi = nEx % 4
                        p.wait(te, "p1_free", max(0, nEx - 3))
                        p.inc(te.matmul(ps_s[bi][:], qTh[ub][:, qsl],
                                        kTh[ub][:, c * 512:(c + 1) * 512],
                                        start=True, stop=True), "t_s1")
                        nEx += 1
                    p.inc(te.drain(), "t_s1", 3)
                    if prev is not None:
                        emit_colsums(prev)
                    for c in range(nch):
                        bi = (nEx - nch + c) % 4
                        vflush = min(base + c + 3, qend + 1)
                        diag = (c == nch - 1)
                        if diag:
                            p.wait(ve, "t_s1", vflush)
                            ve.tensor_add(
                                smask[eb][:], ps_s[bi][:],
                                cs["p1tri_%d" % (qt % 4)][:])
                            p.inc(ve.drain(), "v_m1")
                            p.wait(sc, "v_m1")
                            src = smask[eb][:]
                        else:
                            p.wait(sc, "t_s1", vflush)
                            src = ps_s[bi][:]
                        p.inc(sc.activation(
                            out=Ebuf[eb][:, c * 512:(c + 1) * 512], in_=src,
                            func=AF.Exp, bias=cs["negC_col"][:, 0:1],
                            accum_out=lacc[eb][:, c:c + 1]), "p1_free")
                    p.inc(sc.drain(), "l_rdy")
                    p.wait(ve, "l_rdy")
                    ve.reduce_sum(ltot[:], lacc[eb][:, 0:nch], axis=AX.X)
                    ve.drain()
                    ve.reciprocal(out=invl[:], in_=ltot[:])
                    ve.drain()
                    p.wait(ve, "t_cs", max(0, nCS))
                    for c in range(nch):
                        ve.tensor_scalar(out=ohw[c % 8][:],
                                         in0=cs["oh_%d" % c][:],
                                         scalar1=invl[:], scalar2=None,
                                         op0=ALU.mult)
                    p.inc(ve.drain(), "v_oh", nch)
                    prev = {"qt": qt, "nch": nch, "eb": eb,
                            "voh": p.n("v_oh"), "lrdy": p.n("l_rdy")}
                emit_colsums(prev)
                p.wait(ve, "t_cs")
                ve.tensor_copy(out=tmp_sb[:], in_=ps_tmp[:])
                p.inc(ve.drain(), "v_ts")
                p.wait(gp, "v_ts")
                g = u // cfg.UPG
                ul = u % cfg.UPG
                for cch in range(max(CH, 1)):
                    p.inc(gp.dma_start(
                        out=tsum[g][ul * KC + cch * 4:
                                    ul * KC + (cch + 1) * 4, :],
                        in_=tmp_sb[cch:cch + 1, :]), "g_ts", 16)
                p.wait(te, "g_ts")
                p.inc(te.nop(), "t_p12")
            barrier()

        if cfg.stop_after < 3:
            _finish(p, gp, out, cs, dbg, debug_taps, locals())
            return nc, dbg

        with ExitStack() as ph:
            sb = lambda n, sh, dt=F32: ph.enter_context(
                nc.sbuf_tensor(n, list(sh), dt))
            upg = cfg.UPG
            PBG = upg * KC
            b_ge = sb("b_ge", [128, 128])
            b_cnt = sb("b_cnt", [128, 1])
            b_c4 = sb("b_c4", [upg, 1])
            b_L = sb("b_L", [upg, 1])
            b_H = sb("b_H", [upg, 1])
            b_m = sb("b_m", [upg, 1])
            b_sel = sb("b_sel", [upg, 1])
            b_nsel = sb("b_nsel", [upg, 1])
            b_t1 = sb("b_t1", [upg, 1])
            b_T = sb("b_T", [128, 1])
            h01 = sb("h01", [128, 128])
            ps_b = ph.enter_context(nc.psum_tensor("ps_b", [128, 512], F32))

            for g in range(cfg.NBG):
                p.wait(te, "v_A", g)
                ve.memset(b_L[:], 0.0)
                p.inc(ve.memset(b_H[:], float(S + 1)), "v_bi")

                def thresh_bcast(src_col):
                    p.wait(te, "v_bi")
                    te.matmul(ps_b[0:PBG, 0:1], cs["bis_bc"][:],
                              src_col, start=True, stop=True)
                    p.inc(te.drain(), "t_bT")
                    p.wait(ve, "t_bT")
                    p.inc(ve.tensor_copy(out=b_T[0:PBG, :],
                                         in_=ps_b[0:PBG, 0:1]), "v_bT")
                    ve.drain()

                for it in range(cfg.bisect_iters):
                    ve.tensor_add(b_m[:], b_L[:], b_H[:])
                    ve.drain()
                    ve.tensor_scalar(out=b_m[:], in0=b_m[:],
                                     scalar1=0.5, scalar2=None,
                                     op0=ALU.mult)
                    p.inc(ve.drain(), "v_bi")
                    thresh_bcast(b_m[:])
                    ve.tensor_scalar(out=b_ge[0:PBG, :],
                                     in0=tsum[g][0:PBG, :],
                                     scalar1=b_T[0:PBG, :], scalar2=None,
                                     op0=ALU.is_ge)
                    ve.drain()
                    ve.reduce_sum(b_cnt[0:PBG, :], b_ge[0:PBG, :],
                                  axis=AX.X)
                    p.inc(ve.drain(), "v_bc")
                    p.wait(te, "v_bc")
                    te.matmul(ps_b[0:upg, 128:129],
                              cs["bis_sel"][0:PBG, :],
                              b_cnt[0:PBG, :], start=True, stop=True)
                    p.inc(te.drain(), "t_bc")
                    p.wait(ve, "t_bc")
                    ve.tensor_copy(out=b_c4[:], in_=ps_b[0:upg, 128:129])
                    ve.drain()
                    ve.tensor_scalar(out=b_sel[:], in0=b_c4[:],
                                     scalar1=float(cfg.heavy), scalar2=None,
                                     op0=ALU.is_ge)
                    ve.drain()
                    ve.tensor_scalar(out=b_nsel[:], in0=b_sel[:],
                                     scalar1=-1.0, scalar2=1.0,
                                     op0=ALU.mult, op1=ALU.add)
                    ve.tensor_sub(b_t1[:], b_m[:], b_L[:])
                    ve.drain()
                    ve.tensor_mul(b_t1[:], b_t1[:], b_sel[:])
                    ve.drain()
                    ve.tensor_add(b_L[:], b_L[:], b_t1[:])
                    ve.tensor_sub(b_t1[:], b_m[:], b_H[:])
                    ve.drain()
                    ve.tensor_mul(b_t1[:], b_t1[:], b_nsel[:])
                    ve.drain()
                    ve.tensor_add(b_H[:], b_H[:], b_t1[:])
                    p.inc(ve.drain(), "v_bi")
                thresh_bcast(b_L[:])
                ve.tensor_scalar(out=h01[0:PBG, :],
                                 in0=tsum[g][0:PBG, :],
                                 scalar1=b_T[0:PBG, :], scalar2=None,
                                 op0=ALU.is_ge)
                p.inc(ve.drain(), "v_h01")
                p.wait(te, "v_h01")
                te.transpose(ps_b[:, 256:256 + PBG], h01[0:PBG, :],
                             cs["ident_f32"][0:PBG, 0:PBG])
                p.inc(te.drain(), "t_h01")
                p.wait(ve, "t_h01")
                asl = slice(g * PBG, (g + 1) * PBG)
                ve.tensor_scalar(out=Aadd[:, asl],
                                 in0=ps_b[:, 256:256 + PBG],
                                 scalar1=1.0e38, scalar2=NBIG,
                                 op0=ALU.mult, op1=ALU.add)
                ve.tensor_scalar(out=Amin[:, asl],
                                 in0=ps_b[:, 256:256 + PBG],
                                 scalar1=2.0e38, scalar2=NBIG,
                                 op0=ALU.mult, op1=ALU.add)
                ve.drain()
                ve.tensor_scalar(out=Abias[:, asl], in0=Aadd[:, asl],
                                 scalar1=-CSHIFT, scalar2=None,
                                 op0=ALU.add)
                p.inc(ve.drain(), "v_A")
            barrier()

        if cfg.stop_after < 4:
            _finish(p, gp, out, cs, dbg, debug_taps, locals())
            return nc, dbg

        with ExitStack() as ph:
            sb = lambda n, sh, dt=F32: ph.enter_context(
                nc.sbuf_tensor(n, list(sh), dt))
            ps = lambda n, sh, dt=F32: ph.enter_context(
                nc.psum_tensor(n, list(sh), dt))
            qTh = [sb("qTh3%d" % i, [128, S], BF16) for i in range(2)]
            kTh = [sb("kTh3%d" % i, [128, S], BF16) for i in range(2)]
            vh = [sb("vh%d" % i, [128, TT * 128], BF16) for i in range(2)]
            E2 = [sb("E2_%d" % i, [128, 512], BF16) for i in range(4)]
            tcaus = sb("tcaus", [128, 512])
            tband = sb("tband", [128, 512])
            smask3 = sb("smask3", [128, 512])
            l2sb = sb("l2sb", [1, 512])
            invl2 = sb("invl2", [1, 512])
            il2bc = sb("il2bc", [128, 512])
            atq = sb("atq", [128, 512], BF16)
            ps_s3 = [ps("ps_s3%d" % i, [128, 512]) for i in range(4)]
            ps_pv = ps("ps_pv", [128, 512])
            ps_l2 = ps("ps_l2", [1, 512])
            ps_bc = ps("ps_bc", [128, 512])

            nK = 0
            nQG = 0
            for u in range(hpc):
                ub = u % 2
                p.wait(sy, "t_p3", max(0, (u - 1) * QG))
                p.inc(sy.dma_start(out=qTh[ub][:],
                                   in_=q_hbm[u * 128:(u + 1) * 128, :]),
                      "d_qk3", 16)
                p.inc(sy.dma_start(out=kTh[ub][:],
                                   in_=k_hbm[u * 128:(u + 1) * 128, :]),
                      "d_qk3", 16)
                p.wait(gp, "t_p3", max(0, (u - 1) * QG))
                p.inc(gp.dma_start(
                    out=vh[ub][:],
                    in_=v_hbm[:, u * TT * 128:(u + 1) * TT * 128]),
                    "g_v3", 16)
                p.wait(te, "d_qk3")
                p.wait(te, "g_v3")
                for qg in range(QG):
                    qsl = slice(qg * 512, (qg + 1) * 512)
                    kmax = (qg * 512 + 511) // 128
                    for g0 in range(0, kmax + 1, 4):
                        grp = list(range(g0, min(g0 + 4, kmax + 1)))
                        sbase = p.n("t_s3")
                        gend = sbase + len(grp)
                        for j, kc in enumerate(grp):
                            bi = (nK + j) % 4
                            p.wait(te, "p3_free", max(0, nK + j - 3))
                            p.inc(te.matmul(
                                ps_s3[bi][:],
                                kTh[ub][:, kc * 128:(kc + 1) * 128],
                                qTh[ub][:, qsl],
                                start=True, stop=True), "t_s3")
                        p.inc(te.drain(), "t_s3", 3)
                        for j, kc in enumerate(grp):
                            bi = (nK + j) % 4
                            vfl = min(sbase + j + 3, gend + 1)
                            rel = 128 * kc - 512 * qg
                            acol = Amin[:, u * KC + kc:u * KC + kc + 1]
                            abias = Abias[:, u * KC + kc:u * KC + kc + 1]
                            p.wait(sc, "t_pv", max(0, nK + j - 3))
                            if rel < -(cfg.recent + 127):
                                p.wait(sc, "t_s3", vfl)
                                sc.activation(out=E2[bi][:],
                                              in_=ps_s3[bi][:],
                                              func=AF.Exp, bias=abias)
                                sc.sem_inc(p.s("p3_free"), 1)
                                p.cnt["p3_free"] += 1
                            else:
                                p.wait(ve, "t_s3", vfl)
                                ve.tensor_add(tcaus[:], ps_s3[bi][:],
                                              cs["p3ca_%d" % rel][:])
                                p.inc(ve.tensor_add(tband[:], ps_s3[bi][:],
                                                    cs["p3ba_%d" % rel][:]),
                                      "p3_free")
                                ve.scalar_tensor_tensor(
                                    out=smask3[:], in0=tcaus[:], scalar=acol,
                                    in1=tband[:], op0=ALU.min, op1=ALU.max)
                                p.inc(ve.drain(), "v_m3")
                                p.wait(sc, "v_m3")
                                sc.activation(out=E2[bi][:], in_=smask3[:],
                                              func=AF.Exp,
                                              bias=cs["negC_col"][:, 0:1])
                        p.inc(sc.drain(), "s_e2", len(grp))
                        p.wait(te, "s_e2")
                        for j, kc in enumerate(grp):
                            bi = (nK + j) % 4
                            te.matmul(ps_pv[:],
                                      vh[ub][:, kc * 128:(kc + 1) * 128],
                                      E2[bi][:], start=(kc == 0),
                                      stop=(kc == kmax))
                            ml2 = te.matmul(ps_l2[:], cs["ones_col_bf"][:],
                                            E2[bi][:], start=(kc == 0),
                                            stop=(kc == kmax))
                            if kc == kmax:
                                p.inc(te.drain(), "t_pv")
                            else:
                                p.inc(ml2, "t_pv")
                        nK += len(grp)
                    p.wait(ve, "t_pv")
                    ve.tensor_copy(out=l2sb[:], in_=ps_l2[:])
                    ve.drain()
                    ve.reciprocal(out=invl2[:], in_=l2sb[:])
                    p.inc(ve.drain(), "v_il2")
                    p.wait(te, "v_il2")
                    te.matmul(ps_bc[:], cs["ones_row_f32"][:], invl2[:],
                              start=True, stop=True)
                    p.inc(te.drain(), "t_bc3")
                    p.wait(ve, "t_bc3")
                    ve.tensor_copy(out=il2bc[:], in_=ps_bc[:])
                    ve.drain()
                    p.wait(ve, "g_at", max(0, p.n("g_at") - 16))
                    ve.tensor_mul(atq[:], ps_pv[:], il2bc[:])
                    p.inc(ve.drain(), "v_at")
                    p.wait(gp, "v_at")
                    p.inc(gp.dma_start(
                        out=at_hbm[u * 128:(u + 1) * 128, qsl],
                        in_=atq[:]), "g_at", 16)
                    nQG += 1
                    p.wait(te, "v_at", nQG)
                    p.inc(te.nop(), "t_p3")
            p.wait(gp, "g_at")
            barrier()

        if cfg.stop_after < 5:
            _finish(p, gp, out, cs, dbg, debug_taps, locals())
            return nc, dbg

        ffnO = outer.enter_context(
            nc.sbuf_tensor("ffnO", [128, OTT * D], BF16, side="right"))
        attnF = outer.enter_context(
            nc.sbuf_tensor("attnF", [128, FIN * ownT], BF16, side="right"))
        with ExitStack() as ph:
            exb = ph.enter_context(
                nc.sbuf_tensor("exb", [128, 2 * ownT], BF16))
            s_ld = p.s("x_ld")
            s_st = p.s("x_st")
            s_af = p.s("x_af")
            with gp.register("oc") as oc, gp.register("sv") as sv, \
                    gp.register("pf") as pf:
                arm_parity(oc, cfg.cpb)
                for jj in range(cfg.cpb):
                    with gp.If_eq(oc, jj):
                        w = jj % 2
                        pbc = (jj ^ 1) * ownT
                        obc = jj * ownT
                        for u in range(hpc):
                            sl = (u % 2) * ownT
                            if u >= 2:
                                gp.wait_ge(s_st, (u - 1) * 16)
                            gp.dma_start(
                                out=exb[:, sl:sl + ownT],
                                in_=at_hbm[u * 128:(u + 1) * 128,
                                           pbc:pbc + ownT]).then_inc(s_ld, 16)
                            gp.wait_ge(s_ld, (u + 1) * 16)
                            gp.dma_start(
                                out=sh_attn[w, u * 128:(u + 1) * 128, :],
                                in_=exb[:, sl:sl + ownT]).then_inc(s_st, 16)
                        gp.wait_ge(s_st, hpc * 16)
                        gp.reg_mov(sv, MAGIC_ATTN)
                        gp.store(sh_flag[w, 0:1, 1:2], sv)
                        gp.reg_mov(pf, 0)
                        with gp.While(RuntimeValue(pf) != MAGIC_ATTN):
                            gp.reg_load(pf, sh_flag[1 - w, 0:1, 1:2])
                        for f in range(FIN):
                            uu = f % hpc
                            if (f // hpc) == w:
                                src = at_hbm[uu * 128:(uu + 1) * 128,
                                             obc:obc + ownT]
                            else:
                                src = sh_attn[1 - w, uu * 128:(uu + 1) * 128, :]
                            gp.dma_start(
                                out=attnF[:, f * ownT:(f + 1) * ownT],
                                in_=src).then_inc(s_af, 16)
                        gp.wait_ge(s_af, FIN * 16)
            barrier()

        if cfg.stop_after < 6:
            _finish(p, gp, out, cs, dbg, debug_taps, locals())
            return nc, dbg

        with ExitStack() as ph:
            sb = lambda n, sh, dt=F32: ph.enter_context(
                nc.sbuf_tensor(n, list(sh), dt))
            ps = lambda n, sh, dt=F32: ph.enter_context(
                nc.psum_tensor(n, list(sh), dt))
            w1b = [sb("w1b%d" % i, [128, FIN * 2 * 128], FP8) for i in range(2)]
            xn8t = sb("xn8t", [128, FIN * TGF], FP8)
            ffg = sb("ffg", [128, TGF], BF16)
            ff1 = sb("ff1", [128, TGF], BF16)
            gvt = sb("gvt", [128, TGF], BF16)
            gt1 = sb("gt1", [128, TGF], BF16)
            gt2 = sb("gt2", [128, TGF], BF16)
            prod = sb("prod", [128, TGF], BF16)
            act8 = sb("act8", [128, HC * TGF], FP8)
            wl8b = [sb("wl8b%d" % i, [128, D], FP8) for i in range(2)]
            ps_g = [ps("ps_g%d" % i, [128, TGF]) for i in range(2)]
            ps_o = [ps("ps_o%d" % i, [128, 512]) for i in range(NFG)]

            wg8m = wg8.rearrange("n m -> (n m)").rearrange(
                "(g b c) -> g b c", g=2, b=D)
            wl8m = wl8.rearrange("n m -> (n m)").rearrange(
                "(b c) -> b c", b=FF)
            nH = 0
            nO = 0
            for tg in range(NTG):
                tgs = slice(tg * TGF, (tg + 1) * TGF)
                p.wait(sy, "t_g1")
                for f in range(FIN):
                    p.inc(sy.dma_start(out=xn8t[:, f * TGF:(f + 1) * TGF],
                                       in_=xn8[f * 128:(f + 1) * 128, tgs]),
                          "d_x8", 16)
                for hc in range(HC):
                    bi = nH % 2
                    p.wait(sy, "t_g1", max(0, nH - 1))
                    p.wait(gp, "t_g1", max(0, nH - 1))
                    for g in range(2):
                        eng_w = sy if g == 0 else gp
                        sem_w = "d_w1" if g == 0 else "g_w1"
                        for f in range(FIN):
                            p.inc(eng_w.dma_start(
                                out=w1b[bi][:, (f * 2 + g) * 128:
                                            (f * 2 + g + 1) * 128],
                                in_=wg8m[g, f * 128:(f + 1) * 128,
                                         hc * 128:(hc + 1) * 128]),
                                sem_w, 16)
                    p.wait(te, "d_w1")
                    p.wait(te, "g_w1")
                    p.wait(te, "d_x8")
                    p.wait(te, "v_ffc", max(0, 2 * nH))
                    lg_ = None
                    for g in range(2):
                        for f in range(FIN):
                            lg_ = te.matmul(
                                ps_g[g][:],
                                w1b[bi][:, (f * 2 + g) * 128:
                                        (f * 2 + g + 1) * 128],
                                xn8t[:, f * TGF:(f + 1) * TGF],
                                start=(f == 0), stop=(f == FIN - 1))
                    p.inc(te.drain(), "t_g1")
                    p.wait(ve, "t_g1")
                    ve.tensor_copy(out=ffg[:], in_=ps_g[0][:])
                    ve.tensor_copy(out=ff1[:], in_=ps_g[1][:])
                    p.inc(ve.drain(), "v_ffc", 2)
                    ve.tensor_mul(gt1[:], ffg[:], ffg[:])
                    ve.tensor_mul(gt2[:], gt1[:], ffg[:])
                    ve.tensor_scalar(out=gt1[:], in0=gt2[:],
                                     scalar1=0.044677734375, scalar2=None,
                                     op0=ALU.mult)
                    ve.tensor_add(gt2[:], ffg[:], gt1[:])
                    ve.tensor_scalar(out=gt1[:], in0=gt2[:],
                                     scalar1=0.796875, scalar2=None,
                                     op0=ALU.mult)
                    p.inc(ve.drain(), "v_gel")
                    p.wait(sc, "v_gel")
                    sc.activation(out=gvt[:], in_=gt1[:], func=AF.Tanh)
                    p.inc(sc.drain(), "s_gelu")
                    p.wait(ve, "s_gelu")
                    ve.tensor_scalar(out=gt2[:], in0=gvt[:], scalar1=1.0,
                                     scalar2=0.5, op0=ALU.add, op1=ALU.mult)
                    ve.tensor_mul(gt1[:], ffg[:], gt2[:])
                    ve.tensor_mul(prod[:], gt1[:], ff1[:])
                    ve.tensor_copy(
                        out=act8[:, hc * TGF:(hc + 1) * TGF], in_=prod[:])
                    p.inc(ve.drain(), "v_ff")
                    nH += 1
                for t4 in range(TGF // 128):
                    ti = tg * (TGF // 128) + t4
                    p.wait(te, "v_fo", max(0, nO * NFG))
                    for hc in range(HC):
                        bi = hc % 2
                        eng_l = sy if hc % 2 == 0 else gp
                        sem_l = "d_wl" if hc % 2 == 0 else "g_wl"
                        p.wait(eng_l, "t_o", max(0, p.n("t_o") - 1))
                        p.inc(eng_l.dma_start(
                            out=wl8b[bi][:],
                            in_=wl8m[hc * 128:(hc + 1) * 128, :]), sem_l, 16)
                        p.wait(te, "d_wl")
                        p.wait(te, "g_wl")
                        p.wait(te, "v_ff")
                        lo = None
                        for fg in range(NFG):
                            lo = te.matmul(
                                ps_o[fg][:],
                                act8[:, hc * TGF + t4 * 128:
                                     hc * TGF + (t4 + 1) * 128],
                                wl8b[bi][:, fg * 512:(fg + 1) * 512],
                                start=(hc == 0), stop=(hc == HC - 1))
                        if hc == HC - 1:
                            p.inc(te.drain(), "t_o")
                        else:
                            p.inc(lo, "t_o")
                    p.wait(ve, "t_o")
                    for fg in range(NFG):
                        p.inc(ve.tensor_copy(
                            out=ffnO[:, ti * D + fg * 512:
                                     ti * D + (fg + 1) * 512],
                            in_=ps_o[fg][:]), "v_fo")
                    nO += 1
            barrier()

        if cfg.stop_after < 7:
            _finish(p, gp, out, cs, dbg, debug_taps, locals())
            return nc, dbg

        with ExitStack() as ph:
            sb = lambda n, sh, dt=F32: ph.enter_context(
                nc.sbuf_tensor(n, list(sh), dt))
            ps = lambda n, sh, dt=F32: ph.enter_context(
                nc.psum_tensor(n, list(sh), dt))
            wo_bf = sb("wo_bf", [128, FIN * D], BF16)
            wof = [sb("wof%d" % i, [128, D]) for i in range(2)]
            xr = [sb("xr%d" % i, [128, D]) for i in range(2)]
            outb = [sb("outb%d" % i, [128, D]) for i in range(2)]
            ps_w = [ps("ps_w%d" % i, [128, 512]) for i in range(NFG)]

            for f in range(FIN):
                bi = f % 2
                p.wait(sy, "v_wo", max(0, f - 1))
                p.inc(sy.dma_start(out=wof[bi][:],
                                   in_=wo_t[f * 128:(f + 1) * 128, :]),
                      "d_wo", 16)
                p.wait(ve, "d_wo")
                ve.tensor_copy(out=wo_bf[:, f * D:(f + 1) * D],
                               in_=wof[bi][:])
                p.inc(ve.drain(), "v_wo")
            for tt in range(OTT):
                bi = tt % 2
                p.wait(sy, "v_out", max(0, (tt - 1) * NFG))
                p.inc(sy.dma_start(out=xr[bi][:],
                                   in_=xown[tt * 128:(tt + 1) * 128, :]),
                      "d_xr", 16)
                p.wait(te, "v_wo")
                p.wait(te, "v_out", tt * NFG)
                lw = None
                for f in range(FIN):
                    for fg in range(NFG):
                        lw = te.matmul(
                            ps_w[fg][:],
                            attnF[:, f * ownT + tt * 128:
                                  f * ownT + (tt + 1) * 128],
                            wo_bf[:, f * D + fg * 512:f * D + (fg + 1) * 512],
                            start=(f == 0), stop=(f == FIN - 1))
                p.inc(te.drain(), "t_wo")
                p.wait(ve, "t_wo")
                p.wait(ve, "d_xr")
                p.wait(ve, "g_out", max(0, (tt - 1) * 16))
                for fg in range(NFG):
                    fsl = slice(fg * 512, (fg + 1) * 512)
                    ve.tensor_add(outb[bi][:, fsl], ps_w[fg][:],
                                  xr[bi][:, fsl])
                    ve.tensor_add(
                        outb[bi][:, fsl], outb[bi][:, fsl],
                        ffnO[:, tt * D + fg * 512:tt * D + (fg + 1) * 512])
                    p.inc(ve.drain(), "v_out")
                p.wait(gp, "v_out")
                p.inc(gp.dma_start(out=out[tt * 128:(tt + 1) * 128, :],
                                   in_=outb[bi][:]), "g_out", 16)
            _finish(p, gp, out, cs, dbg, debug_taps, locals())
    return nc, dbg


def _finish(p, gp, out, cs, dbg, debug_taps, env):
    nc = p.nc
    for name in debug_taps:
        if name in dbg:
            continue
        tile = None
        m = re.match(r"^(tsum)(\d+)$", name)
        if m:
            tile = env[m.group(1)][int(m.group(2))]
        elif name in env:
            tile = env[name]
        if tile is None:
            continue
        shp = list(tile.shape)
        d = nc.dram_tensor("dbg_" + name, shp, tile.dtype,
                           kind="ExternalOutput").ap()
        dbg[name] = d
        p.inc(gp.dma_start(out=d, in_=tile[:]), "g_out", 16)
    p.wait(gp, "g_out")



def prep_inputs(cfg, x, Wq, Wk, Wv, Wo, w_gating, w_linear):
    B, S, D = cfg.B, cfg.S, cfg.D
    hpc, dh, cpb = cfg.hpc, cfg.dh, cfg.cpb
    tabs = _rope_tables(cfg)
    xf = np.ascontiguousarray(np.asarray(x, dtype=np.float32).reshape(B * S, D))
    wo_tv = np.ascontiguousarray(np.asarray(Wo, dtype=np.float32).T)
    wgv = np.ascontiguousarray(np.asarray(w_gating, dtype=np.float32))
    wlv = np.ascontiguousarray(np.asarray(w_linear, dtype=np.float32))
    Wq = np.asarray(Wq, dtype=np.float32)
    Wk = np.asarray(Wk, dtype=np.float32)
    Wv = np.asarray(Wv, dtype=np.float32)
    in_maps = []
    for c in range(cfg.n_cores):
        b = c // cpb
        hs = hpc * (c % 2)
        m = {}
        m["xb"] = np.ascontiguousarray(xf[b * S:(b + 1) * S])
        m["xown"] = np.ascontiguousarray(xf[c * cfg.ownT:(c + 1) * cfg.ownT])
        m["wq_t"] = np.ascontiguousarray(Wq[hs * dh:(hs + hpc) * dh, :].T)
        m["wk_t"] = np.ascontiguousarray(Wk[hs * dh:(hs + hpc) * dh, :].T)
        m["wv_t"] = np.ascontiguousarray(Wv[hs * dh:(hs + hpc) * dh, :].T)
        m["wo_t"] = wo_tv
        m["wg"] = wgv
        m["wl"] = wlv
        for k, v in tabs.items():
            m[k] = v
        in_maps.append(m)
    return in_maps


def kernel(L0_attn_hidden_states, L0_attn_Wq, L0_attn_Wk, L0_attn_Wv,
           L0_attn_Wo, L0_ffn_w_gating, L0_ffn_w_linear):
    cfg = Cfg()
    in_maps = prep_inputs(cfg, L0_attn_hidden_states, L0_attn_Wq,
                          L0_attn_Wk, L0_attn_Wv, L0_attn_Wo,
                          L0_ffn_w_gating, L0_ffn_w_linear)
    nc, _ = build_program(cfg)
    res = run_bass_kernel_spmd(nc, in_maps, core_ids=list(range(cfg.n_cores)))
    shards = [r["out"] for r in res.results]
    outv = np.concatenate(shards, axis=0).reshape(cfg.B, cfg.S, cfg.D)
    return outv


# version 14
# speedup vs baseline: 1.2917x; 1.0144x over previous
import math
import re
from contextlib import ExitStack

import numpy as np
import ml_dtypes

import concourse.bass as bass
import concourse.mybir as mybir
from concourse.bass_utils import run_bass_kernel_spmd
from concourse.expressions import RuntimeValue

F32 = mybir.dt.float32
BF16 = mybir.dt.bfloat16
FP8 = mybir.dt.float8e4
I32 = mybir.dt.int32
AF = mybir.ActivationFunctionType
ALU = mybir.AluOpType
AX = mybir.AxisListType

NBIG = -1.0e38
CSHIFT = 14.0
MAGIC_SENT = 0x0BAD0001
MAGIC_ATTN = 0x51CA0001


class Cfg:
    def __init__(self, B=2, S=4096, D=2048, H=16, FF=8192, n_cores=8,
                 bisect_iters=42, theta=10000.0, stop_after=99):
        self.B, self.S, self.D, self.H, self.FF = B, S, D, H, FF
        self.n_cores = n_cores
        self.dh = 128
        self.theta = theta
        self.bisect_iters = bisect_iters
        self.stop_after = stop_after
        assert D == H * self.dh
        self.cpb = n_cores // B
        assert self.cpb % 2 == 0
        self.hpc = H // 2
        self.ownT = B * S // n_cores
        self.heavy = int(0.1 * S)
        self.recent = int(0.1 * S)
        self.FIN = D // 128
        self.KC = S // 128
        self.CH = S // 512
        self.QT = S // 128
        self.QG = S // 512
        self.HC = FF // 128
        self.TT = S // 128
        self.TCOL = S // 512
        self.OTT = self.ownT // 128
        self.TGF = min(512, self.ownT)
        self.NTG = self.ownT // self.TGF
        self.HD2 = self.hpc * self.dh
        assert self.HD2 % 512 == 0
        self.HG = self.HD2 // 512
        self.UPG = min(self.hpc, 128 // self.KC)
        self.NBG = (self.hpc + self.UPG - 1) // self.UPG
        assert self.hpc % self.UPG == 0
        assert self.CH >= 1 and S % 512 == 0
        assert self.ownT % self.TGF == 0 and D % 512 == 0


def _rope_tables(cfg):
    dh = cfg.dh
    ex = np.arange(0, dh, 2, dtype=np.float64) / dh
    inv = 1.0 / (cfg.theta ** ex)
    pos = np.arange(cfg.S, dtype=np.float64)
    fr = inv[:, None] * pos[None, :]
    cos = np.cos(fr)
    sin = np.sin(fr)
    cos_full = np.concatenate([cos, cos], 0)
    sinpm = np.concatenate([-sin, sin], 0)
    qs = 1.0 / math.sqrt(dh)
    out = {}
    for name, tab in (("cq", cos_full * qs), ("sq", sinpm * qs),
                      ("ck", cos_full), ("sk", sinpm)):
        hi = tab.astype(np.float32)
        lo = (tab - hi.astype(np.float64)).astype(np.float32)
        out[name + "_hi"] = np.ascontiguousarray(hi)
        out[name + "_lo"] = np.ascontiguousarray(lo)
    return out


def _const_tables(cfg):
    c = {}
    c["ident_bf"] = np.eye(128, dtype=ml_dtypes.bfloat16)
    c["ident_f32"] = np.eye(128, dtype=np.float32)
    c["ones_col_bf"] = np.ones((128, 1), dtype=ml_dtypes.bfloat16)
    c["ones_row_f32"] = np.ones((1, 128), dtype=np.float32)
    p = np.arange(128, dtype=np.float32).reshape(128, 1)
    j = np.arange(512, dtype=np.float32)[None, :]
    big = np.float32(NBIG)
    for r in range(4):
        c["p1tri_%d" % r] = np.ascontiguousarray(np.where(
            j < 128 * r + p + 1, np.float32(0), big).astype(ml_dtypes.bfloat16))
    c["eps_col"] = np.full((128, 1), 1e-5, dtype=np.float32)
    c["negC_col"] = np.full((128, 1), -CSHIFT, dtype=np.float32)
    for rel in range(-512, 385, 128):
        cs_ = np.clip(rel + p, 0, 512)
        be_ = np.clip(rel + p + cfg.recent + 1, 0, 512)
        c["p3ca_%d" % rel] = np.ascontiguousarray(np.where(
            j >= cs_, np.float32(0), big).astype(ml_dtypes.bfloat16))
        c["p3ba_%d" % rel] = np.ascontiguousarray(np.where(
            (j >= cs_) & (j < be_), np.float32(0), big).astype(ml_dtypes.bfloat16))
    for ci in range(cfg.CH):
        m = np.zeros((128, cfg.CH), dtype=np.float32)
        m[:, ci] = 1.0
        c["oh_%d" % ci] = m
    upg = cfg.UPG
    pbg = upg * cfg.KC
    sel = np.zeros((pbg, upg), dtype=np.float32)
    for u in range(upg):
        sel[u * cfg.KC:(u + 1) * cfg.KC, u] = 1
    c["bis_sel"] = sel
    bc = np.zeros((upg, pbg), dtype=np.float32)
    for u in range(upg):
        bc[u, u * cfg.KC:(u + 1) * cfg.KC] = 1
    c["bis_bc"] = bc
    return c


class P:
    def __init__(self, nc, ctx):
        self.nc = nc
        self.ctx = ctx
        self.sems = {}
        self.cnt = {}

    def s(self, name):
        if name not in self.sems:
            self.sems[name] = self.ctx.enter_context(self.nc.semaphore(name))
            self.cnt[name] = 0
        return self.sems[name]

    def inc(self, instr, name, k=1):
        instr.then_inc(self.s(name), k)
        self.cnt[name] += k
        return self.cnt[name]

    def wait(self, eng, name, value=None):
        v = self.cnt.get(name, 0) if value is None else value
        self.s(name)
        if v > 0:
            eng.wait_ge(self.sems[name], v)

    def n(self, name):
        self.s(name)
        return self.cnt[name]


def build_program(cfg, debug_taps=(), hbm_taps=()):
    nc = bass.Bass("TRN2", target_bir_lowering=False, debug=False,
                   enable_partition_id=False, detect_race_conditions=False)

    def internal(name):
        return "ExternalOutput" if name in hbm_taps else "Internal"
    B, S, D, FF, dh = cfg.B, cfg.S, cfg.D, cfg.FF, cfg.dh
    hpc, FIN, KC, CH, QT, QG, HC = (cfg.hpc, cfg.FIN, cfg.KC, cfg.CH,
                                    cfg.QT, cfg.QG, cfg.HC)
    ownT, TT, TCOL, OTT, TGF, NTG = (cfg.ownT, cfg.TT, cfg.TCOL, cfg.OTT,
                                     cfg.TGF, cfg.NTG)
    HD2, HG = cfg.HD2, cfg.HG
    NFG = D // 512

    xb = nc.dram_tensor("xb", [S, D], F32, kind="ExternalInput").ap()
    xown = nc.dram_tensor("xown", [ownT, D], F32, kind="ExternalInput").ap()
    wq_t = nc.dram_tensor("wq_t", [D, HD2], F32, kind="ExternalInput").ap()
    wk_t = nc.dram_tensor("wk_t", [D, HD2], F32, kind="ExternalInput").ap()
    wv_t = nc.dram_tensor("wv_t", [D, HD2], F32, kind="ExternalInput").ap()
    wo_t = nc.dram_tensor("wo_t", [D, D], F32, kind="ExternalInput").ap()
    wg = nc.dram_tensor("wg", [2, D, FF], F32, kind="ExternalInput").ap()
    wl = nc.dram_tensor("wl", [FF, D], F32, kind="ExternalInput").ap()
    tabs = {}
    for nme in ("cq_hi", "cq_lo", "sq_hi", "sq_lo",
                "ck_hi", "ck_lo", "sk_hi", "sk_lo"):
        tabs[nme] = nc.dram_tensor(nme, [128, S], F32, kind="ExternalInput").ap()
    out = nc.dram_tensor("out", [ownT, D], F32, kind="ExternalOutput").ap()

    wg8 = nc.dram_tensor("wg8", [2 * D * FF // 4096, 4096], FP8,
                         kind="Internal").ap()
    wl8 = nc.dram_tensor("wl8", [FF * D // 4096, 4096], FP8,
                         kind="Internal").ap()
    wqkv16 = nc.dram_tensor("wqkv16", [3 * D * HD2 // 4096, 4096], BF16,
                            kind="Internal").ap()
    xn8 = nc.dram_tensor("xn8", [FIN * 128, ownT], FP8,
                         kind=internal("xn8")).ap()
    q_hbm = nc.dram_tensor("q_hbm", [hpc * 128, S], BF16,
                           kind=internal("q_hbm")).ap()
    k_hbm = nc.dram_tensor("k_hbm", [hpc * 128, S], BF16,
                           kind=internal("k_hbm")).ap()
    v_hbm = nc.dram_tensor("v_hbm", [128, hpc * TT * 128], BF16,
                           kind=internal("v_hbm")).ap()
    at_hbm = nc.dram_tensor("at_hbm", [hpc * 128, S], BF16,
                            kind=internal("at_hbm")).ap()
    sh_attn = nc.dram_tensor("sh_attn", [2, hpc * 128, ownT], BF16,
                             kind="Internal", addr_space="Shared").ap()
    sh_flag = nc.dram_tensor("sh_flag", [2, 1, 32], I32,
                             kind="Internal", addr_space="Shared").ap()

    dbg = {}
    consts = _const_tables(cfg)
    cth = {k: nc.inline_tensor(np.ascontiguousarray(v), "c_" + k)
           for k, v in consts.items()}

    outer = ExitStack()
    p = P(nc, outer)
    sy, te, ve, sc, gp = nc.sync, nc.tensor, nc.vector, nc.scalar, nc.gpsimd

    def barrier():
        nc.all_engine_barrier()

    def arm_parity(reg, mod):
        for arm in gp.switch_core_id(n=128):
            try:
                lg = arm.logical % 8
            except Exception:
                lg = 0
            gp.reg_mov(reg, lg % mod)
        gp.snap(reg, min_val=0, max_val=mod - 1)

    with outer:
        cs = {}
        for k, v in consts.items():
            t = outer.enter_context(
                nc.sbuf_tensor("cs_" + k, list(v.shape),
                               mybir.dt.from_np(v.dtype)))
            p.inc(sy.dma_start(out=t[:], in_=cth[k].ap()), "d_c", 16)
            cs[k] = t

        tsum = [outer.enter_context(
            nc.sbuf_tensor("tsum%d" % g, [128, 128], F32))
            for g in range(cfg.NBG)]
        Aadd = outer.enter_context(
            nc.sbuf_tensor("Aadd", [128, hpc * KC], F32))
        Amin = outer.enter_context(
            nc.sbuf_tensor("Amin", [128, hpc * KC], F32))
        Abias = outer.enter_context(
            nc.sbuf_tensor("Abias", [128, hpc * KC], F32))
        lntap = outer.enter_context(nc.sbuf_tensor("lntap", [128, 8], F32))
        for g in range(cfg.NBG):
            ve.memset(tsum[g][:], 0.0)

        if not getattr(cfg, "dbg_no_sentinel", False):
            with gp.register("oc0") as oc0:
                arm_parity(oc0, 2)
                with gp.register("sv0") as sv0:
                    gp.reg_mov(sv0, MAGIC_SENT)
                    for s in range(2):
                        with gp.If_eq(oc0, s):
                            gp.store(sh_flag[s, 0:1, 1:2], sv0)

        with ExitStack() as ph:
            qb = [ph.enter_context(nc.sbuf_tensor("qb%d" % i, [128, 4096], F32))
                  for i in range(2)]
            qb8 = [ph.enter_context(nc.sbuf_tensor("qb8%d" % i, [128, 4096],
                                                   FP8))
                   for i in range(2)]
            qb16 = [ph.enter_context(nc.sbuf_tensor("qb16%d" % i, [128, 4096],
                                                    BF16))
                    for i in range(2)]
            wgf = wg.rearrange("a b c -> (a b c)").rearrange(
                "(n p m) -> n p m", p=128, m=4096)
            wlf = wl.rearrange("a b -> (a b)").rearrange(
                "(n p m) -> n p m", p=128, m=4096)
            wg8d = wg8.rearrange("(n p) m -> n p m", p=128)
            wl8d = wl8.rearrange("(n p) m -> n p m", p=128)
            wqkvd = wqkv16.rearrange("(n p) m -> n p m", p=128)
            nqkv = D * HD2 // (128 * 4096)
            srcs16 = []
            for wi, w_src in enumerate((wq_t, wk_t, wv_t)):
                wsf = w_src.rearrange("a b -> (a b)").rearrange(
                    "(n p m) -> n p m", p=128, m=4096)
                for i in range(nqkv):
                    srcs16.append((wsf[i], wqkvd[wi * nqkv + i]))
            ntile_g = 2 * D * FF // (128 * 4096)
            ntile_l = FF * D // (128 * 4096)
            n16 = len(srcs16)
            nW = 0 if getattr(cfg, "dbg_no_phasew", False) else \
                n16 + ntile_g + ntile_l
            if getattr(cfg, "dbg_phasew_nops", False):
                nW = 0
                for e_ in (ve, sc, te, gp, sy):
                    for _ in range(1500):
                        e_.nop()
            wmode = getattr(cfg, "dbg_phasew_mode", "full")
            for i in range(nW):
                bi = i % 2
                if i < n16:
                    srcA, dstA = srcs16[i]
                    stg = qb16[bi]
                elif i < n16 + ntile_g:
                    srcA, dstA = wgf[i - n16], wg8d[i - n16]
                    stg = qb8[bi]
                else:
                    srcA = wlf[i - n16 - ntile_g]
                    dstA = wl8d[i - n16 - ntile_g]
                    stg = qb8[bi]
                if wmode in ("full", "load"):
                    if wmode == "load":
                        p.wait(sy, "d_q", max(0, (i - 1) * 16))
                    else:
                        p.wait(sy, "g_q8", max(0, (i - 1) * 16))
                    p.inc(sy.dma_start(out=qb[bi][:], in_=srcA), "d_q", 16)
                if wmode in ("full", "cast"):
                    eng = ve if i % 2 == 0 else sc
                    p.wait(eng, "d_q")
                    if eng is ve:
                        ve.tensor_copy(out=stg[:], in_=qb[bi][:])
                    else:
                        sc.copy(out=stg[:], in_=qb[bi][:])
                    p.inc(eng.drain(), "c_q8")
                if wmode in ("full", "store"):
                    sub = getattr(cfg, "dbg_store_subset", "")
                    skip = (sub == "16" and i >= n16) or \
                           (sub == "g" and not (n16 <= i < n16 + ntile_g)) or \
                           (sub == "l" and i < n16 + ntile_g)
                    if sub == "16x":
                        dstA = wqkvd[i % nqkv]
                        stg = qb16[bi]
                        skip = False
                    if not skip:
                        p.wait(gp, "c_q8")
                        p.inc(gp.dma_start(out=dstA, in_=stg[:]), "g_q8", 16)
            p.wait(gp, "g_q8")
            if getattr(cfg, "dbg_zero_qb", False):
                p.wait(ve, "g_q8")
                for t_ in qb + qb8 + qb16:
                    ve.memset(t_[:], 0.0)
                ve.drain()
            barrier()

        if "wdump" in debug_taps:
            dwd = nc.dram_tensor("dbg_wdump", [128, 1664], F32,
                                 kind="ExternalOutput").ap()
            dbg["wdump"] = dwd
            p.inc(gp.dma_start(out=dwd[:, 0:128],
                               in_=cth["ident_f32"].ap()), "g_wd", 16)
            p.inc(gp.dma_start(out=dwd[:, 128:640],
                               in_=tabs["cq_hi"][:, 0:512]), "g_wd", 16)
            p.inc(gp.dma_start(out=dwd[:, 640:1152],
                               in_=xb[0:128, 0:512]), "g_wd", 16)
            p.inc(gp.dma_start(out=dwd[:, 1152:1664],
                               in_=wo_t[0:128, 0:512]), "g_wd", 16)
            p.wait(gp, "g_wd")
            barrier()

        with ExitStack() as ph:
            sb = lambda n, sh, dt=F32: ph.enter_context(
                nc.sbuf_tensor(n, list(sh), dt))
            ps = lambda n, sh, dt=F32: ph.enter_context(
                nc.psum_tensor(n, list(sh), dt))
            wqkb = [sb("wqkb%d" % i, [128, FIN * HD2], BF16) for i in range(3)]
            x8s = sb("x8s", [128, 128], FP8)
            xf = [sb("xf%d" % i, [128, D]) for i in range(2)]
            xnf = sb("xnf", [128, D])
            xnb = sb("xnb", [128, D], BF16)
            xsq = sb("xsq", [128, D])
            smn = sb("smn", [128, 1])
            ssq = sb("ssq", [128, 1])
            mean = sb("mean", [128, 1])
            ex2 = sb("ex2", [128, 1])
            t1v = sb("t1v", [128, 1])
            var1 = sb("var1", [128, 1])
            sd1 = sb("sd1", [128, 1])
            rstd1 = sb("rstd1", [128, 1])
            nmr1 = sb("nmr1", [128, 1])
            xnTc = sb("xnTc", [128, FIN * 512], BF16)
            thi = sb("thi", [128, 512])
            tb4 = [sb("tb4_%d" % i, [128, 512]) for i in range(4)]
            qraw = sb("qraw", [128, 512], BF16)
            kraw = sb("kraw", [128, 512], BF16)
            qrot = sb("qrot", [128, 512], BF16)
            krot = sb("krot", [128, 512], BF16)
            rt1 = sb("rt1", [128, 512], BF16)
            qcol = sb("qcol", [128, 512], BF16)
            kcol = sb("kcol", [128, 512], BF16)
            vcol = sb("vcol", [128, HD2], BF16)
            ps_tb = [ps("ps_tb%d" % i, [128, 128], BF16) for i in range(2)]
            ps_tf = [ps("ps_tf%d" % i, [128, 128]) for i in range(2)]
            ps_q = ps("ps_q", [128, 512])
            ps_k = ps("ps_k", [128, 512])
            ps_v = [ps("ps_v%d" % i, [128, 512]) for i in range(HG)]

            wqkvv = wqkv16.rearrange("a b -> (a b)").rearrange(
                "(w d h) -> w d h", w=3, d=D)

            def load_w(widx, tcol):
                for f in range(FIN):
                    p.inc(sy.dma_start(
                        out=wqkb[widx][:, f * HD2:(f + 1) * HD2],
                        in_=wqkvv[widx, f * 128:(f + 1) * 128, :]), "d_wq", 16)
                return widx

            def ln_tile(src_ap, dst_f32, dst_bf=None):
                bi = p.n("d_x") // 16 % 2
                p.wait(sy, "s_ln", max(0, p.n("s_ln") - 1))
                p.inc(sy.dma_start(out=xf[bi][:], in_=src_ap), "d_x", 16)
                if getattr(cfg, "dbg_no_ln", False):
                    p.wait(ve, "d_x")
                    p.wait(ve, "t_tp")
                    ve.tensor_copy(out=dst_f32[:], in_=xf[bi][:])
                    p.inc(ve.drain(), "s_ln")
                    if dst_bf is not None:
                        ve.tensor_copy(out=dst_bf[:], in_=dst_f32[:])
                        p.inc(ve.drain(), "v_xnb")
                    return
                p.wait(sc, "d_x")
                p.wait(sc, "v_sq", p.n("v_sq"))
                sc.activation(out=xsq[:], in_=xf[bi][:], func=AF.Square,
                              accum_out=ssq[:])
                p.inc(sc.drain(), "s_sq")
                p.wait(ve, "d_x")
                ve.reduce_sum(smn[:], xf[bi][:], axis=AX.X)
                ve.drain()
                p.wait(ve, "s_sq")
                p.inc(ve.tensor_scalar(out=ex2[:], in0=ssq[:],
                                       scalar1=1.0 / D, scalar2=None,
                                       op0=ALU.mult), "v_sq")
                ve.tensor_scalar(out=mean[:], in0=smn[:], scalar1=1.0 / D,
                                 scalar2=None, op0=ALU.mult)
                ve.drain()
                ve.tensor_mul(t1v[:], mean[:], mean[:])
                ve.drain()
                ve.tensor_sub(var1[:], ex2[:], t1v[:])
                p.inc(ve.drain(), "v_ln")
                p.wait(sc, "v_ln")
                sc.activation(out=sd1[:], in_=var1[:], func=AF.Sqrt,
                              bias=cs["eps_col"][:, 0:1])
                p.inc(sc.drain(), "s_sd")
                p.wait(ve, "s_sd")
                ve.reciprocal(out=rstd1[:], in_=sd1[:])
                ve.drain()
                ve.tensor_mul(nmr1[:], mean[:], rstd1[:])
                ve.drain()
                ve.tensor_scalar(out=nmr1[:], in0=nmr1[:], scalar1=-1.0,
                                 scalar2=None, op0=ALU.mult)
                p.inc(ve.drain(), "v_rs")
                p.wait(sc, "v_rs")
                p.wait(sc, "t_tp")
                sc.activation(out=dst_f32[:], in_=xf[bi][:],
                              func=AF.Identity, bias=nmr1[:],
                              scale=rstd1[:])
                p.inc(sc.drain(), "s_ln")
                if dst_bf is not None:
                    p.wait(ve, "s_ln")
                    ve.tensor_copy(out=dst_bf[:], in_=dst_f32[:])
                    p.inc(ve.drain(), "v_xnb")
                if "lntap" in debug_taps and p.n("s_ln") == 1:
                    p.wait(gp, "v_rs")
                    for j, tt_ in enumerate((smn, ssq, mean, ex2, var1, sd1,
                                             rstd1, nmr1)):
                        p.inc(gp.dma_start(out=lntap[:, j:j + 1],
                                           in_=tt_[:]), "g_lt", 16)

            def transpose128(src, dst, f32, ready):
                n = p.n("t_tp")
                pb = n % 2
                pt = ps_tf[pb] if f32 else ps_tb[pb]
                p.wait(te, ready)
                p.wait(te, "v_tp", max(0, n - 1))
                te.transpose(pt[:, 0:128], src,
                             cs["ident_f32" if f32 else "ident_bf"][:])
                p.inc(te.drain(), "t_tp")
                p.wait(ve, "t_tp")
                ve.tensor_copy(out=dst, in_=pt[:, 0:128])
                p.inc(ve.drain(), "v_tp")

            for tcol in range(TCOL):
                tsl = slice(tcol * 512, (tcol + 1) * 512)
                no_qkv = getattr(cfg, "dbg_no_qkv", False)
                no_rope = getattr(cfg, "dbg_no_rope", False)
                for i, nme in enumerate(("cq", "sq", "ck", "sk")):
                    if no_qkv or no_rope:
                        break
                    p.wait(sy, "v_rope", max(0, 2 * hpc * tcol))
                    p.wait(sy, "v_tab", max(0, 4 * tcol + i))
                    p.inc(sy.dma_start(out=thi[:],
                                       in_=tabs[nme + "_hi"][:, tsl]),
                          "d_tab", 16)
                    p.inc(sy.dma_start(out=tb4[i][:],
                                       in_=tabs[nme + "_lo"][:, tsl]),
                          "d_tab", 16)
                    p.wait(ve, "d_tab")
                    p.inc(ve.tensor_add(tb4[i][:], thi[:], tb4[i][:]), "v_tab")

                if not no_qkv:
                    p.wait(ve, "t_qk", max(0, 2 * hpc * tcol))
                    p.wait(ve, "t_v", max(0, 4 * HG * tcol))
                for t4 in range(4):
                    ti = tcol * 4 + t4
                    ln_tile(xb[ti * 128:(ti + 1) * 128, :], xnf, xnb)
                    for f in range(FIN):
                        transpose128(
                            xnb[:, f * 128:(f + 1) * 128],
                            xnTc[:, f * 512 + t4 * 128:f * 512 + (t4 + 1) * 128],
                            f32=False, ready="v_xnb")

                if tcol == 0 and "xnTc" in debug_taps:
                    dxnTc = nc.dram_tensor("dbg_xnTc", [128, FIN * 512], BF16,
                                           kind="ExternalOutput").ap()
                    dxnb = nc.dram_tensor("dbg_xnb", [128, D], BF16,
                                          kind="ExternalOutput").ap()
                    dbg["xnTc"] = dxnTc
                    dbg["xnb"] = dxnb
                    tapeng = gp if getattr(cfg, "dbg_tap_gp", True) else sy
                    p.wait(tapeng, "v_tp")
                    p.wait(tapeng, "v_xnb")
                    p.inc(tapeng.dma_start(out=dxnTc, in_=xnTc[:]),
                          "g_dbg", 16)
                    p.inc(tapeng.dma_start(out=dxnb, in_=xnb[:]),
                          "g_dbg", 16)
                    p.wait(te, "g_dbg")

                if no_qkv:
                    continue
                if tcol == 0:
                    bq = load_w(0, tcol)
                    bk = load_w(1, tcol)
                    bv = load_w(2, tcol)
                    dwq = p.n("d_wq")
                for widx, (bw, hbmt) in enumerate(((bq, q_hbm), (bk, k_hbm))):
                    p.wait(te, "d_wq", dwq - (2 - widx) * 16 * FIN)
                    p.wait(te, "v_tp")
                    raw = qraw if widx == 0 else kraw
                    rot = qrot if widx == 0 else krot
                    colt = qcol if widx == 0 else kcol
                    ti0, ti1 = (0, 1) if widx == 0 else (2, 3)
                    for h in range(hpc):
                        pdst = ps_q if h % 2 == 0 else ps_k
                        p.wait(te, "v_qk", max(0, p.n("t_qk") - 1))
                        lastm = None
                        for f in range(FIN):
                            lastm = te.matmul(
                                pdst[:],
                                wqkb[bw][:, f * HD2 + h * dh:
                                         f * HD2 + (h + 1) * dh],
                                xnTc[:, f * 512:(f + 1) * 512],
                                start=(f == 0), stop=(f == FIN - 1))
                        p.inc(te.drain(), "t_qk")
                        p.wait(ve, "t_qk")
                        p.wait(ve, "g_rot", p.n("g_rot"))
                        ve.tensor_copy(out=raw[:], in_=pdst[:])
                        p.inc(ve.drain(), "v_qk")
                        if no_rope:
                            p.wait(ve, "g_qk", p.n("g_qk"))
                            ve.tensor_copy(out=colt[:], in_=raw[:])
                            p.inc(ve.drain(), "v_rope")
                            p.wait(gp, "v_rope")
                            p.inc(gp.dma_start(
                                out=hbmt[h * 128:(h + 1) * 128, tsl],
                                in_=colt[:]), "g_qk", 16)
                            continue
                        p.wait(gp, "v_qk")
                        p.inc(gp.dma_start(out=rot[0:64, :],
                                           in_=raw[64:128, :]), "g_rot", 16)
                        p.inc(gp.dma_start(out=rot[64:128, :],
                                           in_=raw[0:64, :]), "g_rot", 16)
                        p.wait(ve, "v_tab")
                        p.wait(ve, "g_rot")
                        p.wait(ve, "g_qk", p.n("g_qk"))
                        ve.tensor_mul(rt1[:], raw[:], tb4[ti0][:])
                        ve.tensor_mul(colt[:], rot[:], tb4[ti1][:])
                        ve.tensor_add(colt[:], rt1[:], colt[:])
                        p.inc(ve.drain(), "v_rope")
                        p.wait(gp, "v_rope")
                        p.inc(gp.dma_start(
                            out=hbmt[h * 128:(h + 1) * 128, tsl],
                            in_=colt[:]), "g_qk", 16)
                    p.inc(te.nop(), "t_wq")

                p.wait(te, "d_wq", dwq)
                for hg in range(HG):
                    for t4 in range(4):
                        ti = tcol * 4 + t4
                        p.wait(te, "v_v", p.n("t_v"))
                        lastv = None
                        for f in range(FIN):
                            lastv = te.matmul(
                                ps_v[hg][:],
                                xnTc[:, f * 512 + t4 * 128:
                                     f * 512 + (t4 + 1) * 128],
                                wqkb[bv][:, f * HD2 + hg * 512:
                                         f * HD2 + (hg + 1) * 512],
                                start=(f == 0), stop=(f == FIN - 1))
                        p.inc(te.drain(), "t_v")
                        p.wait(ve, "t_v")
                        p.wait(ve, "g_v", p.n("g_v"))
                        ve.tensor_copy(
                            out=vcol[:, hg * 512:(hg + 1) * 512],
                            in_=ps_v[hg][:])
                        p.inc(ve.drain(), "v_v")
                        p.wait(gp, "v_v")
                        for hh in range(4):
                            u = hg * 4 + hh
                            p.inc(gp.dma_start(
                                out=v_hbm[:, (u * TT + ti) * 128:
                                          (u * TT + ti + 1) * 128],
                                in_=vcol[:, hg * 512 + hh * 128:
                                         hg * 512 + (hh + 1) * 128]),
                                "g_v", 16)
                p.inc(te.nop(), "t_wq")

            for tt in range(0 if getattr(cfg, "dbg_no_xn8", False) else OTT):
                ln_tile(xown[tt * 128:(tt + 1) * 128, :], xnf, None)
                for f in range(FIN):
                    p.wait(ve, "g_x8", p.n("g_x8"))
                    transpose128(xnf[:, f * 128:(f + 1) * 128],
                                 x8s[:, 0:128], f32=True, ready="s_ln")
                    p.wait(gp, "v_tp")
                    p.inc(gp.dma_start(
                        out=xn8[f * 128:(f + 1) * 128,
                                tt * 128:(tt + 1) * 128],
                        in_=x8s[:, 0:128]), "g_x8", 16)
            p.wait(gp, "g_qk")
            p.wait(gp, "g_v")
            p.wait(gp, "g_x8")
            barrier()

        if cfg.stop_after < 2:
            _finish(p, gp, out, cs, dbg, debug_taps, locals())
            return nc, dbg

        with ExitStack() as ph:
            sb = lambda n, sh, dt=F32: ph.enter_context(
                nc.sbuf_tensor(n, list(sh), dt))
            ps = lambda n, sh, dt=F32: ph.enter_context(
                nc.psum_tensor(n, list(sh), dt))
            qTh = [sb("qTh%d" % i, [128, S], BF16) for i in range(2)]
            kTh = [sb("kTh%d" % i, [128, S], BF16) for i in range(2)]
            Ebuf = [sb("Ebuf%d" % i, [128, S], BF16) for i in range(2)]
            lacc = [sb("lacc%d" % i, [128, max(CH, 1)]) for i in range(2)]
            ltot = sb("ltot", [128, 1])
            invl = sb("invl", [128, 1])
            ohw = [sb("ohw%d" % i, [128, CH], BF16) for i in range(8)]
            smask = [sb("smask%d" % i, [128, 512]) for i in range(2)]
            tmp_sb = sb("tmp_sb", [max(CH, 1), 512])
            ps_s = [ps("ps_s%d" % i, [128, 512]) for i in range(4)]
            ps_tmp = ps("ps_tmp", [max(CH, 1), 512])

            nEx = 0
            nCS = 0
            for u in range(hpc):
                ub = u % 2
                p.wait(sy, "t_p12", max(0, u - 1))
                p.inc(sy.dma_start(out=qTh[ub][:],
                                   in_=q_hbm[u * 128:(u + 1) * 128, :]),
                      "d_qk1", 16)
                p.inc(sy.dma_start(out=kTh[ub][:],
                                   in_=k_hbm[u * 128:(u + 1) * 128, :]),
                      "d_qk1", 16)
                p.wait(te, "d_qk1")

                def emit_colsums(pv_):
                    p.wait(te, "v_oh", pv_["voh"])
                    p.wait(te, "l_rdy", pv_["lrdy"])
                    nonlocal nCS
                    qt_, eb_ = pv_["qt"], pv_["eb"]
                    for c in range(pv_["nch"]):
                        mcs = te.matmul(ps_tmp[:], ohw[c % 8][:],
                                        Ebuf[eb_][:, c * 512:(c + 1) * 512],
                                        start=(qt_ == 0 and c == 0),
                                        stop=(qt_ == QT - 1 and
                                              c == pv_["nch"] - 1))
                        if qt_ == QT - 1 and c == pv_["nch"] - 1:
                            p.inc(te.drain(), "t_cs")
                        else:
                            p.inc(mcs, "t_cs")
                        nCS += 1

                prev = None
                for qt in range(QT):
                    nch = qt // 4 + 1
                    eb = qt % 2
                    qsl = slice(qt * 128, (qt + 1) * 128)
                    base = p.n("t_s1")
                    qend = base + nch
                    for c in range(nch):
                        bi = nEx % 4
                        p.wait(te, "p1_free", max(0, nEx - 3))
                        p.inc(te.matmul(ps_s[bi][:], qTh[ub][:, qsl],
                                        kTh[ub][:, c * 512:(c + 1) * 512],
                                        start=True, stop=True), "t_s1")
                        nEx += 1
                    p.inc(te.drain(), "t_s1", 3)
                    if prev is not None:
                        emit_colsums(prev)
                    for c in range(nch):
                        bi = (nEx - nch + c) % 4
                        vflush = min(base + c + 3, qend + 1)
                        diag = (c == nch - 1)
                        if diag:
                            p.wait(ve, "t_s1", vflush)
                            ve.tensor_add(
                                smask[eb][:], ps_s[bi][:],
                                cs["p1tri_%d" % (qt % 4)][:])
                            p.inc(ve.drain(), "v_m1")
                            p.wait(sc, "v_m1")
                            src = smask[eb][:]
                        else:
                            p.wait(sc, "t_s1", vflush)
                            src = ps_s[bi][:]
                        p.inc(sc.activation(
                            out=Ebuf[eb][:, c * 512:(c + 1) * 512], in_=src,
                            func=AF.Exp, bias=cs["negC_col"][:, 0:1],
                            accum_out=lacc[eb][:, c:c + 1]), "p1_free")
                    p.inc(sc.drain(), "l_rdy")
                    p.wait(ve, "l_rdy")
                    ve.reduce_sum(ltot[:], lacc[eb][:, 0:nch], axis=AX.X)
                    ve.drain()
                    ve.reciprocal(out=invl[:], in_=ltot[:])
                    ve.drain()
                    p.wait(ve, "t_cs", max(0, nCS))
                    for c in range(nch):
                        ve.tensor_scalar(out=ohw[c % 8][:],
                                         in0=cs["oh_%d" % c][:],
                                         scalar1=invl[:], scalar2=None,
                                         op0=ALU.mult)
                    p.inc(ve.drain(), "v_oh", nch)
                    prev = {"qt": qt, "nch": nch, "eb": eb,
                            "voh": p.n("v_oh"), "lrdy": p.n("l_rdy")}
                emit_colsums(prev)
                p.wait(ve, "t_cs")
                ve.tensor_copy(out=tmp_sb[:], in_=ps_tmp[:])
                p.inc(ve.drain(), "v_ts")
                p.wait(gp, "v_ts")
                g = u // cfg.UPG
                ul = u % cfg.UPG
                for cch in range(max(CH, 1)):
                    p.inc(gp.dma_start(
                        out=tsum[g][ul * KC + cch * 4:
                                    ul * KC + (cch + 1) * 4, :],
                        in_=tmp_sb[cch:cch + 1, :]), "g_ts", 16)
                p.wait(te, "g_ts")
                p.inc(te.nop(), "t_p12")
            barrier()

        if cfg.stop_after < 3:
            _finish(p, gp, out, cs, dbg, debug_taps, locals())
            return nc, dbg

        with ExitStack() as ph:
            sb = lambda n, sh, dt=F32: ph.enter_context(
                nc.sbuf_tensor(n, list(sh), dt))
            upg = cfg.UPG
            PBG = upg * KC
            b_ge = sb("b_ge", [128, 128])
            b_cnt = sb("b_cnt", [128, 1])
            b_c4 = sb("b_c4", [upg, 1])
            b_L = sb("b_L", [upg, 1])
            b_H = sb("b_H", [upg, 1])
            b_m = sb("b_m", [upg, 1])
            b_sel = sb("b_sel", [upg, 1])
            b_nsel = sb("b_nsel", [upg, 1])
            b_t1 = sb("b_t1", [upg, 1])
            b_T = sb("b_T", [128, 1])
            h01 = sb("h01", [128, 128])
            ps_b = ph.enter_context(nc.psum_tensor("ps_b", [128, 512], F32))

            for g in range(cfg.NBG):
                p.wait(te, "v_A", g)
                ve.memset(b_L[:], 0.0)
                p.inc(ve.memset(b_H[:], float(S + 1)), "v_bi")

                def thresh_bcast(src_col):
                    p.wait(te, "v_bi")
                    te.matmul(ps_b[0:PBG, 0:1], cs["bis_bc"][:],
                              src_col, start=True, stop=True)
                    p.inc(te.drain(), "t_bT")
                    p.wait(ve, "t_bT")
                    p.inc(ve.tensor_copy(out=b_T[0:PBG, :],
                                         in_=ps_b[0:PBG, 0:1]), "v_bT")
                    ve.drain()

                for it in range(cfg.bisect_iters):
                    ve.tensor_add(b_m[:], b_L[:], b_H[:])
                    ve.drain()
                    ve.tensor_scalar(out=b_m[:], in0=b_m[:],
                                     scalar1=0.5, scalar2=None,
                                     op0=ALU.mult)
                    p.inc(ve.drain(), "v_bi")
                    thresh_bcast(b_m[:])
                    ve.tensor_scalar(out=b_ge[0:PBG, :],
                                     in0=tsum[g][0:PBG, :],
                                     scalar1=b_T[0:PBG, :], scalar2=None,
                                     op0=ALU.is_ge)
                    ve.drain()
                    ve.reduce_sum(b_cnt[0:PBG, :], b_ge[0:PBG, :],
                                  axis=AX.X)
                    p.inc(ve.drain(), "v_bc")
                    p.wait(te, "v_bc")
                    te.matmul(ps_b[0:upg, 128:129],
                              cs["bis_sel"][0:PBG, :],
                              b_cnt[0:PBG, :], start=True, stop=True)
                    p.inc(te.drain(), "t_bc")
                    p.wait(ve, "t_bc")
                    ve.tensor_copy(out=b_c4[:], in_=ps_b[0:upg, 128:129])
                    ve.drain()
                    ve.tensor_scalar(out=b_sel[:], in0=b_c4[:],
                                     scalar1=float(cfg.heavy), scalar2=None,
                                     op0=ALU.is_ge)
                    ve.drain()
                    ve.tensor_scalar(out=b_nsel[:], in0=b_sel[:],
                                     scalar1=-1.0, scalar2=1.0,
                                     op0=ALU.mult, op1=ALU.add)
                    ve.tensor_sub(b_t1[:], b_m[:], b_L[:])
                    ve.drain()
                    ve.tensor_mul(b_t1[:], b_t1[:], b_sel[:])
                    ve.drain()
                    ve.tensor_add(b_L[:], b_L[:], b_t1[:])
                    ve.tensor_sub(b_t1[:], b_m[:], b_H[:])
                    ve.drain()
                    ve.tensor_mul(b_t1[:], b_t1[:], b_nsel[:])
                    ve.drain()
                    ve.tensor_add(b_H[:], b_H[:], b_t1[:])
                    p.inc(ve.drain(), "v_bi")
                thresh_bcast(b_L[:])
                ve.tensor_scalar(out=h01[0:PBG, :],
                                 in0=tsum[g][0:PBG, :],
                                 scalar1=b_T[0:PBG, :], scalar2=None,
                                 op0=ALU.is_ge)
                p.inc(ve.drain(), "v_h01")
                p.wait(te, "v_h01")
                te.transpose(ps_b[:, 256:256 + PBG], h01[0:PBG, :],
                             cs["ident_f32"][0:PBG, 0:PBG])
                p.inc(te.drain(), "t_h01")
                p.wait(ve, "t_h01")
                asl = slice(g * PBG, (g + 1) * PBG)
                ve.tensor_scalar(out=Aadd[:, asl],
                                 in0=ps_b[:, 256:256 + PBG],
                                 scalar1=1.0e38, scalar2=NBIG,
                                 op0=ALU.mult, op1=ALU.add)
                ve.tensor_scalar(out=Amin[:, asl],
                                 in0=ps_b[:, 256:256 + PBG],
                                 scalar1=2.0e38, scalar2=NBIG,
                                 op0=ALU.mult, op1=ALU.add)
                ve.drain()
                ve.tensor_scalar(out=Abias[:, asl], in0=Aadd[:, asl],
                                 scalar1=-CSHIFT, scalar2=None,
                                 op0=ALU.add)
                p.inc(ve.drain(), "v_A")
            barrier()

        if cfg.stop_after < 4:
            _finish(p, gp, out, cs, dbg, debug_taps, locals())
            return nc, dbg

        with ExitStack() as ph:
            sb = lambda n, sh, dt=F32: ph.enter_context(
                nc.sbuf_tensor(n, list(sh), dt))
            ps = lambda n, sh, dt=F32: ph.enter_context(
                nc.psum_tensor(n, list(sh), dt))
            qTh = [sb("qTh3%d" % i, [128, S], BF16) for i in range(2)]
            kTh = [sb("kTh3%d" % i, [128, S], BF16) for i in range(2)]
            vh = [sb("vh%d" % i, [128, TT * 128], BF16) for i in range(2)]
            E2 = [sb("E2_%d" % i, [128, 512], BF16) for i in range(4)]
            tcaus = sb("tcaus", [128, 512])
            tband = sb("tband", [128, 512])
            smask3 = sb("smask3", [128, 512])
            l2sb = sb("l2sb", [1, 512])
            invl2 = sb("invl2", [1, 512])
            il2bc = sb("il2bc", [128, 512])
            atq = sb("atq", [128, 512], BF16)
            ps_s3 = [ps("ps_s3%d" % i, [128, 512]) for i in range(4)]
            ps_pv = ps("ps_pv", [128, 512])
            ps_l2 = ps("ps_l2", [1, 512])
            ps_bc = ps("ps_bc", [128, 512])

            nK = 0
            nQG = 0
            for u in range(hpc):
                ub = u % 2
                p.wait(sy, "t_p3", max(0, (u - 1) * QG))
                p.inc(sy.dma_start(out=qTh[ub][:],
                                   in_=q_hbm[u * 128:(u + 1) * 128, :]),
                      "d_qk3", 16)
                p.inc(sy.dma_start(out=kTh[ub][:],
                                   in_=k_hbm[u * 128:(u + 1) * 128, :]),
                      "d_qk3", 16)
                p.wait(gp, "t_p3", max(0, (u - 1) * QG))
                p.inc(gp.dma_start(
                    out=vh[ub][:],
                    in_=v_hbm[:, u * TT * 128:(u + 1) * TT * 128]),
                    "g_v3", 16)
                p.wait(te, "d_qk3")
                p.wait(te, "g_v3")
                for qg in range(QG):
                    qsl = slice(qg * 512, (qg + 1) * 512)
                    kmax = (qg * 512 + 511) // 128
                    for g0 in range(0, kmax + 1, 4):
                        grp = list(range(g0, min(g0 + 4, kmax + 1)))
                        sbase = p.n("t_s3")
                        gend = sbase + len(grp)
                        for j, kc in enumerate(grp):
                            bi = (nK + j) % 4
                            p.wait(te, "p3_free", max(0, nK + j - 3))
                            p.inc(te.matmul(
                                ps_s3[bi][:],
                                kTh[ub][:, kc * 128:(kc + 1) * 128],
                                qTh[ub][:, qsl],
                                start=True, stop=True), "t_s3")
                        p.inc(te.drain(), "t_s3", 3)
                        for j, kc in enumerate(grp):
                            bi = (nK + j) % 4
                            vfl = min(sbase + j + 3, gend + 1)
                            rel = 128 * kc - 512 * qg
                            acol = Amin[:, u * KC + kc:u * KC + kc + 1]
                            abias = Abias[:, u * KC + kc:u * KC + kc + 1]
                            p.wait(sc, "t_pv", max(0, nK + j - 3))
                            if rel < -(cfg.recent + 127):
                                p.wait(sc, "t_s3", vfl)
                                sc.activation(out=E2[bi][:],
                                              in_=ps_s3[bi][:],
                                              func=AF.Exp, bias=abias)
                                sc.sem_inc(p.s("p3_free"), 1)
                                p.cnt["p3_free"] += 1
                            else:
                                p.wait(ve, "t_s3", vfl)
                                ve.tensor_add(tcaus[:], ps_s3[bi][:],
                                              cs["p3ca_%d" % rel][:])
                                p.inc(ve.tensor_add(tband[:], ps_s3[bi][:],
                                                    cs["p3ba_%d" % rel][:]),
                                      "p3_free")
                                ve.scalar_tensor_tensor(
                                    out=smask3[:], in0=tcaus[:], scalar=acol,
                                    in1=tband[:], op0=ALU.min, op1=ALU.max)
                                p.inc(ve.drain(), "v_m3")
                                p.wait(sc, "v_m3")
                                sc.activation(out=E2[bi][:], in_=smask3[:],
                                              func=AF.Exp,
                                              bias=cs["negC_col"][:, 0:1])
                        p.inc(sc.drain(), "s_e2", len(grp))
                        p.wait(te, "s_e2")
                        for j, kc in enumerate(grp):
                            bi = (nK + j) % 4
                            te.matmul(ps_pv[:],
                                      vh[ub][:, kc * 128:(kc + 1) * 128],
                                      E2[bi][:], start=(kc == 0),
                                      stop=(kc == kmax))
                            ml2 = te.matmul(ps_l2[:], cs["ones_col_bf"][:],
                                            E2[bi][:], start=(kc == 0),
                                            stop=(kc == kmax))
                            if kc == kmax:
                                p.inc(te.drain(), "t_pv")
                            else:
                                p.inc(ml2, "t_pv")
                        nK += len(grp)
                    p.wait(ve, "t_pv")
                    ve.tensor_copy(out=l2sb[:], in_=ps_l2[:])
                    ve.drain()
                    ve.reciprocal(out=invl2[:], in_=l2sb[:])
                    p.inc(ve.drain(), "v_il2")
                    p.wait(te, "v_il2")
                    te.matmul(ps_bc[:], cs["ones_row_f32"][:], invl2[:],
                              start=True, stop=True)
                    p.inc(te.drain(), "t_bc3")
                    p.wait(ve, "t_bc3")
                    ve.tensor_copy(out=il2bc[:], in_=ps_bc[:])
                    ve.drain()
                    p.wait(ve, "g_at", max(0, p.n("g_at") - 16))
                    ve.tensor_mul(atq[:], ps_pv[:], il2bc[:])
                    p.inc(ve.drain(), "v_at")
                    p.wait(gp, "v_at")
                    p.inc(gp.dma_start(
                        out=at_hbm[u * 128:(u + 1) * 128, qsl],
                        in_=atq[:]), "g_at", 16)
                    nQG += 1
                    p.wait(te, "v_at", nQG)
                    p.inc(te.nop(), "t_p3")
            p.wait(gp, "g_at")
            barrier()

        if cfg.stop_after < 5:
            _finish(p, gp, out, cs, dbg, debug_taps, locals())
            return nc, dbg

        ffnO = outer.enter_context(
            nc.sbuf_tensor("ffnO", [128, OTT * D], BF16, side="right"))
        attnF = outer.enter_context(
            nc.sbuf_tensor("attnF", [128, FIN * ownT], BF16, side="right"))
        with ExitStack() as ph:
            exb = ph.enter_context(
                nc.sbuf_tensor("exb", [128, 2 * ownT], BF16))
            s_ld = p.s("x_ld")
            s_st = p.s("x_st")
            s_af = p.s("x_af")
            with gp.register("oc") as oc, gp.register("sv") as sv, \
                    gp.register("pf") as pf:
                arm_parity(oc, cfg.cpb)
                for jj in range(cfg.cpb):
                    with gp.If_eq(oc, jj):
                        w = jj % 2
                        pbc = (jj ^ 1) * ownT
                        obc = jj * ownT
                        for u in range(hpc):
                            sl = (u % 2) * ownT
                            if u >= 2:
                                gp.wait_ge(s_st, (u - 1) * 16)
                            gp.dma_start(
                                out=exb[:, sl:sl + ownT],
                                in_=at_hbm[u * 128:(u + 1) * 128,
                                           pbc:pbc + ownT]).then_inc(s_ld, 16)
                            gp.wait_ge(s_ld, (u + 1) * 16)
                            gp.dma_start(
                                out=sh_attn[w, u * 128:(u + 1) * 128, :],
                                in_=exb[:, sl:sl + ownT]).then_inc(s_st, 16)
                        gp.wait_ge(s_st, hpc * 16)
                        gp.reg_mov(sv, MAGIC_ATTN)
                        gp.store(sh_flag[w, 0:1, 1:2], sv)
                        gp.reg_mov(pf, 0)
                        with gp.While(RuntimeValue(pf) != MAGIC_ATTN):
                            gp.reg_load(pf, sh_flag[1 - w, 0:1, 1:2])
                        for f in range(FIN):
                            uu = f % hpc
                            if (f // hpc) == w:
                                src = at_hbm[uu * 128:(uu + 1) * 128,
                                             obc:obc + ownT]
                            else:
                                src = sh_attn[1 - w, uu * 128:(uu + 1) * 128, :]
                            gp.dma_start(
                                out=attnF[:, f * ownT:(f + 1) * ownT],
                                in_=src).then_inc(s_af, 16)
                        gp.wait_ge(s_af, FIN * 16)
            barrier()

        if cfg.stop_after < 6:
            _finish(p, gp, out, cs, dbg, debug_taps, locals())
            return nc, dbg

        with ExitStack() as ph:
            sb = lambda n, sh, dt=F32: ph.enter_context(
                nc.sbuf_tensor(n, list(sh), dt))
            ps = lambda n, sh, dt=F32: ph.enter_context(
                nc.psum_tensor(n, list(sh), dt))
            w1b = [sb("w1b%d" % i, [128, FIN * 2 * 128], FP8) for i in range(2)]
            xn8t = sb("xn8t", [128, FIN * TGF], FP8)
            ffg = sb("ffg", [128, TGF], BF16)
            ff1 = sb("ff1", [128, TGF], BF16)
            gvt = sb("gvt", [128, TGF], BF16)
            gt1 = sb("gt1", [128, TGF], BF16)
            gt2 = sb("gt2", [128, TGF], BF16)
            prod = sb("prod", [128, TGF], BF16)
            act8 = sb("act8", [128, HC * TGF], FP8)
            wl8b = [sb("wl8b%d" % i, [128, D], FP8) for i in range(2)]
            ps_g = [ps("ps_g%d" % i, [128, TGF]) for i in range(2)]
            ps_o = [ps("ps_o%d" % i, [128, 512]) for i in range(NFG)]

            wg8m = wg8.rearrange("n m -> (n m)").rearrange(
                "(g b c) -> g b c", g=2, b=D)
            wl8m = wl8.rearrange("n m -> (n m)").rearrange(
                "(b c) -> b c", b=FF)
            nH = 0
            nO = 0
            for tg in range(NTG):
                tgs = slice(tg * TGF, (tg + 1) * TGF)
                p.wait(sy, "t_g1")
                for f in range(FIN):
                    p.inc(sy.dma_start(out=xn8t[:, f * TGF:(f + 1) * TGF],
                                       in_=xn8[f * 128:(f + 1) * 128, tgs]),
                          "d_x8", 16)
                for hc in range(HC):
                    bi = nH % 2
                    p.wait(sy, "t_g1", max(0, nH - 1))
                    p.wait(gp, "t_g1", max(0, nH - 1))
                    for g in range(2):
                        eng_w = sy if g == 0 else gp
                        sem_w = "d_w1" if g == 0 else "g_w1"
                        for f in range(FIN):
                            p.inc(eng_w.dma_start(
                                out=w1b[bi][:, (f * 2 + g) * 128:
                                            (f * 2 + g + 1) * 128],
                                in_=wg8m[g, f * 128:(f + 1) * 128,
                                         hc * 128:(hc + 1) * 128]),
                                sem_w, 16)
                    p.wait(te, "d_w1")
                    p.wait(te, "g_w1")
                    p.wait(te, "d_x8")
                    p.wait(te, "v_ffc", max(0, 2 * nH))
                    lg_ = None
                    for g in range(2):
                        for f in range(FIN):
                            lg_ = te.matmul(
                                ps_g[g][:],
                                w1b[bi][:, (f * 2 + g) * 128:
                                        (f * 2 + g + 1) * 128],
                                xn8t[:, f * TGF:(f + 1) * TGF],
                                start=(f == 0), stop=(f == FIN - 1))
                    p.inc(te.drain(), "t_g1")
                    p.wait(ve, "t_g1")
                    ve.tensor_copy(out=ffg[:], in_=ps_g[0][:])
                    ve.tensor_copy(out=ff1[:], in_=ps_g[1][:])
                    p.inc(ve.drain(), "v_ffc", 2)
                    ve.tensor_mul(gt1[:], ffg[:], ffg[:])
                    ve.tensor_mul(gt2[:], gt1[:], ffg[:])
                    ve.tensor_scalar(out=gt1[:], in0=gt2[:],
                                     scalar1=0.044677734375, scalar2=None,
                                     op0=ALU.mult)
                    ve.tensor_add(gt2[:], ffg[:], gt1[:])
                    ve.tensor_scalar(out=gt1[:], in0=gt2[:],
                                     scalar1=0.796875, scalar2=None,
                                     op0=ALU.mult)
                    p.inc(ve.drain(), "v_gel")
                    p.wait(sc, "v_gel")
                    sc.activation(out=gvt[:], in_=gt1[:], func=AF.Tanh)
                    p.inc(sc.drain(), "s_gelu")
                    p.wait(ve, "s_gelu")
                    ve.tensor_scalar(out=gt2[:], in0=gvt[:], scalar1=1.0,
                                     scalar2=0.5, op0=ALU.add, op1=ALU.mult)
                    ve.tensor_mul(gt1[:], ffg[:], gt2[:])
                    ve.tensor_mul(prod[:], gt1[:], ff1[:])
                    ve.tensor_copy(
                        out=act8[:, hc * TGF:(hc + 1) * TGF], in_=prod[:])
                    p.inc(ve.drain(), "v_ff")
                    nH += 1
                for t4 in range(TGF // 128):
                    ti = tg * (TGF // 128) + t4
                    p.wait(te, "v_fo", max(0, nO * NFG))
                    for hc in range(HC):
                        bi = hc % 2
                        p.wait(sy, "t_o", max(0, p.n("t_o") - 1))
                        p.inc(sy.dma_start(
                            out=wl8b[bi][:],
                            in_=wl8m[hc * 128:(hc + 1) * 128, :]), "d_wl", 16)
                        p.wait(te, "d_wl")
                        p.wait(te, "v_ff")
                        lo = None
                        for fg in range(NFG):
                            lo = te.matmul(
                                ps_o[fg][:],
                                act8[:, hc * TGF + t4 * 128:
                                     hc * TGF + (t4 + 1) * 128],
                                wl8b[bi][:, fg * 512:(fg + 1) * 512],
                                start=(hc == 0), stop=(hc == HC - 1))
                        if hc == HC - 1:
                            p.inc(te.drain(), "t_o")
                        else:
                            p.inc(lo, "t_o")
                    p.wait(ve, "t_o")
                    for fg in range(NFG):
                        p.inc(ve.tensor_copy(
                            out=ffnO[:, ti * D + fg * 512:
                                     ti * D + (fg + 1) * 512],
                            in_=ps_o[fg][:]), "v_fo")
                    nO += 1
            barrier()

        if cfg.stop_after < 7:
            _finish(p, gp, out, cs, dbg, debug_taps, locals())
            return nc, dbg

        with ExitStack() as ph:
            sb = lambda n, sh, dt=F32: ph.enter_context(
                nc.sbuf_tensor(n, list(sh), dt))
            ps = lambda n, sh, dt=F32: ph.enter_context(
                nc.psum_tensor(n, list(sh), dt))
            wo_bf = sb("wo_bf", [128, FIN * D], BF16)
            wof = [sb("wof%d" % i, [128, D]) for i in range(2)]
            xr = [sb("xr%d" % i, [128, D]) for i in range(2)]
            outb = [sb("outb%d" % i, [128, D]) for i in range(2)]
            ps_w = [ps("ps_w%d" % i, [128, 512]) for i in range(NFG)]

            for f in range(FIN):
                bi = f % 2
                p.wait(sy, "v_wo", max(0, f - 1))
                p.inc(sy.dma_start(out=wof[bi][:],
                                   in_=wo_t[f * 128:(f + 1) * 128, :]),
                      "d_wo", 16)
                p.wait(ve, "d_wo")
                ve.tensor_copy(out=wo_bf[:, f * D:(f + 1) * D],
                               in_=wof[bi][:])
                p.inc(ve.drain(), "v_wo")
            for tt in range(OTT):
                bi = tt % 2
                p.wait(sy, "v_out", max(0, (tt - 1) * NFG))
                p.inc(sy.dma_start(out=xr[bi][:],
                                   in_=xown[tt * 128:(tt + 1) * 128, :]),
                      "d_xr", 16)
                p.wait(te, "v_wo")
                p.wait(te, "v_out", tt * NFG)
                lw = None
                for f in range(FIN):
                    for fg in range(NFG):
                        lw = te.matmul(
                            ps_w[fg][:],
                            attnF[:, f * ownT + tt * 128:
                                  f * ownT + (tt + 1) * 128],
                            wo_bf[:, f * D + fg * 512:f * D + (fg + 1) * 512],
                            start=(f == 0), stop=(f == FIN - 1))
                p.inc(te.drain(), "t_wo")
                p.wait(ve, "t_wo")
                p.wait(ve, "d_xr")
                p.wait(ve, "g_out", max(0, (tt - 1) * 16))
                for fg in range(NFG):
                    fsl = slice(fg * 512, (fg + 1) * 512)
                    ve.tensor_add(outb[bi][:, fsl], ps_w[fg][:],
                                  xr[bi][:, fsl])
                    ve.tensor_add(
                        outb[bi][:, fsl], outb[bi][:, fsl],
                        ffnO[:, tt * D + fg * 512:tt * D + (fg + 1) * 512])
                    p.inc(ve.drain(), "v_out")
                p.wait(gp, "v_out")
                p.inc(gp.dma_start(out=out[tt * 128:(tt + 1) * 128, :],
                                   in_=outb[bi][:]), "g_out", 16)
            _finish(p, gp, out, cs, dbg, debug_taps, locals())
    return nc, dbg


def _finish(p, gp, out, cs, dbg, debug_taps, env):
    nc = p.nc
    for name in debug_taps:
        if name in dbg:
            continue
        tile = None
        m = re.match(r"^(tsum)(\d+)$", name)
        if m:
            tile = env[m.group(1)][int(m.group(2))]
        elif name in env:
            tile = env[name]
        if tile is None:
            continue
        shp = list(tile.shape)
        d = nc.dram_tensor("dbg_" + name, shp, tile.dtype,
                           kind="ExternalOutput").ap()
        dbg[name] = d
        p.inc(gp.dma_start(out=d, in_=tile[:]), "g_out", 16)
    p.wait(gp, "g_out")



def prep_inputs(cfg, x, Wq, Wk, Wv, Wo, w_gating, w_linear):
    B, S, D = cfg.B, cfg.S, cfg.D
    hpc, dh, cpb = cfg.hpc, cfg.dh, cfg.cpb
    tabs = _rope_tables(cfg)
    xf = np.ascontiguousarray(np.asarray(x, dtype=np.float32).reshape(B * S, D))
    wo_tv = np.ascontiguousarray(np.asarray(Wo, dtype=np.float32).T)
    wgv = np.ascontiguousarray(np.asarray(w_gating, dtype=np.float32))
    wlv = np.ascontiguousarray(np.asarray(w_linear, dtype=np.float32))
    Wq = np.asarray(Wq, dtype=np.float32)
    Wk = np.asarray(Wk, dtype=np.float32)
    Wv = np.asarray(Wv, dtype=np.float32)
    in_maps = []
    for c in range(cfg.n_cores):
        b = c // cpb
        hs = hpc * (c % 2)
        m = {}
        m["xb"] = np.ascontiguousarray(xf[b * S:(b + 1) * S])
        m["xown"] = np.ascontiguousarray(xf[c * cfg.ownT:(c + 1) * cfg.ownT])
        m["wq_t"] = np.ascontiguousarray(Wq[hs * dh:(hs + hpc) * dh, :].T)
        m["wk_t"] = np.ascontiguousarray(Wk[hs * dh:(hs + hpc) * dh, :].T)
        m["wv_t"] = np.ascontiguousarray(Wv[hs * dh:(hs + hpc) * dh, :].T)
        m["wo_t"] = wo_tv
        m["wg"] = wgv
        m["wl"] = wlv
        for k, v in tabs.items():
            m[k] = v
        in_maps.append(m)
    return in_maps


def kernel(L0_attn_hidden_states, L0_attn_Wq, L0_attn_Wk, L0_attn_Wv,
           L0_attn_Wo, L0_ffn_w_gating, L0_ffn_w_linear):
    cfg = Cfg()
    in_maps = prep_inputs(cfg, L0_attn_hidden_states, L0_attn_Wq,
                          L0_attn_Wk, L0_attn_Wv, L0_attn_Wo,
                          L0_ffn_w_gating, L0_ffn_w_linear)
    nc, _ = build_program(cfg)
    res = run_bass_kernel_spmd(nc, in_maps, core_ids=list(range(cfg.n_cores)))
    shards = [r["out"] for r in res.results]
    outv = np.concatenate(shards, axis=0).reshape(cfg.B, cfg.S, cfg.D)
    return outv
